# version 12
# baseline (speedup 1.0000x reference)
import numpy as np
import concourse.bass as bass
import concourse.mybir as mybir
from concourse.bass_utils import run_bass_kernel_spmd

F32 = mybir.dt.float32
BF16 = mybir.dt.bfloat16
U8 = mybir.dt.uint8
AF = mybir.ActivationFunctionType
ALU = mybir.AluOpType
AX = mybir.AxisListType

D = 1024
L = 2048
H = 16
KC = 8
NG = 4
GT = 512
NE = 9
EPS = 1e-5

XN_O   = 0
Y1_O   = 32768
SST_O  = 32768
C_O    = 65536
D_O    = 98304
E_O    = 180224
F_O    = 188416
SB_TOTAL = 211968

Q_O    = C_O
KF_O   = C_O + 8192
KT_O   = C_O + 16384
V_O    = C_O + 20480
ATM_O  = C_O + 24576
TMP_O  = C_O + 24576
SSB_O  = C_O + 28672
X2_O   = C_O + 30720
IOT_O  = C_O + 20480
IOTC_O = C_O + 20992
IOT2_O = C_O + 21024
TRIR_O = C_O + 21056
EITMP_O = C_O + 23104
H_O    = C_O
T1_O   = C_O + 16384
OG_O   = C_O + 24576
GTS_O  = C_O + 28672
SR_O   = C_O + 30720

WCT_O  = D_O
WQ_O   = D_O + 16384
WK_O   = D_O + 32768
WV_O   = D_O + 49152
WCE_O  = D_O + 65536
W1_O   = D_O
W3_O   = D_O + 16384
W2_O   = D_O + 32768
Y0_O   = D_O + 49152

Y_O    = E_O
XC_O   = E_O

MASK_O = F_O
IDN_O  = F_O + 2048
ONE1_O = F_O + 2560
ONEK_O = F_O + 3072
B1_O   = F_O + 3104
B3_O   = F_O + 3424
GW_O   = F_O + 3744
CMB_O  = F_O + 3872
GS_O   = F_O + 4032
GK_O   = F_O + 4288
EPS_O  = F_O + 4544
CMBT_O = F_O + 4576
EIND_O = F_O + 8672
B2S_O  = F_O + 10720
LTRI_O = F_O + 12768
LMT_O  = F_O + 13792
BM_O   = F_O + 14816
STAT_O = F_O + 15840
IDNB_O = F_O + 21984


class Arena:
    def __init__(self, nc, total_bytes):
        self.nc = nc
        self.handle = nc.sbuf_tensor("arena", [128, total_bytes], U8).__enter__()
        self.base = nc.lookup_mloc(self.handle).addr

    def t(self, name, shape, dtype, off):
        nc = self.nc
        hnd = nc.virtual_sbuf_tensor(name, list(shape), dtype)
        nc._tensor(hnd.name, hnd.shape, hnd.dtype, type="SB")
        mloc = nc.lookup_mloc(hnd)
        mloc.allocated = True
        mloc.addr = self.base + off
        return hnd


class Walk:
    OWN = {"T": "PE", "C": "ACT", "V": "DVE"}

    KEYS = ("PE", "ACT", "DVE", "GP", "X",
            "WL", "WA", "WQ", "WG", "WK", "WV", "WB", "W1", "W3", "W2",
            "XG0", "XG1", "XG2", "XG3",
            "SO0", "SO1", "XC0", "XC1", "XC2", "XC3",
            "SR0", "SR1", "OT0", "OT1")

    def __init__(self, role, engines, sems):
        self.role = role
        self.eng = engines
        self.sem = sems
        self.c = {k: 0 for k in self.KEYS}
        self.waited = {k: 0 for k in self.KEYS}
        self.bank_rd = [dict() for _ in range(8)]

    def _do(self, key, erole, fn, inc):
        if inc:
            self.c[key] += 1
        if self.role == erole:
            inst = fn(self.eng[erole])
            if inc:
                inst.then_inc(self.sem[key], 1)
        return self.c[key]

    def t(self, fn, inc=False):
        return self._do("PE", "T", fn, inc)

    def v(self, fn, inc=False):
        return self._do("DVE", "V", fn, inc)

    def sc(self, fn, inc=False):
        return self._do("ACT", "C", fn, inc)

    def g(self, fn, inc=False):
        return self._do("GP", "G", fn, inc)

    def vdrain(self):
        if self.role == "V":
            self.eng["V"].drain()

    def scdrain(self):
        if self.role == "C":
            self.eng["C"].drain()

    def gdma(self, dst, src, key):
        self.c[key] += 16
        if self.role == "G":
            self.eng["G"].dma_start(out=dst, in_=src).then_inc(self.sem[key], 16)
        return self.c[key]

    def sdma(self, dst, src, key):
        self.c[key] += 16
        if self.role == "S":
            self.eng["S"].dma_start(out=dst, in_=src).then_inc(self.sem[key], 16)
        return self.c[key]

    def wt(self, r, key, val):
        assert val <= self.c[key], f"wait on future {key}: {val} > {self.c[key]}"
        if val <= 0 or self.OWN.get(r) == key:
            return
        if self.role == r:
            if val <= self.waited[key]:
                return
            self.waited[key] = val
            self.eng[r].wait_ge(self.sem[key], val)

    def fill(self, b):
        for k, v in self.bank_rd[b].items():
            self.wt("T", k, v)
        self.bank_rd[b] = {}

    def rd(self, b, key, val):
        self.bank_rd[b][key] = max(self.bank_rd[b].get(key, 0), val)


def build_program():
    nc = bass.Bass("TRN2", target_bir_lowering=False)
    dx = nc.dram_tensor("x", [128, KC * L], F32, kind="ExternalInput").ap()
    dwq = nc.dram_tensor("wq", [128, 8192], F32, kind="ExternalInput").ap()
    dwk = nc.dram_tensor("wk", [128, 8192], F32, kind="ExternalInput").ap()
    dwv = nc.dram_tensor("wv", [128, 8192], F32, kind="ExternalInput").ap()
    dwct = nc.dram_tensor("wct", [128, 8192], F32, kind="ExternalInput").ap()
    dlt = nc.dram_tensor("lt", [128, 512], F32, kind="ExternalInput").ap()
    dgw = nc.dram_tensor("gw", [128, 64], F32, kind="ExternalInput").ap()
    dw1 = nc.dram_tensor("w1", [NE, 128, 8192], F32, kind="ExternalInput").ap()
    dw3 = nc.dram_tensor("w3", [NE, 128, 8192], F32, kind="ExternalInput").ap()
    dw2 = nc.dram_tensor("w2", [NE, 128, 8192], F32, kind="ExternalInput").ap()
    db1 = nc.dram_tensor("b1", [128, 72], F32, kind="ExternalInput").ap()
    db3 = nc.dram_tensor("b3", [128, 72], F32, kind="ExternalInput").ap()
    db2s = nc.dram_tensor("b2s", [9, 1024], F32, kind="ExternalInput").ap()
    dout = nc.dram_tensor("out", [128, KC * L], F32, kind="ExternalOutput").ap()
    dsd = nc.dram_tensor("sdram", [128, KC * L], BF16, kind="Internal").ap()

    ar = Arena(nc, SB_TOTAL)
    T = ar.t
    xn = T("xn", [128, KC * L], BF16, XN_O)
    ysb0 = T("ysb0", [128, 8192], F32, Y0_O)
    ysb1 = T("ysb1", [128, 8192], F32, Y1_O)
    sst = T("sst", [128, 2 * 512], BF16, SST_O)
    Qb = T("Qb", [128, KC * GT], BF16, Q_O)
    Kf = T("Kf", [128, KC * GT], BF16, KF_O)
    Kt = T("Kt", [128, 2 * 1024], BF16, KT_O)
    Vt = T("Vt", [128, 2 * 1024], BF16, V_O)
    atm = T("atm", [128, 2048], BF16, ATM_O)
    tmp = T("tmp", [128, 2 * 512], F32, TMP_O)
    Ssb = T("Ssb", [128, 1024], BF16, SSB_O)
    x2 = T("x2", [128, 2 * 512], BF16, X2_O)
    iot = T("iot", [128, 128], F32, IOT_O)
    iotc = T("iotc", [128, 1], F32, IOTC_O)
    iot2 = T("iot2", [128, 1], F32, IOT2_O)
    trir = T("trir", [128, 512], F32, TRIR_O)
    eitmp = T("eitmp", [9, 1024], F32, EITMP_O)
    hb = T("hb", [128, 8 * 1024], BF16, H_O)
    t1 = T("t1", [128, 8 * 512], BF16, T1_O)
    og = T("og", [128, 2 * 512], F32, OG_O)
    gts = T("gts", [128, 512], F32, GTS_O)
    srg = T("srg", [128, 2 * 512], BF16, SR_O)
    wct = T("wct", [128, 8192], BF16, WCT_O)
    wq = T("wq", [128, 8192], BF16, WQ_O)
    wk = T("wk", [128, 8192], BF16, WK_O)
    wv = T("wv", [128, 8192], BF16, WV_O)
    wce = T("wce", [128, 8192], BF16, WCE_O)
    w1b = T("w1b", [128, 8192], BF16, W1_O)
    w3b = T("w3b", [128, 8192], BF16, W3_O)
    w2b = T("w2b", [128, 8192], BF16, W2_O)
    Yb = T("Yb", [128, KC * GT], BF16, Y_O)
    xcr = T("xcr", [128, 4 * 512], F32, XC_O)
    maskr = T("maskr", [128, 512], F32, MASK_O)
    idn = T("idn", [128, 128], F32, IDN_O)
    idnb = T("idnb", [128, 128], BF16, IDNB_O)
    one1 = T("one1", [1, 128], F32, ONE1_O)
    onek = T("onek", [128, 1], BF16, ONEK_O)
    b1s = T("b1s", [128, 72], F32, B1_O)
    b3s = T("b3s", [128, 72], F32, B3_O)
    gw = T("gw", [128, 64], BF16, GW_O)
    cmb4 = T("cmb4", [128, 36], F32, CMB_O)
    gs = T("gs", [128, 64], F32, GS_O)
    gk = T("gk", [128, 64], F32, GK_O)
    eps_t = T("eps_t", [1, 1], F32, EPS_O)
    cmbT = T("cmbT", [9, 2048], BF16, CMBT_O)
    eind = T("eind", [9, 1024], BF16, EIND_O)
    b2s = T("b2s", [9, 1024], BF16, B2S_O)
    ltri = T("ltri", [128, 512], BF16, LTRI_O)
    lmt = T("lmt", [128, 512], BF16, LMT_O)
    bm = T("bm", [128, 512], BF16, BM_O)
    st0 = T("st0", [1, 512], F32, STAT_O)
    st1 = T("st1", [1, 512], F32, STAT_O + 2048)
    st2 = T("st2", [1, 512], F32, STAT_O + 4096)

    from contextlib import ExitStack
    est = ExitStack()
    pb = [est.enter_context(nc.psum_tensor(f"pb{i}", [128, 512], F32))
          for i in range(8)]
    sems = {k: est.enter_context(nc.semaphore(f"s{k}"))
            for k in Walk.KEYS}

    def emit(w: Walk):
        MM = lambda E, *a, **k: E.matmul(*a, **k)
        x_d = dx.rearrange("p (k t) -> p k t", k=KC)
        xn_d = xn[:].rearrange("p (k t) -> p k t", k=KC)

        w.g(lambda E: E.iota(iot[:], [[1, 128]], channel_multiplier=0,
                             allow_small_or_imprecise_dtypes=True))
        w.g(lambda E: E.iota(iotc[:], [[0, 1]], channel_multiplier=1,
                             allow_small_or_imprecise_dtypes=True))
        gp_iota = w.g(lambda E: E.iota(
            eitmp[:].rearrange("p (e m) -> p e m", e=8), [[1, 8], [0, 128]],
            channel_multiplier=0, allow_small_or_imprecise_dtypes=True),
            inc=True)
        wd_lt = w.gdma(ltri[:], dlt, key="WL")
        xg_w = {}
        xg_w[0] = w.gdma(xn_d[:, :, 0:GT], x_d[:, :, 0:GT], key="XG0")
        wd_gw = w.gdma(gw[:], dgw, key="WG")
        wd_wct = w.gdma(wct[:], dwct, key="WA")
        wd_wq = w.gdma(wq[:], dwq, key="WQ")
        wd_wk = w.gdma(wk[:], dwk, key="WK")
        for gg in range(1, NG):
            xg_w[gg] = w.gdma(xn_d[:, :, gg * GT:(gg + 1) * GT],
                              x_d[:, :, gg * GT:(gg + 1) * GT], key=f"XG{gg}")
        wd_wv = w.gdma(wv[:], dwv, key="WV")
        wd_b2 = w.gdma(b2s[:], db2s, key="WB")
        xd_b1 = w.sdma(b1s[:], db1, key="X")
        xd_b3 = w.sdma(b3s[:], db3, key="X")
        w.wt("V", "GP", gp_iota)
        w.v(lambda E: E.memset(one1[:], 1.0))
        w.v(lambda E: E.memset(onek[:], 1.0))
        w.v(lambda E: E.memset(
            cmb4[:].rearrange("p (c e) -> p c e", c=4)[:, :, 8:9], 1.0))
        w.v(lambda E: E.memset(Ssb[:], 0.0))
        w.v(lambda E: E.memset(pb[7][:], 0.0))
        dv_eps = w.v(lambda E: E.memset(eps_t[:], EPS), inc=True)
        w.v(lambda E: E.tensor_single_scalar(iot2[:], iotc[:], 64.0, ALU.subtract))
        w.v(lambda E: E.tensor_scalar(idn[:], iot[:], iotc[:], None, ALU.is_equal))
        w.v(lambda E: E.tensor_scalar(maskr[:, 0:128], iot[:], iotc[:], None,
                                      ALU.is_ge))
        w.v(lambda E: E.tensor_scalar(eind[:], eitmp[:], iotc[0:9, :], None,
                                      ALU.is_equal))
        w.vdrain()
        w.v(lambda E: E.tensor_copy(idnb[:], idn[:]))
        for q in range(1, 4):
            w.v(lambda E, q=q: E.tensor_copy(maskr[:, q * 128:(q + 1) * 128],
                                             maskr[:, 0:128]))
        w.v(lambda E: E.tensor_scalar(trir[0:64, 0:64], iot[0:64, 0:64],
                                      iotc[0:64], None, ALU.is_ge))
        w.v(lambda E: E.tensor_scalar(trir[64:128, 0:64], iot[64:128, 0:64],
                                      iot2[64:128], None, ALU.is_ge))
        ei = w.v(lambda E: E.engine_nop(), inc=True)
        w.vdrain()
        for q in range(1, 8):
            w.v(lambda E, q=q: E.tensor_copy(trir[:, q * 64:(q + 1) * 64],
                                             trir[:, 0:64]))
        dv_const = w.v(lambda E: E.engine_nop(), inc=True)
        w.vdrain()
        w.wt("V", "WL", wd_lt)
        dv_lmt = w.v(lambda E: E.tensor_tensor(lmt[:], ltri[:], trir[:],
                                               ALU.mult), inc=True)

        st_last = {"mm": 0, "dv": 0, "apply": {}, "sq": {}, "mix": 0}

        def ln_stats(gg):
            stat_mm = {}
            w.wt("T", "DVE", dv_const)
            for k in range(KC):
                sl = (k % 2) * 512
                w.wt("V", f"XG{gg}", xg_w[gg])
                if k >= 2:
                    w.wt("V", "PE", stat_mm[k - 2])
                else:
                    w.wt("V", "PE", st_last["sq"].get(k, 0))
                dv_sq = w.v(lambda E, k=k, sl=sl, gg=gg: E.tensor_tensor(
                    x2[:, sl:sl + 512], xn_d[:, k, gg * GT:(gg + 1) * GT],
                    xn_d[:, k, gg * GT:(gg + 1) * GT], ALU.mult), inc=True)
                w.wt("T", "DVE", dv_sq)
                w.wt("T", f"XG{gg}", xg_w[gg])
                if k == 0:
                    w.fill(2)
                    w.fill(3)
                w.t(lambda E, k=k, gg=gg: MM(
                    E, pb[2][0:1, :], onek[:], xn_d[:, k, gg * GT:(gg + 1) * GT],
                    start=(k == 0), stop=(k == KC - 1)))
                stat_mm[k] = w.t(lambda E, k=k, sl=sl: MM(
                    E, pb[3][0:1, :], onek[:], x2[:, sl:sl + 512],
                    start=(k == 0), stop=(k == KC - 1)), inc=True)
            st_last["mm"] = stat_mm[KC - 1]
            st_last["sq"] = {k - (KC - 2): v for k, v in stat_mm.items()
                             if k >= KC - 2}
            w.wt("V", "PE", st_last["mm"])
            w.v(lambda E: E.tensor_scalar(st0[:], pb[2][0:1, :], 1.0 / D, None,
                                          ALU.mult))
            dv_a = w.v(lambda E: E.tensor_scalar(st1[:], pb[3][0:1, :], 1.0 / D,
                                                 None, ALU.mult), inc=True)
            w.rd(2, "DVE", dv_a)
            w.rd(3, "DVE", dv_a)
            w.vdrain()
            w.v(lambda E: E.tensor_mul(st2[:], st0[:], st0[:]))
            w.vdrain()
            dv_var = w.v(lambda E: E.tensor_sub(st1[:], st1[:], st2[:]), inc=True)
            w.wt("C", "DVE", dv_var)
            w.wt("C", "DVE", dv_eps)
            ac_std = w.sc(lambda E: E.activation(st1[:], st1[:], AF.Sqrt,
                                                 bias=eps_t[:]), inc=True)
            w.wt("V", "ACT", ac_std)
            w.v(lambda E: E.reciprocal(st2[:], st1[:]))
            w.vdrain()
            st_last["dv"] = w.v(lambda E: E.tensor_mul(st0[:], st0[:], st2[:]),
                                inc=True)

        def ln_apply(gg):
            w.wt("T", "DVE", st_last["dv"])
            w.fill(2)
            w.fill(3)
            w.t(lambda E: MM(E, pb[2][:], one1[:], st2[:], start=True, stop=True))
            pe_pl = w.t(lambda E: MM(E, pb[3][:], one1[:], st0[:],
                                     start=True, stop=True), inc=True)
            w.wt("V", "PE", pe_pl)
            w.wt("V", "PE", st_last["mix"])
            dv_apply = 0
            for k in range(KC):
                tsl = (k % 2) * 512
                w.v(lambda E, k=k, tsl=tsl, gg=gg: E.tensor_tensor(
                    tmp[:, tsl:tsl + 512], xn_d[:, k, gg * GT:(gg + 1) * GT],
                    pb[2][:], ALU.mult))
                w.vdrain()
                dv_apply = w.v(lambda E, k=k, gg=gg, tsl=tsl: E.tensor_tensor(
                    xn[:, k * L + gg * GT: k * L + gg * GT + GT],
                    tmp[:, tsl:tsl + 512], pb[3][:], ALU.subtract), inc=True)
                w.vdrain()
            st_last["apply"][gg] = dv_apply
            w.rd(2, "DVE", dv_apply)
            w.rd(3, "DVE", dv_apply)

        ln_stats(0)
        ln_apply(0)

        w.wt("T", "DVE", dv_lmt)
        w.wt("T", "WA", wd_wct)
        w.fill(0)
        w.fill(1)
        pe_bm = {}
        for h in range(H):
            b = (h % 2) * 64
            j = h // 2
            bmb = h % 2
            pe_bm[bmb] = w.t(lambda E, b=b, j=j, bmb=bmb: MM(
                E, pb[bmb][b:b + 64, j * 64:j * 64 + 64],
                lmt[b:b + 64, j * 64:j * 64 + 64],
                lmt[b:b + 64, j * 64:j * 64 + 64],
                start=True, stop=True, tile_position=(b, b),
                skip_group_check=True), inc=(h >= H - 2))
        ac_bm = {}
        for bmb in range(2):
            b = bmb * 64
            w.wt("C", "PE", pe_bm[bmb])
            ac_bm[bmb] = w.sc(lambda E, b=b, bmb=bmb: E.activation(
                bm[b:b + 64, :], pb[bmb][b:b + 64, :], AF.Copy), inc=True)
            w.rd(bmb, "ACT", ac_bm[bmb])
        w.wt("T", "ACT", ac_bm[0])
        w.wt("T", "ACT", ac_bm[1])
        pe_fold_done = 0
        fo_seq = {0: 0, 1: 0}
        for h in range(H):
            b = (h % 2) * 64
            j = h // 2
            par = h % 2
            for o2 in range(2):
                fob = 2 + par + 2 * (fo_seq[par] % 2)
                fo_seq[par] += 1
                w.fill(fob)
                pe_fo = w.t(lambda E, b=b, j=j, o2=o2, fob=fob: MM(
                    E, pb[fob][b:b + 64, 0:512],
                    bm[b:b + 64, j * 64:j * 64 + 64],
                    wct[b:b + 64, j * 1024 + o2 * 512: j * 1024 + o2 * 512 + 512],
                    start=True, stop=True, tile_position=(b, b),
                    skip_group_check=True), inc=True)
                w.wt("C", "PE", pe_fo)
                af = w.sc(lambda E, b=b, j=j, o2=o2, fob=fob: E.activation(
                    wce[b:b + 64, j * 1024 + o2 * 512: j * 1024 + o2 * 512 + 512],
                    pb[fob][b:b + 64, 0:512], AF.Copy), inc=True)
                w.rd(fob, "ACT", af)
                pe_fold_done = pe_fo

        ac_scopy = 0
        ac_evQK = 0
        pe_mix_last = 0
        pe_sproj_done = 0
        pe_q_last = 0
        pe_ktv_last = 0
        pe_tr_last = 0
        ac_cmbT_last = 0
        ac_sp = {}
        so_w = {}
        so_i = 0
        for g in range(NG):
            gcol = g * GT
            w.wt("T", "DVE", st_last["apply"][g])
            w.wt("T", "WG", wd_gw)
            w.fill(6)
            for tc in range(4):
                for k in range(KC):
                    pe_gate = w.t(lambda E, k=k, tc=tc, gcol=gcol: MM(
                        E, pb[6][:, tc * 8:tc * 8 + 8],
                        xn_d[:, k, gcol + tc * 128: gcol + tc * 128 + 128],
                        gw[:, k * 8:(k + 1) * 8],
                        start=(k == 0), stop=(k == KC - 1)),
                        inc=(tc == 3 and k == KC - 1))
            w.wt("V", "PE", pe_gate)
            dv_m = 0
            for tc in range(4):
                dv_m = w.v(lambda E, tc=tc: E.reduce_max(
                    gs[:, tc * 16 + 8:tc * 16 + 9], pb[6][:, tc * 8:tc * 8 + 8],
                    axis=AX.X, negate=True), inc=True)
            w.rd(6, "DVE", dv_m)
            ac_exp = 0
            for tc in range(4):
                w.wt("C", "DVE", dv_m)
                ac_exp = w.sc(lambda E, tc=tc: E.activation(
                    gs[:, tc * 16:tc * 16 + 8], pb[6][:, tc * 8:tc * 8 + 8],
                    AF.Exp, bias=gs[:, tc * 16 + 8:tc * 16 + 9]), inc=True)
            w.rd(6, "ACT", ac_exp)
            w.wt("V", "ACT", ac_exp)
            for tc in range(4):
                w.v(lambda E, tc=tc: E.reduce_sum(
                    gs[:, tc * 16 + 9:tc * 16 + 10], gs[:, tc * 16:tc * 16 + 8],
                    axis=AX.X))
            w.vdrain()
            for tc in range(4):
                w.v(lambda E, tc=tc: E.reciprocal(
                    gs[:, tc * 16 + 10:tc * 16 + 11],
                    gs[:, tc * 16 + 9:tc * 16 + 10]))
            w.vdrain()
            for tc in range(4):
                w.v(lambda E, tc=tc: E.tensor_scalar(
                    gs[:, tc * 16:tc * 16 + 8], gs[:, tc * 16:tc * 16 + 8],
                    gs[:, tc * 16 + 10:tc * 16 + 11], None, ALU.mult))
            w.vdrain()
            for tc in range(4):
                w.v(lambda E, tc=tc: E.reduce_max(
                    gs[:, tc * 16 + 9:tc * 16 + 10], gs[:, tc * 16:tc * 16 + 8],
                    axis=AX.X))
            w.vdrain()
            for tc in range(4):
                w.v(lambda E, tc=tc: E.tensor_scalar(
                    gk[:, tc * 16:tc * 16 + 8], gs[:, tc * 16:tc * 16 + 8],
                    gs[:, tc * 16 + 9:tc * 16 + 10], None, ALU.is_ge))
            w.vdrain()
            for tc in range(4):
                w.v(lambda E, tc=tc: E.tensor_tensor(
                    gk[:, tc * 16 + 8:tc * 16 + 16], gs[:, tc * 16:tc * 16 + 8],
                    gk[:, tc * 16:tc * 16 + 8], ALU.mult))
            w.vdrain()
            for tc in range(4):
                w.v(lambda E, tc=tc: E.tensor_sub(
                    gk[:, tc * 16 + 8:tc * 16 + 16], gs[:, tc * 16:tc * 16 + 8],
                    gk[:, tc * 16 + 8:tc * 16 + 16]))
            w.vdrain()
            for tc in range(4):
                w.v(lambda E, tc=tc: E.reduce_max(
                    gs[:, tc * 16 + 9:tc * 16 + 10],
                    gk[:, tc * 16 + 8:tc * 16 + 16], axis=AX.X))
            w.vdrain()
            for tc in range(4):
                w.v(lambda E, tc=tc: E.tensor_scalar(
                    gk[:, tc * 16 + 8:tc * 16 + 16],
                    gk[:, tc * 16 + 8:tc * 16 + 16],
                    gs[:, tc * 16 + 9:tc * 16 + 10], None, ALU.is_ge))
            w.vdrain()
            for tc in range(4):
                w.v(lambda E, tc=tc: E.tensor_add(
                    gk[:, tc * 16:tc * 16 + 8], gk[:, tc * 16:tc * 16 + 8],
                    gk[:, tc * 16 + 8:tc * 16 + 16]))
            w.vdrain()
            w.wt("V", "PE", pe_tr_last)
            dv_gate_done = 0
            for tc in range(4):
                dv_gate_done = w.v(lambda E, tc=tc: E.tensor_tensor(
                    cmb4[:, tc * 9:tc * 9 + 8], gs[:, tc * 16:tc * 16 + 8],
                    gk[:, tc * 16:tc * 16 + 8], ALU.mult), inc=True)

            w.wt("T", "WQ", wd_wq)
            w.wt("T", "DVE", st_last["apply"][g])
            for m in range(KC):
                bk = m % 2
                w.fill(bk)
                for k in range(KC):
                    pe_fill = w.t(lambda E, k=k, m=m, bk=bk, gcol=gcol: MM(
                        E, pb[bk][:],
                        wq[:, k * 1024 + m * 128: k * 1024 + m * 128 + 128],
                        xn_d[:, k, gcol:gcol + GT],
                        start=(k == 0), stop=(k == KC - 1)),
                        inc=(k == KC - 1))
                w.wt("C", "PE", pe_mix_last)
                w.wt("C", "PE", pe_fill)
                ac_evQK = w.sc(lambda E, m=m, bk=bk, gcol=gcol: E.activation(
                    Qb[:, m * GT:(m + 1) * GT], pb[bk][:], AF.Copy), inc=True)
                w.rd(bk, "ACT", ac_evQK)
            pe_q_last = pe_fill
            if g < NG - 1:
                ln_stats(g + 1)
            w.wt("T", "WK", wd_wk)
            for m in range(KC):
                bk = m % 2
                w.fill(bk)
                for k in range(KC):
                    pe_fill = w.t(lambda E, k=k, m=m, bk=bk, gcol=gcol: MM(
                        E, pb[bk][:],
                        wk[:, k * 1024 + m * 128: k * 1024 + m * 128 + 128],
                        xn_d[:, k, gcol:gcol + GT],
                        start=(k == 0), stop=(k == KC - 1)),
                        inc=(k == KC - 1))
                w.wt("C", "PE", pe_mix_last)
                w.wt("C", "PE", pe_fill)
                ac_evQK = w.sc(lambda E, m=m, bk=bk, gcol=gcol: E.activation(
                    Kf[:, m * GT:(m + 1) * GT], pb[bk][:], AF.Copy), inc=True)
                w.rd(bk, "ACT", ac_evQK)
            pe_q_last = max(pe_q_last, pe_fill)

            ac_yev = {}
            ac_ktv = 0
            for tc in range(4):
                Ch = g * 4 + tc
                sl = tc % 2
                w.wt("T", "ACT", ac_evQK)
                pe_at = {}
                for q in range(2):
                    Ab = 2 + q
                    Bb = 4 + q
                    w.fill(Ab)
                    w.fill(Bb)
                    for jj in range(4):
                        j = q * 4 + jj
                        w.t(lambda E, j=j, jj=jj, Ab=Ab, tc=tc: MM(
                            E, pb[Ab][:, jj * 128:jj * 128 + 128],
                            Kf[0:64, j * GT + tc * 128: j * GT + tc * 128 + 128],
                            Qb[0:64, j * GT + tc * 128: j * GT + tc * 128 + 128],
                            start=True, stop=True, skip_group_check=True))
                        pe_at[q] = w.t(lambda E, j=j, jj=jj, Bb=Bb, tc=tc: MM(
                            E, pb[Bb][:, jj * 128:jj * 128 + 128],
                            Kf[64:128, j * GT + tc * 128: j * GT + tc * 128 + 128],
                            Qb[64:128, j * GT + tc * 128: j * GT + tc * 128 + 128],
                            start=True, stop=True, tile_position=(64, 0),
                            skip_group_check=True), inc=(jj == 3))
                fi = 0
                w.wt("T", "WV", wd_wv)
                for half in range(2):
                    bk = fi % 2
                    w.fill(bk)
                    for k in range(KC):
                        pe_fill = w.t(lambda E, k=k, half=half, bk=bk,
                                      tc=tc, gcol=gcol: MM(
                            E, pb[bk][:],
                            xn_d[:, k, gcol + tc * 128: gcol + tc * 128 + 128],
                            wv[:, k * 1024 + half * 512: k * 1024 + half * 512 + 512],
                            start=(k == 0), stop=(k == KC - 1)),
                            inc=(k == KC - 1))
                    w.wt("C", "PE", pe_fill)
                    ac_ktv = w.sc(lambda E, half=half, bk=bk, sl=sl:
                                  E.activation(
                        Vt[:, sl * 1024 + half * 512: sl * 1024 + half * 512 + 512],
                        pb[bk][:], AF.Copy), inc=True)
                    w.rd(bk, "ACT", ac_ktv)
                    fi += 1
                for half in range(2):
                    bk = fi % 2
                    w.fill(bk)
                    for m4 in range(4):
                        m = half * 4 + m4
                        pe_fill = w.t(lambda E, m=m, m4=m4, bk=bk, tc=tc,
                                      gcol=gcol: MM(
                            E, pb[bk][:, m4 * 128:m4 * 128 + 128],
                            Kf[:, m * GT + tc * 128: m * GT + tc * 128 + 128],
                            idnb[:],
                            start=True, stop=True, skip_group_check=True),
                            inc=(m4 == 3))
                    w.wt("C", "PE", pe_fill)
                    ac_ktv = w.sc(lambda E, half=half, bk=bk, sl=sl:
                                  E.activation(
                        Kt[:, sl * 1024 + half * 512: sl * 1024 + half * 512 + 512],
                        pb[bk][:], AF.Copy), inc=True)
                    w.rd(bk, "ACT", ac_ktv)
                    fi += 1
                pe_ktv_last = pe_fill
                if tc == 1:
                    w.wt("T", "DVE", dv_gate_done)
                    w.fill(6)
                    for tc2 in range(4):
                        pe_tr_last = w.t(lambda E, tc2=tc2: E.transpose(
                            pb[6][0:9, tc2 * 128:tc2 * 128 + 128],
                            cmb4[:, tc2 * 9:tc2 * 9 + 9], idn[:]), inc=True)
                        w.wt("C", "PE", pe_tr_last)
                        Cg = g * 4 + tc2
                        ac_cmbT_last = w.sc(lambda E, Cg=Cg, tc2=tc2:
                                            E.activation(
                            cmbT[0:9, Cg * 128:(Cg + 1) * 128],
                            pb[6][0:9, tc2 * 128:tc2 * 128 + 128],
                            AF.Copy), inc=True)
                        w.rd(6, "ACT", ac_cmbT_last)
                dv_mask = {}
                for q in range(2):
                    Ab = 2 + q
                    Bb = 4 + q
                    w.wt("V", "PE", pe_at[q])
                    w.v(lambda E, q=q, Ab=Ab: E.tensor_tensor(
                        atm[:, q * 1024: q * 1024 + 512], pb[Ab][:],
                        maskr[:], ALU.mult))
                    dv_mask[q] = w.v(lambda E, q=q, Bb=Bb: E.tensor_tensor(
                        atm[:, q * 1024 + 512: q * 1024 + 1024], pb[Bb][:],
                        maskr[:], ALU.mult), inc=True)
                    w.rd(Ab, "DVE", dv_mask[q])
                    w.rd(Bb, "DVE", dv_mask[q])
                w.wt("T", "ACT", ac_ktv)
                w.wt("T", "ACT", ac_scopy)
                for q in range(2):
                    w.wt("T", "DVE", dv_mask[q])
                    for jj in range(4):
                        j = q * 4 + jj
                        yb = (j // 2) % 2
                        st = (j % 2) * 128
                        if j % 2 == 0:
                            w.fill(yb)
                        w.t(lambda E, j=j, yb=yb, st=st, tc=tc: MM(
                            E, pb[yb][:, st:st + 128],
                            Ssb[:, j * 128:(j + 1) * 128],
                            Qb[:, j * GT + tc * 128: j * GT + tc * 128 + 128],
                            start=True, stop=False, skip_group_check=True))
                        w.t(lambda E, j=j, jj=jj, yb=yb, st=st, sl=sl, q=q: MM(
                            E, pb[yb][0:64, st:st + 128],
                            Vt[:, sl * 1024 + (2 * j) * 64: sl * 1024 + (2 * j) * 64 + 64],
                            atm[:, q * 1024 + jj * 128: q * 1024 + jj * 128 + 128],
                            start=False, stop=False, tile_position=(0, 0),
                            skip_group_check=True))
                        pe_y = w.t(lambda E, j=j, jj=jj, yb=yb, st=st, sl=sl,
                                   q=q: MM(
                            E, pb[yb][64:128, st:st + 128],
                            Vt[:, sl * 1024 + (2 * j + 1) * 64: sl * 1024 + (2 * j + 1) * 64 + 64],
                            atm[:, q * 1024 + 512 + jj * 128: q * 1024 + 512 + jj * 128 + 128],
                            start=False, stop=True, tile_position=(0, 64),
                            skip_group_check=True), inc=True)
                        w.t(lambda E, j=j, sl=sl: MM(
                            E, pb[7][0:64, j * 64:(j + 1) * 64],
                            Kt[:, sl * 1024 + (2 * j) * 64: sl * 1024 + (2 * j) * 64 + 64],
                            Vt[:, sl * 1024 + (2 * j) * 64: sl * 1024 + (2 * j) * 64 + 64],
                            start=False, stop=False, tile_position=(0, 0),
                            skip_group_check=True))
                        pe_state = w.t(lambda E, j=j, sl=sl: MM(
                            E, pb[7][64:128, j * 64:(j + 1) * 64],
                            Kt[:, sl * 1024 + (2 * j + 1) * 64: sl * 1024 + (2 * j + 1) * 64 + 64],
                            Vt[:, sl * 1024 + (2 * j + 1) * 64: sl * 1024 + (2 * j + 1) * 64 + 64],
                            start=False, stop=False, tile_position=(0, 64),
                            skip_group_check=True), inc=True)
                        if j % 2 == 1:
                            w.wt("C", "PE", pe_y)
                            aye = w.sc(lambda E, j=j, yb=yb, tc=tc: E.activation(
                                Yb[:].rearrange("p (m t) -> p m t", m=KC)
                                  [:, j - 1:j + 1, tc * 128:tc * 128 + 128],
                                pb[yb][:].rearrange("p (m t) -> p m t", m=4)
                                  [:, 0:2, :], AF.Copy), inc=True)
                            ac_yev[j] = aye
                            w.rd(yb, "ACT", aye)
                    pe_mix_last = pe_state
                w.wt("C", "PE", pe_mix_last)
                w.sc(lambda E: E.activation(
                    Ssb[0:64, :].rearrange("p (j e) -> p j e", j=8)[:, :, 0:64],
                    pb[7][0:64, :].rearrange("p (j e) -> p j e", j=8), AF.Copy))
                ac_scopy = w.sc(lambda E: E.activation(
                    Ssb[64:128, :].rearrange("p (j e) -> p j e", j=8)[:, :, 64:128],
                    pb[7][64:128, :].rearrange("p (j e) -> p j e", j=8),
                    AF.Copy), inc=True)
                w.rd(7, "ACT", ac_scopy)
            if g < NG - 1:
                st_last["mix"] = pe_mix_last
                ln_apply(g + 1)
            w.wt("T", "ACT", ac_yev[7])
            for m in range(KC):
                bk = m % 2
                w.fill(bk)
                for j in range(KC):
                    pe_sp = w.t(lambda E, m=m, j=j, bk=bk: MM(
                        E, pb[bk][:],
                        wce[:, j * 1024 + m * 128: j * 1024 + m * 128 + 128],
                        Yb[:, j * GT:(j + 1) * GT],
                        start=(j == 0), stop=(j == KC - 1)), inc=(j == KC - 1))
                w.wt("C", "PE", pe_sp)
                slot = so_i % 2
                if so_i >= 2:
                    w.wt("C", f"SO{slot}", so_w[so_i - 2])
                acs = w.sc(lambda E, bk=bk, slot=slot: E.activation(
                    sst[:, slot * 512: slot * 512 + 512], pb[bk][:],
                    AF.Copy), inc=True)
                ac_sp[so_i] = acs
                w.rd(bk, "ACT", acs)
                w.wt("S", "ACT", acs)
                so_w[so_i] = w.sdma(
                    dsd[:, m * L + g * GT: m * L + g * GT + GT],
                    sst[:, slot * 512: slot * 512 + 512], key=f"SO{slot}")
                so_i += 1
            pe_sproj_done = pe_sp

        dv_yadd = 0
        pe_u1_done = 0
        pe_u3_done = 0
        pe_y_done = 0
        ac_t1_last = 0
        ac_b2_th = {}
        dvh_ic = {}
        dvh2 = {}
        dv_og = {}
        dv_oadd = {}
        ot_w = {}
        xc_w = {}
        sr_w = {}
        ac_g_prev = 0
        gi_count = 2 * 2 * KC

        def tile_col(gi):
            th_, r = divmod(gi, 16)
            g2_, m_ = divmod(r, KC)
            return m_ * L + th_ * 1024 + g2_ * 512

        def prefetch_xc(gi):
            if gi >= gi_count:
                return
            slot = gi % 4
            if gi < 4:
                w.wt("S", "PE", pe_sproj_done)
            else:
                w.wt("S", "DVE", dv_oadd[gi - 4])
            col = tile_col(gi)
            xc_w[gi] = w.sdma(xcr[:, slot * 512: slot * 512 + 512],
                              dx[:, col:col + 512], key=f"XC{slot}")

        def prefetch_sr(gi):
            if gi >= gi_count:
                return
            slot = gi % 2
            if gi < 2:
                w.wt("G", "PE", st_last["mm"])
                w.wt("G", "SO0", so_w[30])
                w.wt("G", "SO1", so_w[31])
            else:
                w.wt("G", "DVE", dv_og[gi - 2])
            col = tile_col(gi)
            sr_w[gi] = w.gdma(srg[:, slot * 512: slot * 512 + 512],
                              dsd[:, col:col + 512], key=f"SR{slot}")

        def final_tile(gi, dv_ya):
            col = tile_col(gi)
            xcs = gi % 4
            srs = gi % 2
            ogs = gi % 2
            w.wt("C", "DVE", dv_ya)
            w.wt("C", "DVE", dv_og.get(gi - 1, 0))
            w.wt("C", "PE", pe_mix_last)
            ysb_t = ysb0 if gi < 16 else ysb1
            th_, r = divmod(gi, 16)
            g2_, m_ = divmod(r, KC)
            yc = m_ * 1024 + g2_ * 512
            ac_g = w.sc(lambda E, ysb_t=ysb_t, yc=yc: E.activation(
                gts[:], ysb_t[:, yc:yc + 512], AF.Silu), inc=True)
            w.wt("V", "ACT", ac_g)
            w.wt("V", f"SR{srs}", sr_w[gi])
            w.wt("V", f"OT{ogs}", ot_w.get(gi - 2, 0))
            dv_og[gi] = w.v(lambda E, ogs=ogs, srs=srs: E.tensor_tensor(
                og[:, ogs * 512: ogs * 512 + 512], gts[:],
                srg[:, srs * 512: srs * 512 + 512], ALU.mult), inc=True)
            w.vdrain()
            w.wt("V", f"XC{xcs}", xc_w[gi])
            dv_oadd[gi] = w.v(lambda E, ogs=ogs, xcs=xcs: E.tensor_tensor(
                og[:, ogs * 512: ogs * 512 + 512],
                og[:, ogs * 512: ogs * 512 + 512],
                xcr[:, xcs * 512: xcs * 512 + 512], ALU.add), inc=True)
            w.wt("S", "DVE", dv_oadd[gi])
            ot_w[gi] = w.sdma(dout[:, col:col + 512],
                              og[:, ogs * 512: ogs * 512 + 512],
                              key=f"OT{ogs}")
            prefetch_xc(gi + 4)
            prefetch_sr(gi + 2)

        for e in range(NE):
            if e == 0:
                w.wt("G", "PE", pe_fold_done)
            else:
                w.wt("G", "PE", pe_u1_done)
            wd_w1 = w.gdma(w1b[:], dw1[e], key="W1")
            if e == 0:
                w.wt("G", "PE", pe_q_last)
            else:
                w.wt("G", "PE", pe_u3_done)
            wd_w3 = w.gdma(w3b[:], dw3[e], key="W3")
            if e == 0:
                w.wt("G", "PE", pe_ktv_last)
            else:
                w.wt("G", "PE", pe_y_done)
            wd_w2 = w.gdma(w2b[:], dw2[e], key="W2")
            if e == NE - 1:
                for gi0 in range(4):
                    prefetch_xc(gi0)
                for gi0 in range(2):
                    prefetch_sr(gi0)
            for th in range(2):
                base = th * 1024
                pe_cp = 0
                for g2 in range(2):
                    gc2 = base + g2 * 512
                    cpb = 6 + (th * 2 + g2) % 2
                    if e < 8:
                        w.wt("T", "DVE", ei)
                        w.wt("T", "ACT", ac_cmbT_last)
                        w.fill(cpb)
                        pe_cp = w.t(lambda E, e=e, gc2=gc2, cpb=cpb: MM(
                            E, pb[cpb][:], eind[:, e * 128:(e + 1) * 128],
                            cmbT[:, gc2:gc2 + 512], start=True, stop=True),
                            inc=True)
                    w.wt("T", "W1", wd_w1)
                    for ic in range(KC):
                        bk = ic % 2
                        w.fill(bk)
                        for k in range(KC):
                            pe_u1 = w.t(lambda E, k=k, ic=ic, bk=bk, gc2=gc2:
                                        MM(
                                E, pb[bk][:],
                                w1b[:, k * 1024 + ic * 128: k * 1024 + ic * 128 + 128],
                                xn_d[:, k, gc2:gc2 + 512],
                                start=(k == 0), stop=(k == KC - 1)),
                                inc=(k == KC - 1))
                        w.wt("C", "PE", pe_u1)
                        w.wt("C", "X", xd_b3)
                        w.wt("C", "DVE", dvh_ic.get(ic, 0))
                        ac_t1_last = w.sc(lambda E, ic=ic, e=e, bk=bk:
                                          E.activation(
                            t1[:, ic * 512: ic * 512 + 512],
                            pb[bk][:], AF.Silu,
                            bias=b1s[:, e * 8 + ic: e * 8 + ic + 1]), inc=True)
                        w.rd(bk, "ACT", ac_t1_last)
                    pe_u1_done = pe_u1
                    w.wt("T", "W3", wd_w3)
                    for ic in range(KC):
                        bk = 2 + ic % 2
                        w.fill(bk)
                        for k in range(KC):
                            pe_u3 = w.t(lambda E, k=k, ic=ic, bk=bk, gc2=gc2:
                                        MM(
                                E, pb[bk][:],
                                w3b[:, k * 1024 + ic * 128: k * 1024 + ic * 128 + 128],
                                xn_d[:, k, gc2:gc2 + 512],
                                start=(k == 0), stop=(k == KC - 1)),
                                inc=(k == KC - 1))
                        w.wt("V", "PE", pe_u3)
                        w.wt("V", "ACT", ac_t1_last)
                        w.wt("V", "X", xd_b3)
                        hs = ic * 1024 + g2 * 512
                        if e < 8:
                            w.wt("V", "PE", pe_cp)
                            w.v(lambda E, ic=ic, e=e, bk=bk, hs=hs:
                                E.scalar_tensor_tensor(
                                hb[:, hs:hs + 512], pb[bk][:],
                                b3s[:, e * 8 + ic: e * 8 + ic + 1],
                                t1[:, ic * 512: ic * 512 + 512],
                                ALU.add, ALU.mult))
                            w.vdrain()
                            dv_h = w.v(lambda E, hs=hs, cpb=cpb: E.tensor_tensor(
                                hb[:, hs:hs + 512], hb[:, hs:hs + 512],
                                pb[cpb][:], ALU.mult), inc=True)
                            w.rd(cpb, "DVE", dv_h)
                        else:
                            dv_h = w.v(lambda E, ic=ic, e=e, bk=bk, hs=hs:
                                       E.scalar_tensor_tensor(
                                hb[:, hs:hs + 512], pb[bk][:],
                                b3s[:, e * 8 + ic: e * 8 + ic + 1],
                                t1[:, ic * 512: ic * 512 + 512],
                                ALU.add, ALU.mult),
                                inc=True)
                        w.rd(bk, "DVE", dv_h)
                        dvh_ic[ic] = dv_h
                        dvh2[(g2, ic)] = dv_h
                    pe_u3_done = pe_u3
                if e == 0:
                    w.wt("T", "ACT", ac_cmbT_last)
                    w.wt("T", "WB", wd_b2)
                    ac_b2 = 0
                    fi = 0
                    for g2 in range(2):
                        for m in range(KC):
                            bk = fi % 2
                            w.fill(bk)
                            pe_b2 = w.t(lambda E, m=m, g2=g2, bk=bk, base=base:
                                        MM(
                                E, pb[bk][:], b2s[:, m * 128:(m + 1) * 128],
                                cmbT[:, base + g2 * 512: base + g2 * 512 + 512],
                                start=True, stop=True), inc=True)
                            w.wt("C", "PE", pe_b2)
                            if th == 0:
                                w.wt("C", "PE", pe_sproj_done)
                                w.wt("C", "PE", pe_mix_last)
                            else:
                                w.wt("C", "SO0", so_w[30])
                                w.wt("C", "SO1", so_w[31])
                            ysb_t = ysb0 if th == 0 else ysb1
                            ac_b2 = w.sc(lambda E, m=m, g2=g2, bk=bk,
                                         ysb_t=ysb_t: E.activation(
                                ysb_t[:, m * 1024 + g2 * 512: m * 1024 + g2 * 512 + 512],
                                pb[bk][:], AF.Copy), inc=True)
                            w.rd(bk, "ACT", ac_b2)
                            fi += 1
                    ac_b2_th[th] = ac_b2
                w.wt("T", "W2", wd_w2)
                fi = 0
                for g2 in range(2):
                    w.wt("T", "DVE", dvh2[(g2, KC - 1)])
                    for m in range(KC):
                        bk = fi % 2
                        w.fill(bk)
                        for ic in range(KC):
                            pe_yf = w.t(lambda E, ic=ic, m=m, g2=g2, bk=bk,
                                        base=base: MM(
                                E, pb[bk][:],
                                w2b[:, ic * 1024 + m * 128: ic * 1024 + m * 128 + 128],
                                hb[:, ic * 1024 + g2 * 512: ic * 1024 + g2 * 512 + 512],
                                start=(ic == 0), stop=(ic == KC - 1)),
                                inc=(ic == KC - 1))
                        w.wt("V", "PE", pe_yf)
                        w.wt("V", "ACT", ac_b2_th[th])
                        ysb_t = ysb0 if th == 0 else ysb1
                        dv_yadd = w.v(lambda E, m=m, g2=g2, bk=bk, ysb_t=ysb_t:
                                      E.tensor_tensor(
                            ysb_t[:, m * 1024 + g2 * 512: m * 1024 + g2 * 512 + 512],
                            ysb_t[:, m * 1024 + g2 * 512: m * 1024 + g2 * 512 + 512],
                            pb[bk][:], ALU.add), inc=True)
                        w.rd(bk, "DVE", dv_yadd)
                        if e == NE - 1:
                            gi = th * 16 + g2 * KC + m
                            final_tile(gi, dv_yadd)
                        fi += 1
                pe_y_done = pe_yf

        for slot in range(2):
            w.wt("S", f"OT{slot}", w.c[f"OT{slot}"])

    roles = {}
    with nc.Block() as block:
        def run_role(role, engine):
            w = Walk(role, {role: engine}, sems)
            emit(w)
            roles[role] = w.c

        block.tensor(lambda E: run_role("T", E))
        block.vector(lambda E: run_role("V", E))
        block.scalar(lambda E: run_role("C", E))
        block.gpsimd(lambda E: run_role("G", E))
        block.sync(lambda E: run_role("S", E))
    base = roles["T"]
    for r, cc in roles.items():
        assert cc == base, f"walk divergence {r}: {cc} vs {base}"
    est.close()
    return nc


def _img_km(wkm):
    K, M = wkm.shape
    assert K == 1024
    return np.ascontiguousarray(
        wkm.reshape(8, 128, M).transpose(1, 0, 2).reshape(128, 8 * M))


def prep_inputs(L0_attn_X, L0_attn_Wq, L0_attn_Wk, L0_attn_Wv, L0_attn_Wc,
                L0_attn_Ltri, L0_ffn_gate_w, L0_ffn_w1, L0_ffn_b1, L0_ffn_w2,
                L0_ffn_b2, L0_ffn_w3, L0_ffn_b3, L0_ffn_sw1, L0_ffn_sb1,
                L0_ffn_sw2, L0_ffn_sb2, L0_ffn_sw3, L0_ffn_sb3):
    f = np.float32
    shared = {
        "wq": _img_km(np.asarray(L0_attn_Wq, f).T),
        "wk": _img_km(np.asarray(L0_attn_Wk, f).T),
        "wv": _img_km(np.asarray(L0_attn_Wv, f).T),
        "wct": _img_km(np.asarray(L0_attn_Wc, f).T),
        "gw": np.ascontiguousarray(
            np.asarray(L0_ffn_gate_w, f).T.reshape(8, 128, 8)
            .transpose(1, 0, 2).reshape(128, 64)),
    }
    lt = np.zeros((128, 512), f)
    ltT = np.asarray(L0_attn_Ltri, f).transpose(0, 2, 1)
    for h in range(H):
        lt[(h % 2) * 64:(h % 2) * 64 + 64,
           (h // 2) * 64:(h // 2) * 64 + 64] = ltT[h]
    shared["lt"] = lt
    w1 = np.concatenate([np.asarray(L0_ffn_w1, f),
                         np.asarray(L0_ffn_sw1, f)[None]], 0)
    w3 = np.concatenate([np.asarray(L0_ffn_w3, f),
                         np.asarray(L0_ffn_sw3, f)[None]], 0)
    w2 = np.concatenate([np.asarray(L0_ffn_w2, f),
                         np.asarray(L0_ffn_sw2, f)[None]], 0)
    shared["w1"] = np.stack([_img_km(w1[e]) for e in range(NE)])
    shared["w3"] = np.stack([_img_km(w3[e]) for e in range(NE)])
    shared["w2"] = np.stack([_img_km(w2[e]) for e in range(NE)])
    b1e = np.concatenate([np.asarray(L0_ffn_b1, f),
                          np.asarray(L0_ffn_sb1, f)[None]], 0)
    b3e = np.concatenate([np.asarray(L0_ffn_b3, f),
                          np.asarray(L0_ffn_sb3, f)[None]], 0)
    shared["b1"] = np.ascontiguousarray(
        b1e.reshape(9, 8, 128).transpose(2, 0, 1).reshape(128, 72))
    shared["b3"] = np.ascontiguousarray(
        b3e.reshape(9, 8, 128).transpose(2, 0, 1).reshape(128, 72))
    shared["b2s"] = np.ascontiguousarray(np.concatenate(
        [np.asarray(L0_ffn_b2, f), np.asarray(L0_ffn_sb2, f)[None]], 0))
    X = np.asarray(L0_attn_X, f)
    in_maps = []
    for b in range(8):
        d = dict(shared)
        d["x"] = _img_km(np.ascontiguousarray(X[b].T))
        in_maps.append(d)
    return in_maps


def kernel(**inputs):
    in_maps = prep_inputs(**inputs)
    nc = build_program()
    res = run_bass_kernel_spmd(nc, in_maps, core_ids=list(range(8)))
    outs = []
    for b in range(8):
        o = np.asarray(res.results[b]["out"])
        outs.append(o.reshape(128, 8, 2048).transpose(1, 0, 2)
                    .reshape(1024, 2048).T)
    return np.ascontiguousarray(np.stack(outs)).astype(np.float32)


# version 13
# speedup vs baseline: 1.0449x; 1.0449x over previous
import numpy as np
import concourse.bass as bass
import concourse.mybir as mybir
from concourse.bass_utils import run_bass_kernel_spmd

F32 = mybir.dt.float32
BF16 = mybir.dt.bfloat16
U8 = mybir.dt.uint8
AF = mybir.ActivationFunctionType
ALU = mybir.AluOpType
AX = mybir.AxisListType

D = 1024
L = 2048
H = 16
KC = 8
NG = 4
GT = 512
NE = 9
EPS = 1e-5

XN_O   = 0
Y1_O   = 32768
SST_O  = 32768
C_O    = 65536
D_O    = 98304
E_O    = 180224
F_O    = 188416
SB_TOTAL = 211968

Q_O    = C_O
KF_O   = C_O + 8192
KT_O   = C_O + 16384
V_O    = C_O + 20480
ATM_O  = C_O + 24576
TMP_O  = C_O + 24576
SSB_O  = C_O + 28672
X2_O   = C_O + 30720
IOT_O  = C_O + 20480
IOTC_O = C_O + 20992
IOT2_O = C_O + 21024
TRIR_O = C_O + 21056
EITMP_O = C_O + 23104
H_O    = C_O
T1_O   = C_O + 16384
OG_O   = C_O + 24576
GTS_O  = C_O + 28672
SR_O   = C_O + 30720

WCT_O  = D_O
WQ_O   = D_O + 16384
WK_O   = D_O + 32768
WV_O   = D_O + 49152
WCE_O  = D_O + 65536
W1_O   = D_O
W3_O   = D_O + 16384
W2_O   = D_O + 32768
Y0_O   = D_O + 49152

Y_O    = E_O
XC_O   = E_O

MASK_O = F_O
IDN_O  = F_O + 2048
ONE1_O = F_O + 2560
ONEK_O = F_O + 3072
B1_O   = F_O + 3104
B3_O   = F_O + 3424
GW_O   = F_O + 3744
CMB_O  = F_O + 3872
GS_O   = F_O + 4032
GK_O   = F_O + 4288
EPS_O  = F_O + 4544
CMBT_O = F_O + 4576
EIND_O = F_O + 8672
B2S_O  = F_O + 10720
LTRI_O = F_O + 12768
LMT_O  = F_O + 13792
BM_O   = F_O + 14816
STAT_O = F_O + 15840
IDNB_O = F_O + 21984


class Arena:
    def __init__(self, nc, total_bytes):
        self.nc = nc
        self.handle = nc.sbuf_tensor("arena", [128, total_bytes], U8).__enter__()
        self.base = nc.lookup_mloc(self.handle).addr

    def t(self, name, shape, dtype, off):
        nc = self.nc
        hnd = nc.virtual_sbuf_tensor(name, list(shape), dtype)
        nc._tensor(hnd.name, hnd.shape, hnd.dtype, type="SB")
        mloc = nc.lookup_mloc(hnd)
        mloc.allocated = True
        mloc.addr = self.base + off
        return hnd


class Walk:
    OWN = {"T": "PE", "C": "ACT", "V": "DVE"}

    KEYS = ("PE", "ACT", "DVE", "GP", "X",
            "WL", "WA", "WQ", "WG", "WK", "WV", "WB", "W1", "W3", "W2",
            "XG0", "XG1", "XG2", "XG3",
            "SO0", "SO1", "XC0", "XC1", "XC2", "XC3",
            "SR0", "SR1", "OT0", "OT1")

    def __init__(self, role, engines, sems):
        self.role = role
        self.eng = engines
        self.sem = sems
        self.c = {k: 0 for k in self.KEYS}
        self.waited = {k: 0 for k in self.KEYS}
        self.bank_rd = [dict() for _ in range(8)]

    def _do(self, key, erole, fn, inc):
        if inc:
            self.c[key] += 1
        if self.role == erole:
            inst = fn(self.eng[erole])
            if inc:
                inst.then_inc(self.sem[key], 1)
        return self.c[key]

    def t(self, fn, inc=False):
        return self._do("PE", "T", fn, inc)

    def v(self, fn, inc=False):
        return self._do("DVE", "V", fn, inc)

    def sc(self, fn, inc=False):
        return self._do("ACT", "C", fn, inc)

    def g(self, fn, inc=False):
        return self._do("GP", "G", fn, inc)

    def vdrain(self):
        if self.role == "V":
            self.eng["V"].drain()

    def scdrain(self):
        if self.role == "C":
            self.eng["C"].drain()

    def gdma(self, dst, src, key):
        self.c[key] += 16
        if self.role == "G":
            self.eng["G"].dma_start(out=dst, in_=src).then_inc(self.sem[key], 16)
        return self.c[key]

    def sdma(self, dst, src, key):
        self.c[key] += 16
        if self.role == "S":
            self.eng["S"].dma_start(out=dst, in_=src).then_inc(self.sem[key], 16)
        return self.c[key]

    def wt(self, r, key, val):
        assert val <= self.c[key], f"wait on future {key}: {val} > {self.c[key]}"
        if val <= 0 or self.OWN.get(r) == key:
            return
        if self.role == r:
            if val <= self.waited[key]:
                return
            self.waited[key] = val
            self.eng[r].wait_ge(self.sem[key], val)

    def fill(self, b):
        for k, v in self.bank_rd[b].items():
            self.wt("T", k, v)
        self.bank_rd[b] = {}

    def rd(self, b, key, val):
        self.bank_rd[b][key] = max(self.bank_rd[b].get(key, 0), val)


def build_program():
    nc = bass.Bass("TRN2", target_bir_lowering=False)
    dx = nc.dram_tensor("x", [128, KC * L], F32, kind="ExternalInput").ap()
    dwq = nc.dram_tensor("wq", [128, 8192], F32, kind="ExternalInput").ap()
    dwk = nc.dram_tensor("wk", [128, 8192], F32, kind="ExternalInput").ap()
    dwv = nc.dram_tensor("wv", [128, 8192], F32, kind="ExternalInput").ap()
    dwct = nc.dram_tensor("wct", [128, 8192], F32, kind="ExternalInput").ap()
    dlt = nc.dram_tensor("lt", [128, 512], F32, kind="ExternalInput").ap()
    dgw = nc.dram_tensor("gw", [128, 64], F32, kind="ExternalInput").ap()
    dw1 = nc.dram_tensor("w1", [NE, 128, 8192], F32, kind="ExternalInput").ap()
    dw3 = nc.dram_tensor("w3", [NE, 128, 8192], F32, kind="ExternalInput").ap()
    dw2 = nc.dram_tensor("w2", [NE, 128, 8192], F32, kind="ExternalInput").ap()
    db1 = nc.dram_tensor("b1", [128, 72], F32, kind="ExternalInput").ap()
    db3 = nc.dram_tensor("b3", [128, 72], F32, kind="ExternalInput").ap()
    db2s = nc.dram_tensor("b2s", [9, 1024], F32, kind="ExternalInput").ap()
    dout = nc.dram_tensor("out", [128, KC * L], F32, kind="ExternalOutput").ap()
    dsd = nc.dram_tensor("sdram", [128, KC * L], BF16, kind="Internal").ap()

    ar = Arena(nc, SB_TOTAL)
    T = ar.t
    xn = T("xn", [128, KC * L], BF16, XN_O)
    ysb0 = T("ysb0", [128, 8192], F32, Y0_O)
    ysb1 = T("ysb1", [128, 8192], F32, Y1_O)
    sst = T("sst", [128, 2 * 512], BF16, SST_O)
    Qb = T("Qb", [128, KC * GT], BF16, Q_O)
    Kf = T("Kf", [128, KC * GT], BF16, KF_O)
    Kt = T("Kt", [128, 2 * 1024], BF16, KT_O)
    Vt = T("Vt", [128, 2 * 1024], BF16, V_O)
    atm = T("atm", [128, 2048], BF16, ATM_O)
    tmp = T("tmp", [128, 2 * 512], F32, TMP_O)
    Ssb = T("Ssb", [128, 1024], BF16, SSB_O)
    x2 = T("x2", [128, 2 * 512], BF16, X2_O)
    iot = T("iot", [128, 128], F32, IOT_O)
    iotc = T("iotc", [128, 1], F32, IOTC_O)
    iot2 = T("iot2", [128, 1], F32, IOT2_O)
    trir = T("trir", [128, 512], F32, TRIR_O)
    eitmp = T("eitmp", [9, 1024], F32, EITMP_O)
    hb = T("hb", [128, 8 * 1024], BF16, H_O)
    t1 = T("t1", [128, 8 * 512], BF16, T1_O)
    og = T("og", [128, 2 * 512], F32, OG_O)
    gts = T("gts", [128, 512], F32, GTS_O)
    srg = T("srg", [128, 2 * 512], BF16, SR_O)
    wct = T("wct", [128, 8192], BF16, WCT_O)
    wq = T("wq", [128, 8192], BF16, WQ_O)
    wk = T("wk", [128, 8192], BF16, WK_O)
    wv = T("wv", [128, 8192], BF16, WV_O)
    wce = T("wce", [128, 8192], BF16, WCE_O)
    w1b = T("w1b", [128, 8192], BF16, W1_O)
    w3b = T("w3b", [128, 8192], BF16, W3_O)
    w2b = T("w2b", [128, 8192], BF16, W2_O)
    Yb = T("Yb", [128, KC * GT], BF16, Y_O)
    xcr = T("xcr", [128, 4 * 512], F32, XC_O)
    maskr = T("maskr", [128, 512], F32, MASK_O)
    idn = T("idn", [128, 128], F32, IDN_O)
    idnb = T("idnb", [128, 128], BF16, IDNB_O)
    one1 = T("one1", [1, 128], F32, ONE1_O)
    onek = T("onek", [128, 1], BF16, ONEK_O)
    b1s = T("b1s", [128, 72], F32, B1_O)
    b3s = T("b3s", [128, 72], F32, B3_O)
    gw = T("gw", [128, 64], BF16, GW_O)
    cmb4 = T("cmb4", [128, 36], F32, CMB_O)
    gs = T("gs", [128, 64], F32, GS_O)
    gk = T("gk", [128, 64], F32, GK_O)
    eps_t = T("eps_t", [1, 1], F32, EPS_O)
    cmbT = T("cmbT", [9, 2048], BF16, CMBT_O)
    eind = T("eind", [9, 1024], BF16, EIND_O)
    b2s = T("b2s", [9, 1024], BF16, B2S_O)
    ltri = T("ltri", [128, 512], BF16, LTRI_O)
    lmt = T("lmt", [128, 512], BF16, LMT_O)
    bm = T("bm", [128, 512], BF16, BM_O)
    st0 = T("st0", [1, 512], F32, STAT_O)
    st1 = T("st1", [1, 512], F32, STAT_O + 2048)
    st2 = T("st2", [1, 512], F32, STAT_O + 4096)

    from contextlib import ExitStack
    est = ExitStack()
    pb = [est.enter_context(nc.psum_tensor(f"pb{i}", [128, 512], F32))
          for i in range(8)]
    sems = {k: est.enter_context(nc.semaphore(f"s{k}"))
            for k in Walk.KEYS}

    def emit(w: Walk):
        MM = lambda E, *a, **k: E.matmul(*a, **k)
        x_d = dx.rearrange("p (k t) -> p k t", k=KC)
        xn_d = xn[:].rearrange("p (k t) -> p k t", k=KC)

        w.g(lambda E: E.iota(iot[:], [[1, 128]], channel_multiplier=0,
                             allow_small_or_imprecise_dtypes=True))
        w.g(lambda E: E.iota(iotc[:], [[0, 1]], channel_multiplier=1,
                             allow_small_or_imprecise_dtypes=True))
        gp_iota = w.g(lambda E: E.iota(
            eitmp[:].rearrange("p (e m) -> p e m", e=8), [[1, 8], [0, 128]],
            channel_multiplier=0, allow_small_or_imprecise_dtypes=True),
            inc=True)
        wd_lt = w.gdma(ltri[:], dlt, key="WL")
        wd_wct = w.gdma(wct[:], dwct, key="WA")
        xg_w = {}
        xg_w[0] = w.gdma(xn_d[:, :, 0:GT], x_d[:, :, 0:GT], key="XG0")
        wd_gw = w.gdma(gw[:], dgw, key="WG")
        wd_wq = w.gdma(wq[:], dwq, key="WQ")
        wd_wk = w.gdma(wk[:], dwk, key="WK")
        for gg in range(1, NG):
            xg_w[gg] = w.gdma(xn_d[:, :, gg * GT:(gg + 1) * GT],
                              x_d[:, :, gg * GT:(gg + 1) * GT], key=f"XG{gg}")
        wd_wv = w.gdma(wv[:], dwv, key="WV")
        wd_b2 = w.gdma(b2s[:], db2s, key="WB")
        xd_b1 = w.sdma(b1s[:], db1, key="X")
        xd_b3 = w.sdma(b3s[:], db3, key="X")
        w.wt("V", "GP", gp_iota)
        w.v(lambda E: E.memset(one1[:], 1.0))
        w.v(lambda E: E.memset(onek[:], 1.0))
        w.v(lambda E: E.memset(
            cmb4[:].rearrange("p (c e) -> p c e", c=4)[:, :, 8:9], 1.0))
        w.v(lambda E: E.memset(Ssb[:], 0.0))
        w.v(lambda E: E.memset(pb[7][:], 0.0))
        dv_eps = w.v(lambda E: E.memset(eps_t[:], EPS), inc=True)
        w.v(lambda E: E.tensor_single_scalar(iot2[:], iotc[:], 64.0, ALU.subtract))
        w.v(lambda E: E.tensor_scalar(idn[:], iot[:], iotc[:], None, ALU.is_equal))
        w.v(lambda E: E.tensor_scalar(maskr[:, 0:128], iot[:], iotc[:], None,
                                      ALU.is_ge))
        w.v(lambda E: E.tensor_scalar(eind[:], eitmp[:], iotc[0:9, :], None,
                                      ALU.is_equal))
        w.vdrain()
        w.v(lambda E: E.tensor_copy(idnb[:], idn[:]))
        for q in range(1, 4):
            w.v(lambda E, q=q: E.tensor_copy(maskr[:, q * 128:(q + 1) * 128],
                                             maskr[:, 0:128]))
        w.v(lambda E: E.tensor_scalar(trir[0:64, 0:64], iot[0:64, 0:64],
                                      iotc[0:64], None, ALU.is_ge))
        w.v(lambda E: E.tensor_scalar(trir[64:128, 0:64], iot[64:128, 0:64],
                                      iot2[64:128], None, ALU.is_ge))
        ei = w.v(lambda E: E.engine_nop(), inc=True)
        w.vdrain()
        for q in range(1, 8):
            w.v(lambda E, q=q: E.tensor_copy(trir[:, q * 64:(q + 1) * 64],
                                             trir[:, 0:64]))
        dv_const = w.v(lambda E: E.engine_nop(), inc=True)
        w.vdrain()
        w.wt("V", "WL", wd_lt)
        dv_lmt = w.v(lambda E: E.tensor_tensor(lmt[:], ltri[:], trir[:],
                                               ALU.mult), inc=True)

        st_last = {"mm": 0, "dv": 0, "apply": {}, "sq": {}, "mix": 0}

        def ln_stats(gg):
            stat_mm = {}
            w.wt("T", "DVE", dv_const)
            for k in range(KC):
                sl = (k % 2) * 512
                w.wt("V", f"XG{gg}", xg_w[gg])
                if k >= 2:
                    w.wt("V", "PE", stat_mm[k - 2])
                else:
                    w.wt("V", "PE", st_last["sq"].get(k, 0))
                dv_sq = w.v(lambda E, k=k, sl=sl, gg=gg: E.tensor_tensor(
                    x2[:, sl:sl + 512], xn_d[:, k, gg * GT:(gg + 1) * GT],
                    xn_d[:, k, gg * GT:(gg + 1) * GT], ALU.mult), inc=True)
                w.wt("T", "DVE", dv_sq)
                w.wt("T", f"XG{gg}", xg_w[gg])
                if k == 0:
                    w.fill(2)
                    w.fill(3)
                w.t(lambda E, k=k, gg=gg: MM(
                    E, pb[2][0:1, :], onek[:], xn_d[:, k, gg * GT:(gg + 1) * GT],
                    start=(k == 0), stop=(k == KC - 1)))
                stat_mm[k] = w.t(lambda E, k=k, sl=sl: MM(
                    E, pb[3][0:1, :], onek[:], x2[:, sl:sl + 512],
                    start=(k == 0), stop=(k == KC - 1)), inc=True)
            st_last["mm"] = stat_mm[KC - 1]
            st_last["sq"] = {k - (KC - 2): v for k, v in stat_mm.items()
                             if k >= KC - 2}
            w.wt("V", "PE", st_last["mm"])
            w.v(lambda E: E.tensor_scalar(st0[:], pb[2][0:1, :], 1.0 / D, None,
                                          ALU.mult))
            dv_a = w.v(lambda E: E.tensor_scalar(st1[:], pb[3][0:1, :], 1.0 / D,
                                                 None, ALU.mult), inc=True)
            w.rd(2, "DVE", dv_a)
            w.rd(3, "DVE", dv_a)
            w.vdrain()
            w.v(lambda E: E.tensor_mul(st2[:], st0[:], st0[:]))
            w.vdrain()
            dv_var = w.v(lambda E: E.tensor_sub(st1[:], st1[:], st2[:]), inc=True)
            w.wt("C", "DVE", dv_var)
            w.wt("C", "DVE", dv_eps)
            ac_std = w.sc(lambda E: E.activation(st1[:], st1[:], AF.Sqrt,
                                                 bias=eps_t[:]), inc=True)
            w.wt("V", "ACT", ac_std)
            w.v(lambda E: E.reciprocal(st2[:], st1[:]))
            w.vdrain()
            st_last["dv"] = w.v(lambda E: E.tensor_mul(st0[:], st0[:], st2[:]),
                                inc=True)

        def ln_apply(gg):
            w.wt("T", "DVE", st_last["dv"])
            w.fill(2)
            w.fill(3)
            w.t(lambda E: MM(E, pb[2][:], one1[:], st2[:], start=True, stop=True))
            pe_pl = w.t(lambda E: MM(E, pb[3][:], one1[:], st0[:],
                                     start=True, stop=True), inc=True)
            w.wt("V", "PE", pe_pl)
            w.wt("V", "PE", st_last["mix"])
            dv_apply = 0
            for k in range(KC):
                tsl = (k % 2) * 512
                w.v(lambda E, k=k, tsl=tsl, gg=gg: E.tensor_tensor(
                    tmp[:, tsl:tsl + 512], xn_d[:, k, gg * GT:(gg + 1) * GT],
                    pb[2][:], ALU.mult))
                w.vdrain()
                dv_apply = w.v(lambda E, k=k, gg=gg, tsl=tsl: E.tensor_tensor(
                    xn[:, k * L + gg * GT: k * L + gg * GT + GT],
                    tmp[:, tsl:tsl + 512], pb[3][:], ALU.subtract), inc=True)
                w.vdrain()
            st_last["apply"][gg] = dv_apply
            w.rd(2, "DVE", dv_apply)
            w.rd(3, "DVE", dv_apply)

        w.wt("T", "DVE", dv_lmt)
        w.wt("T", "WA", wd_wct)
        w.fill(0)
        w.fill(1)
        pe_bm = {}
        for h in range(H):
            b = (h % 2) * 64
            j = h // 2
            bmb = h % 2
            pe_bm[bmb] = w.t(lambda E, b=b, j=j, bmb=bmb: MM(
                E, pb[bmb][b:b + 64, j * 64:j * 64 + 64],
                lmt[b:b + 64, j * 64:j * 64 + 64],
                lmt[b:b + 64, j * 64:j * 64 + 64],
                start=True, stop=True, tile_position=(b, b),
                skip_group_check=True), inc=(h >= H - 2))
        ac_bm = {}
        for bmb in range(2):
            b = bmb * 64
            w.wt("C", "PE", pe_bm[bmb])
            ac_bm[bmb] = w.sc(lambda E, b=b, bmb=bmb: E.activation(
                bm[b:b + 64, :], pb[bmb][b:b + 64, :], AF.Copy), inc=True)
            w.rd(bmb, "ACT", ac_bm[bmb])
        w.wt("T", "ACT", ac_bm[0])
        w.wt("T", "ACT", ac_bm[1])
        pe_fold_done = 0
        fo_seq = {0: 0, 1: 0}
        for h in range(H):
            b = (h % 2) * 64
            j = h // 2
            par = h % 2
            for o2 in range(2):
                fob = 2 + par + 2 * (fo_seq[par] % 2)
                fo_seq[par] += 1
                w.fill(fob)
                pe_fo = w.t(lambda E, b=b, j=j, o2=o2, fob=fob: MM(
                    E, pb[fob][b:b + 64, 0:512],
                    bm[b:b + 64, j * 64:j * 64 + 64],
                    wct[b:b + 64, j * 1024 + o2 * 512: j * 1024 + o2 * 512 + 512],
                    start=True, stop=True, tile_position=(b, b),
                    skip_group_check=True), inc=True)
                w.wt("C", "PE", pe_fo)
                af = w.sc(lambda E, b=b, j=j, o2=o2, fob=fob: E.activation(
                    wce[b:b + 64, j * 1024 + o2 * 512: j * 1024 + o2 * 512 + 512],
                    pb[fob][b:b + 64, 0:512], AF.Copy), inc=True)
                w.rd(fob, "ACT", af)
                pe_fold_done = pe_fo

        ln_stats(0)
        ln_apply(0)


        ac_scopy = 0
        ac_evQK = 0
        pe_mix_last = 0
        pe_sproj_done = 0
        pe_q_last = 0
        pe_ktv_last = 0
        pe_tr_last = 0
        ac_cmbT_last = 0
        ac_sp = {}
        so_w = {}
        so_i = 0
        for g in range(NG):
            gcol = g * GT
            w.wt("T", "DVE", st_last["apply"][g])
            w.wt("T", "WG", wd_gw)
            w.fill(6)
            for tc in range(4):
                for k in range(KC):
                    pe_gate = w.t(lambda E, k=k, tc=tc, gcol=gcol: MM(
                        E, pb[6][:, tc * 8:tc * 8 + 8],
                        xn_d[:, k, gcol + tc * 128: gcol + tc * 128 + 128],
                        gw[:, k * 8:(k + 1) * 8],
                        start=(k == 0), stop=(k == KC - 1)),
                        inc=(tc == 3 and k == KC - 1))
            w.wt("V", "PE", pe_gate)
            dv_m = 0
            for tc in range(4):
                dv_m = w.v(lambda E, tc=tc: E.reduce_max(
                    gs[:, tc * 16 + 8:tc * 16 + 9], pb[6][:, tc * 8:tc * 8 + 8],
                    axis=AX.X, negate=True), inc=True)
            w.rd(6, "DVE", dv_m)
            ac_exp = 0
            for tc in range(4):
                w.wt("C", "DVE", dv_m)
                ac_exp = w.sc(lambda E, tc=tc: E.activation(
                    gs[:, tc * 16:tc * 16 + 8], pb[6][:, tc * 8:tc * 8 + 8],
                    AF.Exp, bias=gs[:, tc * 16 + 8:tc * 16 + 9]), inc=True)
            w.rd(6, "ACT", ac_exp)
            w.wt("V", "ACT", ac_exp)
            for tc in range(4):
                w.v(lambda E, tc=tc: E.reduce_sum(
                    gs[:, tc * 16 + 9:tc * 16 + 10], gs[:, tc * 16:tc * 16 + 8],
                    axis=AX.X))
            w.vdrain()
            for tc in range(4):
                w.v(lambda E, tc=tc: E.reciprocal(
                    gs[:, tc * 16 + 10:tc * 16 + 11],
                    gs[:, tc * 16 + 9:tc * 16 + 10]))
            w.vdrain()
            for tc in range(4):
                w.v(lambda E, tc=tc: E.tensor_scalar(
                    gs[:, tc * 16:tc * 16 + 8], gs[:, tc * 16:tc * 16 + 8],
                    gs[:, tc * 16 + 10:tc * 16 + 11], None, ALU.mult))
            w.vdrain()
            for tc in range(4):
                w.v(lambda E, tc=tc: E.reduce_max(
                    gs[:, tc * 16 + 9:tc * 16 + 10], gs[:, tc * 16:tc * 16 + 8],
                    axis=AX.X))
            w.vdrain()
            for tc in range(4):
                w.v(lambda E, tc=tc: E.tensor_scalar(
                    gk[:, tc * 16:tc * 16 + 8], gs[:, tc * 16:tc * 16 + 8],
                    gs[:, tc * 16 + 9:tc * 16 + 10], None, ALU.is_ge))
            w.vdrain()
            for tc in range(4):
                w.v(lambda E, tc=tc: E.tensor_tensor(
                    gk[:, tc * 16 + 8:tc * 16 + 16], gs[:, tc * 16:tc * 16 + 8],
                    gk[:, tc * 16:tc * 16 + 8], ALU.mult))
            w.vdrain()
            for tc in range(4):
                w.v(lambda E, tc=tc: E.tensor_sub(
                    gk[:, tc * 16 + 8:tc * 16 + 16], gs[:, tc * 16:tc * 16 + 8],
                    gk[:, tc * 16 + 8:tc * 16 + 16]))
            w.vdrain()
            for tc in range(4):
                w.v(lambda E, tc=tc: E.reduce_max(
                    gs[:, tc * 16 + 9:tc * 16 + 10],
                    gk[:, tc * 16 + 8:tc * 16 + 16], axis=AX.X))
            w.vdrain()
            for tc in range(4):
                w.v(lambda E, tc=tc: E.tensor_scalar(
                    gk[:, tc * 16 + 8:tc * 16 + 16],
                    gk[:, tc * 16 + 8:tc * 16 + 16],
                    gs[:, tc * 16 + 9:tc * 16 + 10], None, ALU.is_ge))
            w.vdrain()
            for tc in range(4):
                w.v(lambda E, tc=tc: E.tensor_add(
                    gk[:, tc * 16:tc * 16 + 8], gk[:, tc * 16:tc * 16 + 8],
                    gk[:, tc * 16 + 8:tc * 16 + 16]))
            w.vdrain()
            w.wt("V", "PE", pe_tr_last)
            dv_gate_done = 0
            for tc in range(4):
                dv_gate_done = w.v(lambda E, tc=tc: E.tensor_tensor(
                    cmb4[:, tc * 9:tc * 9 + 8], gs[:, tc * 16:tc * 16 + 8],
                    gk[:, tc * 16:tc * 16 + 8], ALU.mult), inc=True)

            w.wt("T", "WQ", wd_wq)
            w.wt("T", "DVE", st_last["apply"][g])
            for m in range(KC):
                bk = m % 2
                w.fill(bk)
                for k in range(KC):
                    pe_fill = w.t(lambda E, k=k, m=m, bk=bk, gcol=gcol: MM(
                        E, pb[bk][:],
                        wq[:, k * 1024 + m * 128: k * 1024 + m * 128 + 128],
                        xn_d[:, k, gcol:gcol + GT],
                        start=(k == 0), stop=(k == KC - 1)),
                        inc=(k == KC - 1))
                w.wt("C", "PE", pe_mix_last)
                w.wt("C", "PE", pe_fill)
                ac_evQK = w.sc(lambda E, m=m, bk=bk, gcol=gcol: E.activation(
                    Qb[:, m * GT:(m + 1) * GT], pb[bk][:], AF.Copy), inc=True)
                w.rd(bk, "ACT", ac_evQK)
            pe_q_last = pe_fill
            if g < NG - 1:
                ln_stats(g + 1)
            w.wt("T", "WK", wd_wk)
            for m in range(KC):
                bk = m % 2
                w.fill(bk)
                for k in range(KC):
                    pe_fill = w.t(lambda E, k=k, m=m, bk=bk, gcol=gcol: MM(
                        E, pb[bk][:],
                        wk[:, k * 1024 + m * 128: k * 1024 + m * 128 + 128],
                        xn_d[:, k, gcol:gcol + GT],
                        start=(k == 0), stop=(k == KC - 1)),
                        inc=(k == KC - 1))
                w.wt("C", "PE", pe_mix_last)
                w.wt("C", "PE", pe_fill)
                ac_evQK = w.sc(lambda E, m=m, bk=bk, gcol=gcol: E.activation(
                    Kf[:, m * GT:(m + 1) * GT], pb[bk][:], AF.Copy), inc=True)
                w.rd(bk, "ACT", ac_evQK)
            pe_q_last = max(pe_q_last, pe_fill)

            ac_yev = {}
            ac_ktv = 0
            for tc in range(4):
                Ch = g * 4 + tc
                sl = tc % 2
                w.wt("T", "ACT", ac_evQK)
                pe_at = {}
                for q in range(2):
                    Ab = 2 + q
                    Bb = 4 + q
                    w.fill(Ab)
                    w.fill(Bb)
                    for jj in range(4):
                        j = q * 4 + jj
                        w.t(lambda E, j=j, jj=jj, Ab=Ab, tc=tc: MM(
                            E, pb[Ab][:, jj * 128:jj * 128 + 128],
                            Kf[0:64, j * GT + tc * 128: j * GT + tc * 128 + 128],
                            Qb[0:64, j * GT + tc * 128: j * GT + tc * 128 + 128],
                            start=True, stop=True, skip_group_check=True))
                        pe_at[q] = w.t(lambda E, j=j, jj=jj, Bb=Bb, tc=tc: MM(
                            E, pb[Bb][:, jj * 128:jj * 128 + 128],
                            Kf[64:128, j * GT + tc * 128: j * GT + tc * 128 + 128],
                            Qb[64:128, j * GT + tc * 128: j * GT + tc * 128 + 128],
                            start=True, stop=True, tile_position=(64, 0),
                            skip_group_check=True), inc=(jj == 3))
                fi = 0
                w.wt("T", "WV", wd_wv)
                for half in range(2):
                    bk = fi % 2
                    w.fill(bk)
                    for k in range(KC):
                        pe_fill = w.t(lambda E, k=k, half=half, bk=bk,
                                      tc=tc, gcol=gcol: MM(
                            E, pb[bk][:],
                            xn_d[:, k, gcol + tc * 128: gcol + tc * 128 + 128],
                            wv[:, k * 1024 + half * 512: k * 1024 + half * 512 + 512],
                            start=(k == 0), stop=(k == KC - 1)),
                            inc=(k == KC - 1))
                    w.wt("C", "PE", pe_fill)
                    ac_ktv = w.sc(lambda E, half=half, bk=bk, sl=sl:
                                  E.activation(
                        Vt[:, sl * 1024 + half * 512: sl * 1024 + half * 512 + 512],
                        pb[bk][:], AF.Copy), inc=True)
                    w.rd(bk, "ACT", ac_ktv)
                    fi += 1
                for half in range(2):
                    bk = fi % 2
                    w.fill(bk)
                    for m4 in range(4):
                        m = half * 4 + m4
                        pe_fill = w.t(lambda E, m=m, m4=m4, bk=bk, tc=tc,
                                      gcol=gcol: MM(
                            E, pb[bk][:, m4 * 128:m4 * 128 + 128],
                            Kf[:, m * GT + tc * 128: m * GT + tc * 128 + 128],
                            idnb[:],
                            start=True, stop=True, skip_group_check=True),
                            inc=(m4 == 3))
                    w.wt("C", "PE", pe_fill)
                    ac_ktv = w.sc(lambda E, half=half, bk=bk, sl=sl:
                                  E.activation(
                        Kt[:, sl * 1024 + half * 512: sl * 1024 + half * 512 + 512],
                        pb[bk][:], AF.Copy), inc=True)
                    w.rd(bk, "ACT", ac_ktv)
                    fi += 1
                pe_ktv_last = pe_fill
                if tc == 1:
                    w.wt("T", "DVE", dv_gate_done)
                    w.fill(6)
                    for tc2 in range(4):
                        pe_tr_last = w.t(lambda E, tc2=tc2: E.transpose(
                            pb[6][0:9, tc2 * 128:tc2 * 128 + 128],
                            cmb4[:, tc2 * 9:tc2 * 9 + 9], idn[:]), inc=True)
                        w.wt("C", "PE", pe_tr_last)
                        Cg = g * 4 + tc2
                        ac_cmbT_last = w.sc(lambda E, Cg=Cg, tc2=tc2:
                                            E.activation(
                            cmbT[0:9, Cg * 128:(Cg + 1) * 128],
                            pb[6][0:9, tc2 * 128:tc2 * 128 + 128],
                            AF.Copy), inc=True)
                        w.rd(6, "ACT", ac_cmbT_last)
                dv_mask = {}
                for q in range(2):
                    Ab = 2 + q
                    Bb = 4 + q
                    w.wt("V", "PE", pe_at[q])
                    w.v(lambda E, q=q, Ab=Ab: E.tensor_tensor(
                        atm[:, q * 1024: q * 1024 + 512], pb[Ab][:],
                        maskr[:], ALU.mult))
                    dv_mask[q] = w.v(lambda E, q=q, Bb=Bb: E.tensor_tensor(
                        atm[:, q * 1024 + 512: q * 1024 + 1024], pb[Bb][:],
                        maskr[:], ALU.mult), inc=True)
                    w.rd(Ab, "DVE", dv_mask[q])
                    w.rd(Bb, "DVE", dv_mask[q])
                w.wt("T", "ACT", ac_ktv)
                w.wt("T", "ACT", ac_scopy)
                for q in range(2):
                    w.wt("T", "DVE", dv_mask[q])
                    for jj in range(4):
                        j = q * 4 + jj
                        yb = (j // 2) % 2
                        st = (j % 2) * 128
                        if j % 2 == 0:
                            w.fill(yb)
                        w.t(lambda E, j=j, yb=yb, st=st, tc=tc: MM(
                            E, pb[yb][:, st:st + 128],
                            Ssb[:, j * 128:(j + 1) * 128],
                            Qb[:, j * GT + tc * 128: j * GT + tc * 128 + 128],
                            start=True, stop=False, skip_group_check=True))
                        w.t(lambda E, j=j, jj=jj, yb=yb, st=st, sl=sl, q=q: MM(
                            E, pb[yb][0:64, st:st + 128],
                            Vt[:, sl * 1024 + (2 * j) * 64: sl * 1024 + (2 * j) * 64 + 64],
                            atm[:, q * 1024 + jj * 128: q * 1024 + jj * 128 + 128],
                            start=False, stop=False, tile_position=(0, 0),
                            skip_group_check=True))
                        pe_y = w.t(lambda E, j=j, jj=jj, yb=yb, st=st, sl=sl,
                                   q=q: MM(
                            E, pb[yb][64:128, st:st + 128],
                            Vt[:, sl * 1024 + (2 * j + 1) * 64: sl * 1024 + (2 * j + 1) * 64 + 64],
                            atm[:, q * 1024 + 512 + jj * 128: q * 1024 + 512 + jj * 128 + 128],
                            start=False, stop=True, tile_position=(0, 64),
                            skip_group_check=True), inc=True)
                        w.t(lambda E, j=j, sl=sl: MM(
                            E, pb[7][0:64, j * 64:(j + 1) * 64],
                            Kt[:, sl * 1024 + (2 * j) * 64: sl * 1024 + (2 * j) * 64 + 64],
                            Vt[:, sl * 1024 + (2 * j) * 64: sl * 1024 + (2 * j) * 64 + 64],
                            start=False, stop=False, tile_position=(0, 0),
                            skip_group_check=True))
                        pe_state = w.t(lambda E, j=j, sl=sl: MM(
                            E, pb[7][64:128, j * 64:(j + 1) * 64],
                            Kt[:, sl * 1024 + (2 * j + 1) * 64: sl * 1024 + (2 * j + 1) * 64 + 64],
                            Vt[:, sl * 1024 + (2 * j + 1) * 64: sl * 1024 + (2 * j + 1) * 64 + 64],
                            start=False, stop=False, tile_position=(0, 64),
                            skip_group_check=True), inc=True)
                        if j % 2 == 1:
                            w.wt("C", "PE", pe_y)
                            aye = w.sc(lambda E, j=j, yb=yb, tc=tc: E.activation(
                                Yb[:].rearrange("p (m t) -> p m t", m=KC)
                                  [:, j - 1:j + 1, tc * 128:tc * 128 + 128],
                                pb[yb][:].rearrange("p (m t) -> p m t", m=4)
                                  [:, 0:2, :], AF.Copy), inc=True)
                            ac_yev[j] = aye
                            w.rd(yb, "ACT", aye)
                    pe_mix_last = pe_state
                w.wt("C", "PE", pe_mix_last)
                w.sc(lambda E: E.activation(
                    Ssb[0:64, :].rearrange("p (j e) -> p j e", j=8)[:, :, 0:64],
                    pb[7][0:64, :].rearrange("p (j e) -> p j e", j=8), AF.Copy))
                ac_scopy = w.sc(lambda E: E.activation(
                    Ssb[64:128, :].rearrange("p (j e) -> p j e", j=8)[:, :, 64:128],
                    pb[7][64:128, :].rearrange("p (j e) -> p j e", j=8),
                    AF.Copy), inc=True)
                w.rd(7, "ACT", ac_scopy)
            if g < NG - 1:
                st_last["mix"] = pe_mix_last
                ln_apply(g + 1)
            w.wt("T", "ACT", ac_yev[7])
            for m in range(KC):
                bk = m % 2
                w.fill(bk)
                for j in range(KC):
                    pe_sp = w.t(lambda E, m=m, j=j, bk=bk: MM(
                        E, pb[bk][:],
                        wce[:, j * 1024 + m * 128: j * 1024 + m * 128 + 128],
                        Yb[:, j * GT:(j + 1) * GT],
                        start=(j == 0), stop=(j == KC - 1)), inc=(j == KC - 1))
                w.wt("C", "PE", pe_sp)
                slot = so_i % 2
                if so_i >= 2:
                    w.wt("C", f"SO{slot}", so_w[so_i - 2])
                acs = w.sc(lambda E, bk=bk, slot=slot: E.activation(
                    sst[:, slot * 512: slot * 512 + 512], pb[bk][:],
                    AF.Copy), inc=True)
                ac_sp[so_i] = acs
                w.rd(bk, "ACT", acs)
                w.wt("S", "ACT", acs)
                so_w[so_i] = w.sdma(
                    dsd[:, m * L + g * GT: m * L + g * GT + GT],
                    sst[:, slot * 512: slot * 512 + 512], key=f"SO{slot}")
                so_i += 1
            pe_sproj_done = pe_sp

        dv_yadd = 0
        pe_u1_done = 0
        pe_u3_done = 0
        pe_y_done = 0
        ac_t1_last = 0
        ac_b2_th = {}
        dvh_ic = {}
        dvh2 = {}
        dv_og = {}
        dv_oadd = {}
        ot_w = {}
        xc_w = {}
        sr_w = {}
        ac_g_prev = 0
        gi_count = 2 * 2 * KC

        def tile_col(gi):
            th_, r = divmod(gi, 16)
            g2_, m_ = divmod(r, KC)
            return m_ * L + th_ * 1024 + g2_ * 512

        def prefetch_xc(gi):
            if gi >= gi_count:
                return
            slot = gi % 4
            if gi < 4:
                w.wt("S", "PE", pe_sproj_done)
            else:
                w.wt("S", "DVE", dv_oadd[gi - 4])
            col = tile_col(gi)
            xc_w[gi] = w.sdma(xcr[:, slot * 512: slot * 512 + 512],
                              dx[:, col:col + 512], key=f"XC{slot}")

        def prefetch_sr(gi):
            if gi >= gi_count:
                return
            slot = gi % 2
            if gi < 2:
                w.wt("G", "PE", st_last["mm"])
                w.wt("G", "SO0", so_w[30])
                w.wt("G", "SO1", so_w[31])
            else:
                w.wt("G", "DVE", dv_og[gi - 2])
            col = tile_col(gi)
            sr_w[gi] = w.gdma(srg[:, slot * 512: slot * 512 + 512],
                              dsd[:, col:col + 512], key=f"SR{slot}")

        def final_tile(gi, dv_ya):
            col = tile_col(gi)
            xcs = gi % 4
            srs = gi % 2
            ogs = gi % 2
            w.wt("C", "DVE", dv_ya)
            w.wt("C", "DVE", dv_og.get(gi - 1, 0))
            w.wt("C", "PE", pe_mix_last)
            ysb_t = ysb0 if gi < 16 else ysb1
            th_, r = divmod(gi, 16)
            g2_, m_ = divmod(r, KC)
            yc = m_ * 1024 + g2_ * 512
            ac_g = w.sc(lambda E, ysb_t=ysb_t, yc=yc: E.activation(
                gts[:], ysb_t[:, yc:yc + 512], AF.Silu), inc=True)
            w.wt("V", "ACT", ac_g)
            w.wt("V", f"SR{srs}", sr_w[gi])
            w.wt("V", f"OT{ogs}", ot_w.get(gi - 2, 0))
            dv_og[gi] = w.v(lambda E, ogs=ogs, srs=srs: E.tensor_tensor(
                og[:, ogs * 512: ogs * 512 + 512], gts[:],
                srg[:, srs * 512: srs * 512 + 512], ALU.mult), inc=True)
            w.vdrain()
            w.wt("V", f"XC{xcs}", xc_w[gi])
            dv_oadd[gi] = w.v(lambda E, ogs=ogs, xcs=xcs: E.tensor_tensor(
                og[:, ogs * 512: ogs * 512 + 512],
                og[:, ogs * 512: ogs * 512 + 512],
                xcr[:, xcs * 512: xcs * 512 + 512], ALU.add), inc=True)
            w.wt("S", "DVE", dv_oadd[gi])
            ot_w[gi] = w.sdma(dout[:, col:col + 512],
                              og[:, ogs * 512: ogs * 512 + 512],
                              key=f"OT{ogs}")
            prefetch_xc(gi + 4)
            prefetch_sr(gi + 2)

        for e in range(NE):
            if e == 0:
                w.wt("G", "PE", pe_fold_done)
            else:
                w.wt("G", "PE", pe_u1_done)
            wd_w1 = w.gdma(w1b[:], dw1[e], key="W1")
            if e == 0:
                w.wt("G", "PE", pe_q_last)
            else:
                w.wt("G", "PE", pe_u3_done)
            wd_w3 = w.gdma(w3b[:], dw3[e], key="W3")
            if e == 0:
                w.wt("G", "PE", pe_ktv_last)
            else:
                w.wt("G", "PE", pe_y_done)
            wd_w2 = w.gdma(w2b[:], dw2[e], key="W2")
            if e == NE - 1:
                for gi0 in range(4):
                    prefetch_xc(gi0)
                for gi0 in range(2):
                    prefetch_sr(gi0)
            for th in range(2):
                base = th * 1024
                pe_cp = 0
                for g2 in range(2):
                    gc2 = base + g2 * 512
                    cpb = 6 + (th * 2 + g2) % 2
                    if e < 8:
                        w.wt("T", "DVE", ei)
                        w.wt("T", "ACT", ac_cmbT_last)
                        w.fill(cpb)
                        pe_cp = w.t(lambda E, e=e, gc2=gc2, cpb=cpb: MM(
                            E, pb[cpb][:], eind[:, e * 128:(e + 1) * 128],
                            cmbT[:, gc2:gc2 + 512], start=True, stop=True),
                            inc=True)
                    w.wt("T", "W1", wd_w1)
                    for ic in range(KC):
                        bk = ic % 2
                        w.fill(bk)
                        for k in range(KC):
                            pe_u1 = w.t(lambda E, k=k, ic=ic, bk=bk, gc2=gc2:
                                        MM(
                                E, pb[bk][:],
                                w1b[:, k * 1024 + ic * 128: k * 1024 + ic * 128 + 128],
                                xn_d[:, k, gc2:gc2 + 512],
                                start=(k == 0), stop=(k == KC - 1)),
                                inc=(k == KC - 1))
                        w.wt("C", "PE", pe_u1)
                        w.wt("C", "X", xd_b3)
                        w.wt("C", "DVE", dvh_ic.get(ic, 0))
                        ac_t1_last = w.sc(lambda E, ic=ic, e=e, bk=bk:
                                          E.activation(
                            t1[:, ic * 512: ic * 512 + 512],
                            pb[bk][:], AF.Silu,
                            bias=b1s[:, e * 8 + ic: e * 8 + ic + 1]), inc=True)
                        w.rd(bk, "ACT", ac_t1_last)
                    pe_u1_done = pe_u1
                    w.wt("T", "W3", wd_w3)
                    for ic in range(KC):
                        bk = 2 + ic % 2
                        w.fill(bk)
                        for k in range(KC):
                            pe_u3 = w.t(lambda E, k=k, ic=ic, bk=bk, gc2=gc2:
                                        MM(
                                E, pb[bk][:],
                                w3b[:, k * 1024 + ic * 128: k * 1024 + ic * 128 + 128],
                                xn_d[:, k, gc2:gc2 + 512],
                                start=(k == 0), stop=(k == KC - 1)),
                                inc=(k == KC - 1))
                        w.wt("V", "PE", pe_u3)
                        w.wt("V", "ACT", ac_t1_last)
                        w.wt("V", "X", xd_b3)
                        hs = ic * 1024 + g2 * 512
                        if e < 8:
                            w.wt("V", "PE", pe_cp)
                            w.v(lambda E, ic=ic, e=e, bk=bk, hs=hs:
                                E.scalar_tensor_tensor(
                                hb[:, hs:hs + 512], pb[bk][:],
                                b3s[:, e * 8 + ic: e * 8 + ic + 1],
                                t1[:, ic * 512: ic * 512 + 512],
                                ALU.add, ALU.mult))
                            w.vdrain()
                            dv_h = w.v(lambda E, hs=hs, cpb=cpb: E.tensor_tensor(
                                hb[:, hs:hs + 512], hb[:, hs:hs + 512],
                                pb[cpb][:], ALU.mult), inc=True)
                            w.rd(cpb, "DVE", dv_h)
                        else:
                            dv_h = w.v(lambda E, ic=ic, e=e, bk=bk, hs=hs:
                                       E.scalar_tensor_tensor(
                                hb[:, hs:hs + 512], pb[bk][:],
                                b3s[:, e * 8 + ic: e * 8 + ic + 1],
                                t1[:, ic * 512: ic * 512 + 512],
                                ALU.add, ALU.mult),
                                inc=True)
                        w.rd(bk, "DVE", dv_h)
                        dvh_ic[ic] = dv_h
                        dvh2[(g2, ic)] = dv_h
                    pe_u3_done = pe_u3
                if e == 0:
                    w.wt("T", "ACT", ac_cmbT_last)
                    w.wt("T", "WB", wd_b2)
                    ac_b2 = 0
                    fi = 0
                    for g2 in range(2):
                        for m in range(KC):
                            bk = fi % 2
                            w.fill(bk)
                            pe_b2 = w.t(lambda E, m=m, g2=g2, bk=bk, base=base:
                                        MM(
                                E, pb[bk][:], b2s[:, m * 128:(m + 1) * 128],
                                cmbT[:, base + g2 * 512: base + g2 * 512 + 512],
                                start=True, stop=True), inc=True)
                            w.wt("C", "PE", pe_b2)
                            if th == 0:
                                w.wt("C", "PE", pe_sproj_done)
                                w.wt("C", "PE", pe_mix_last)
                            else:
                                w.wt("C", "SO0", so_w[30])
                                w.wt("C", "SO1", so_w[31])
                            ysb_t = ysb0 if th == 0 else ysb1
                            ac_b2 = w.sc(lambda E, m=m, g2=g2, bk=bk,
                                         ysb_t=ysb_t: E.activation(
                                ysb_t[:, m * 1024 + g2 * 512: m * 1024 + g2 * 512 + 512],
                                pb[bk][:], AF.Copy), inc=True)
                            w.rd(bk, "ACT", ac_b2)
                            fi += 1
                    ac_b2_th[th] = ac_b2
                w.wt("T", "W2", wd_w2)
                fi = 0
                for g2 in range(2):
                    w.wt("T", "DVE", dvh2[(g2, KC - 1)])
                    for m in range(KC):
                        bk = fi % 2
                        w.fill(bk)
                        for ic in range(KC):
                            pe_yf = w.t(lambda E, ic=ic, m=m, g2=g2, bk=bk,
                                        base=base: MM(
                                E, pb[bk][:],
                                w2b[:, ic * 1024 + m * 128: ic * 1024 + m * 128 + 128],
                                hb[:, ic * 1024 + g2 * 512: ic * 1024 + g2 * 512 + 512],
                                start=(ic == 0), stop=(ic == KC - 1)),
                                inc=(ic == KC - 1))
                        w.wt("V", "PE", pe_yf)
                        w.wt("V", "ACT", ac_b2_th[th])
                        ysb_t = ysb0 if th == 0 else ysb1
                        dv_yadd = w.v(lambda E, m=m, g2=g2, bk=bk, ysb_t=ysb_t:
                                      E.tensor_tensor(
                            ysb_t[:, m * 1024 + g2 * 512: m * 1024 + g2 * 512 + 512],
                            ysb_t[:, m * 1024 + g2 * 512: m * 1024 + g2 * 512 + 512],
                            pb[bk][:], ALU.add), inc=True)
                        w.rd(bk, "DVE", dv_yadd)
                        if e == NE - 1:
                            gi = th * 16 + g2 * KC + m
                            final_tile(gi, dv_yadd)
                        fi += 1
                pe_y_done = pe_yf

        for slot in range(2):
            w.wt("S", f"OT{slot}", w.c[f"OT{slot}"])

    roles = {}
    with nc.Block() as block:
        def run_role(role, engine):
            w = Walk(role, {role: engine}, sems)
            emit(w)
            roles[role] = w.c

        block.tensor(lambda E: run_role("T", E))
        block.vector(lambda E: run_role("V", E))
        block.scalar(lambda E: run_role("C", E))
        block.gpsimd(lambda E: run_role("G", E))
        block.sync(lambda E: run_role("S", E))
    base = roles["T"]
    for r, cc in roles.items():
        assert cc == base, f"walk divergence {r}: {cc} vs {base}"
    est.close()
    return nc


def _img_km(wkm):
    K, M = wkm.shape
    assert K == 1024
    return np.ascontiguousarray(
        wkm.reshape(8, 128, M).transpose(1, 0, 2).reshape(128, 8 * M))


def prep_inputs(L0_attn_X, L0_attn_Wq, L0_attn_Wk, L0_attn_Wv, L0_attn_Wc,
                L0_attn_Ltri, L0_ffn_gate_w, L0_ffn_w1, L0_ffn_b1, L0_ffn_w2,
                L0_ffn_b2, L0_ffn_w3, L0_ffn_b3, L0_ffn_sw1, L0_ffn_sb1,
                L0_ffn_sw2, L0_ffn_sb2, L0_ffn_sw3, L0_ffn_sb3):
    f = np.float32
    shared = {
        "wq": _img_km(np.asarray(L0_attn_Wq, f).T),
        "wk": _img_km(np.asarray(L0_attn_Wk, f).T),
        "wv": _img_km(np.asarray(L0_attn_Wv, f).T),
        "wct": _img_km(np.asarray(L0_attn_Wc, f).T),
        "gw": np.ascontiguousarray(
            np.asarray(L0_ffn_gate_w, f).T.reshape(8, 128, 8)
            .transpose(1, 0, 2).reshape(128, 64)),
    }
    lt = np.zeros((128, 512), f)
    ltT = np.asarray(L0_attn_Ltri, f).transpose(0, 2, 1)
    for h in range(H):
        lt[(h % 2) * 64:(h % 2) * 64 + 64,
           (h // 2) * 64:(h // 2) * 64 + 64] = ltT[h]
    shared["lt"] = lt
    w1 = np.concatenate([np.asarray(L0_ffn_w1, f),
                         np.asarray(L0_ffn_sw1, f)[None]], 0)
    w3 = np.concatenate([np.asarray(L0_ffn_w3, f),
                         np.asarray(L0_ffn_sw3, f)[None]], 0)
    w2 = np.concatenate([np.asarray(L0_ffn_w2, f),
                         np.asarray(L0_ffn_sw2, f)[None]], 0)
    shared["w1"] = np.stack([_img_km(w1[e]) for e in range(NE)])
    shared["w3"] = np.stack([_img_km(w3[e]) for e in range(NE)])
    shared["w2"] = np.stack([_img_km(w2[e]) for e in range(NE)])
    b1e = np.concatenate([np.asarray(L0_ffn_b1, f),
                          np.asarray(L0_ffn_sb1, f)[None]], 0)
    b3e = np.concatenate([np.asarray(L0_ffn_b3, f),
                          np.asarray(L0_ffn_sb3, f)[None]], 0)
    shared["b1"] = np.ascontiguousarray(
        b1e.reshape(9, 8, 128).transpose(2, 0, 1).reshape(128, 72))
    shared["b3"] = np.ascontiguousarray(
        b3e.reshape(9, 8, 128).transpose(2, 0, 1).reshape(128, 72))
    shared["b2s"] = np.ascontiguousarray(np.concatenate(
        [np.asarray(L0_ffn_b2, f), np.asarray(L0_ffn_sb2, f)[None]], 0))
    X = np.asarray(L0_attn_X, f)
    in_maps = []
    for b in range(8):
        d = dict(shared)
        d["x"] = _img_km(np.ascontiguousarray(X[b].T))
        in_maps.append(d)
    return in_maps


def kernel(**inputs):
    in_maps = prep_inputs(**inputs)
    nc = build_program()
    res = run_bass_kernel_spmd(nc, in_maps, core_ids=list(range(8)))
    outs = []
    for b in range(8):
        o = np.asarray(res.results[b]["out"])
        outs.append(o.reshape(128, 8, 2048).transpose(1, 0, 2)
                    .reshape(1024, 2048).T)
    return np.ascontiguousarray(np.stack(outs)).astype(np.float32)


# version 14
# speedup vs baseline: 1.1422x; 1.0931x over previous
import numpy as np
import concourse.bass as bass
import concourse.mybir as mybir
from concourse.bass_utils import run_bass_kernel_spmd

F32 = mybir.dt.float32
BF16 = mybir.dt.bfloat16
U8 = mybir.dt.uint8
AF = mybir.ActivationFunctionType
ALU = mybir.AluOpType
AX = mybir.AxisListType

D = 1024
L = 2048
H = 16
KC = 8
NG = 4
GT = 512
NE = 9
EPS = 1e-5

XN_O   = 0
Y1_O   = 32768
SST_O  = 32768
C_O    = 65536
D_O    = 98304
E_O    = 180224
F_O    = 188416
SB_TOTAL = 211968

Q_O    = C_O
KF_O   = C_O + 8192
KT_O   = C_O + 16384
V_O    = C_O + 20480
ATM_O  = C_O + 24576
TMP_O  = C_O + 24576
SSB_O  = C_O + 28672
X2_O   = C_O + 30720
IOT_O  = C_O + 20480
IOTC_O = C_O + 20992
IOT2_O = C_O + 21024
TRIR_O = C_O + 21056
EITMP_O = C_O + 23104
H_O    = C_O
T1_O   = C_O + 16384
OG_O   = C_O + 24576
GTS_O  = C_O + 28672
SR_O   = C_O + 30720

WCT_O  = D_O
WQ_O   = D_O + 16384
WK_O   = D_O + 32768
WV_O   = D_O + 49152
WCE_O  = D_O + 65536
W1_O   = D_O
W3_O   = D_O + 16384
W2_O   = D_O + 32768
Y0_O   = D_O + 49152

Y_O    = E_O
XC_O   = E_O

MASK_O = F_O
IDN_O  = F_O + 2048
ONE1_O = F_O + 2560
ONEK_O = F_O + 3072
B1_O   = F_O + 3104
B3_O   = F_O + 3424
GW_O   = F_O + 3744
CMB_O  = F_O + 3872
GS_O   = F_O + 4032
GK_O   = F_O + 4288
EPS_O  = F_O + 4544
CMBT_O = F_O + 4576
EIND_O = F_O + 8672
B2S_O  = F_O + 10720
LTRI_O = F_O + 12768
LMT_O  = F_O + 13792
BM_O   = F_O + 14816
STAT_O = F_O + 15840
IDNB_O = F_O + 21984


class Arena:
    def __init__(self, nc, total_bytes):
        self.nc = nc
        self.handle = nc.sbuf_tensor("arena", [128, total_bytes], U8).__enter__()
        self.base = nc.lookup_mloc(self.handle).addr

    def t(self, name, shape, dtype, off):
        nc = self.nc
        hnd = nc.virtual_sbuf_tensor(name, list(shape), dtype)
        nc._tensor(hnd.name, hnd.shape, hnd.dtype, type="SB")
        mloc = nc.lookup_mloc(hnd)
        mloc.allocated = True
        mloc.addr = self.base + off
        return hnd


class Walk:
    OWN = {"T": "PE", "C": "ACT", "V": "DVE"}

    KEYS = ("PE", "ACT", "DVE", "GP", "X",
            "WL", "WA", "WQ", "WG", "WK", "WV", "WB", "W1", "W3", "W2",
            "XG0", "XG1", "XG2", "XG3",
            "SO0", "SO1", "XC0", "XC1", "XC2", "XC3",
            "SR0", "SR1", "OT0", "OT1")

    def __init__(self, role, engines, sems):
        self.role = role
        self.eng = engines
        self.sem = sems
        self.c = {k: 0 for k in self.KEYS}
        self.waited = {k: 0 for k in self.KEYS}
        self.bank_rd = [dict() for _ in range(8)]

    def _do(self, key, erole, fn, inc):
        if inc:
            self.c[key] += 1
        if self.role == erole:
            inst = fn(self.eng[erole])
            if inc:
                inst.then_inc(self.sem[key], 1)
        return self.c[key]

    def t(self, fn, inc=False):
        return self._do("PE", "T", fn, inc)

    def v(self, fn, inc=False):
        return self._do("DVE", "V", fn, inc)

    def sc(self, fn, inc=False):
        return self._do("ACT", "C", fn, inc)

    def g(self, fn, inc=False):
        return self._do("GP", "G", fn, inc)

    def vdrain(self):
        if self.role == "V":
            self.eng["V"].drain()

    def scdrain(self):
        if self.role == "C":
            self.eng["C"].drain()

    def gdma(self, dst, src, key):
        self.c[key] += 16
        if self.role == "G":
            self.eng["G"].dma_start(out=dst, in_=src).then_inc(self.sem[key], 16)
        return self.c[key]

    def sdma(self, dst, src, key):
        self.c[key] += 16
        if self.role == "S":
            self.eng["S"].dma_start(out=dst, in_=src).then_inc(self.sem[key], 16)
        return self.c[key]

    def wt(self, r, key, val):
        assert val <= self.c[key], f"wait on future {key}: {val} > {self.c[key]}"
        if val <= 0 or self.OWN.get(r) == key:
            return
        if self.role == r:
            if val <= self.waited[key]:
                return
            self.waited[key] = val
            self.eng[r].wait_ge(self.sem[key], val)

    def fill(self, b):
        for k, v in self.bank_rd[b].items():
            self.wt("T", k, v)
        self.bank_rd[b] = {}

    def rd(self, b, key, val):
        self.bank_rd[b][key] = max(self.bank_rd[b].get(key, 0), val)


def build_program():
    nc = bass.Bass("TRN2", target_bir_lowering=False)
    dx = nc.dram_tensor("x", [128, KC * L], F32, kind="ExternalInput").ap()
    dwq = nc.dram_tensor("wq", [128, 8192], F32, kind="ExternalInput").ap()
    dwk = nc.dram_tensor("wk", [128, 8192], F32, kind="ExternalInput").ap()
    dwv = nc.dram_tensor("wv", [128, 8192], F32, kind="ExternalInput").ap()
    dwct = nc.dram_tensor("wct", [128, 8192], F32, kind="ExternalInput").ap()
    dlt = nc.dram_tensor("lt", [128, 512], F32, kind="ExternalInput").ap()
    dgw = nc.dram_tensor("gw", [128, 64], F32, kind="ExternalInput").ap()
    dw1 = nc.dram_tensor("w1", [NE, 128, 8192], F32, kind="ExternalInput").ap()
    dw3 = nc.dram_tensor("w3", [NE, 128, 8192], F32, kind="ExternalInput").ap()
    dw2 = nc.dram_tensor("w2", [NE, 128, 8192], F32, kind="ExternalInput").ap()
    db1 = nc.dram_tensor("b1", [128, 72], F32, kind="ExternalInput").ap()
    db3 = nc.dram_tensor("b3", [128, 72], F32, kind="ExternalInput").ap()
    db2s = nc.dram_tensor("b2s", [9, 1024], F32, kind="ExternalInput").ap()
    dout = nc.dram_tensor("out", [128, KC * L], F32, kind="ExternalOutput").ap()
    dsd = nc.dram_tensor("sdram", [128, KC * L], BF16, kind="Internal").ap()

    ar = Arena(nc, SB_TOTAL)
    T = ar.t
    xn = T("xn", [128, KC * L], BF16, XN_O)
    ysb0 = T("ysb0", [128, 8192], F32, Y0_O)
    ysb1 = T("ysb1", [128, 8192], F32, Y1_O)
    sst = T("sst", [128, 2 * 512], BF16, SST_O)
    Qb = T("Qb", [128, KC * GT], BF16, Q_O)
    Kf = T("Kf", [128, KC * GT], BF16, KF_O)
    Kt = T("Kt", [128, 2 * 1024], BF16, KT_O)
    Vt = T("Vt", [128, 2 * 1024], BF16, V_O)
    atm = T("atm", [128, 2048], BF16, ATM_O)
    tmp = T("tmp", [128, 2 * 512], F32, TMP_O)
    Ssb = T("Ssb", [128, 1024], BF16, SSB_O)
    x2 = T("x2", [128, 2 * 512], BF16, X2_O)
    iot = T("iot", [128, 128], F32, IOT_O)
    iotc = T("iotc", [128, 1], F32, IOTC_O)
    iot2 = T("iot2", [128, 1], F32, IOT2_O)
    trir = T("trir", [128, 512], F32, TRIR_O)
    eitmp = T("eitmp", [9, 1024], F32, EITMP_O)
    hb = T("hb", [128, 8 * 1024], BF16, H_O)
    t1 = T("t1", [128, 8 * 512], BF16, T1_O)
    og = T("og", [128, 2 * 512], F32, OG_O)
    gts = T("gts", [128, 512], F32, GTS_O)
    srg = T("srg", [128, 2 * 512], BF16, SR_O)
    wct = T("wct", [128, 8192], BF16, WCT_O)
    wq = T("wq", [128, 8192], BF16, WQ_O)
    wk = T("wk", [128, 8192], BF16, WK_O)
    wv = T("wv", [128, 8192], BF16, WV_O)
    wce = T("wce", [128, 8192], BF16, WCE_O)
    w1b = T("w1b", [128, 8192], BF16, W1_O)
    w3b = T("w3b", [128, 8192], BF16, W3_O)
    w2b = T("w2b", [128, 8192], BF16, W2_O)
    Yb = T("Yb", [128, KC * GT], BF16, Y_O)
    xcr = T("xcr", [128, 4 * 512], F32, XC_O)
    maskr = T("maskr", [128, 512], F32, MASK_O)
    idn = T("idn", [128, 128], F32, IDN_O)
    idnb = T("idnb", [128, 128], BF16, IDNB_O)
    one1 = T("one1", [1, 128], F32, ONE1_O)
    onek = T("onek", [128, 1], BF16, ONEK_O)
    b1s = T("b1s", [128, 72], F32, B1_O)
    b3s = T("b3s", [128, 72], F32, B3_O)
    gw = T("gw", [128, 64], BF16, GW_O)
    cmb4 = T("cmb4", [128, 36], F32, CMB_O)
    gs = T("gs", [128, 64], F32, GS_O)
    gk = T("gk", [128, 64], F32, GK_O)
    eps_t = T("eps_t", [1, 1], F32, EPS_O)
    cmbT = T("cmbT", [9, 2048], BF16, CMBT_O)
    eind = T("eind", [9, 1024], BF16, EIND_O)
    b2s = T("b2s", [9, 1024], BF16, B2S_O)
    ltri = T("ltri", [128, 512], BF16, LTRI_O)
    lmt = T("lmt", [128, 512], BF16, LMT_O)
    bm = T("bm", [128, 512], BF16, BM_O)
    st0 = T("st0", [1, 512], F32, STAT_O)
    st1 = T("st1", [1, 512], F32, STAT_O + 2048)
    st2 = T("st2", [1, 512], F32, STAT_O + 4096)

    from contextlib import ExitStack
    est = ExitStack()
    pb = [est.enter_context(nc.psum_tensor(f"pb{i}", [128, 512], F32))
          for i in range(8)]
    sems = {k: est.enter_context(nc.semaphore(f"s{k}"))
            for k in Walk.KEYS}

    def emit(w: Walk):
        MM = lambda E, *a, **k: E.matmul(*a, **k)
        x_d = dx.rearrange("p (k t) -> p k t", k=KC)
        xn_d = xn[:].rearrange("p (k t) -> p k t", k=KC)

        w.g(lambda E: E.iota(iot[:], [[1, 128]], channel_multiplier=0,
                             allow_small_or_imprecise_dtypes=True))
        w.g(lambda E: E.iota(iotc[:], [[0, 1]], channel_multiplier=1,
                             allow_small_or_imprecise_dtypes=True))
        gp_iota = w.g(lambda E: E.iota(
            eitmp[:].rearrange("p (e m) -> p e m", e=8), [[1, 8], [0, 128]],
            channel_multiplier=0, allow_small_or_imprecise_dtypes=True),
            inc=True)
        wd_lt = w.gdma(ltri[:], dlt, key="WL")
        wd_wct = w.gdma(wct[:], dwct, key="WA")
        xg_w = {}
        xg_w[0] = w.gdma(xn_d[:, :, 0:GT], x_d[:, :, 0:GT], key="XG0")
        wd_gw = w.gdma(gw[:], dgw, key="WG")
        wd_wq = w.gdma(wq[:], dwq, key="WQ")
        wd_wk = w.gdma(wk[:], dwk, key="WK")
        for gg in range(1, NG):
            xg_w[gg] = w.gdma(xn_d[:, :, gg * GT:(gg + 1) * GT],
                              x_d[:, :, gg * GT:(gg + 1) * GT], key=f"XG{gg}")
        wd_wv = w.gdma(wv[:], dwv, key="WV")
        wd_b2 = w.gdma(b2s[:], db2s, key="WB")
        xd_b1 = w.sdma(b1s[:], db1, key="X")
        xd_b3 = w.sdma(b3s[:], db3, key="X")
        w.wt("V", "GP", gp_iota)
        w.v(lambda E: E.memset(one1[:], 1.0))
        w.v(lambda E: E.memset(onek[:], 1.0))
        w.v(lambda E: E.memset(
            cmb4[:].rearrange("p (c e) -> p c e", c=4)[:, :, 8:9], 1.0))
        w.v(lambda E: E.memset(Ssb[:], 0.0))
        w.v(lambda E: E.memset(pb[7][:], 0.0))
        dv_eps = w.v(lambda E: E.memset(eps_t[:], EPS), inc=True)
        w.v(lambda E: E.tensor_single_scalar(iot2[:], iotc[:], 64.0, ALU.subtract))
        w.v(lambda E: E.tensor_scalar(idn[:], iot[:], iotc[:], None, ALU.is_equal))
        w.v(lambda E: E.tensor_scalar(maskr[:, 0:128], iot[:], iotc[:], None,
                                      ALU.is_ge))
        w.v(lambda E: E.tensor_scalar(eind[:], eitmp[:], iotc[0:9, :], None,
                                      ALU.is_equal))
        w.vdrain()
        w.v(lambda E: E.tensor_copy(idnb[:], idn[:]))
        for q in range(1, 4):
            w.v(lambda E, q=q: E.tensor_copy(maskr[:, q * 128:(q + 1) * 128],
                                             maskr[:, 0:128]))
        w.v(lambda E: E.tensor_scalar(trir[0:64, 0:64], iot[0:64, 0:64],
                                      iotc[0:64], None, ALU.is_ge))
        w.v(lambda E: E.tensor_scalar(trir[64:128, 0:64], iot[64:128, 0:64],
                                      iot2[64:128], None, ALU.is_ge))
        ei = w.v(lambda E: E.engine_nop(), inc=True)
        w.vdrain()
        for q in range(1, 8):
            w.v(lambda E, q=q: E.tensor_copy(trir[:, q * 64:(q + 1) * 64],
                                             trir[:, 0:64]))
        dv_const = w.v(lambda E: E.engine_nop(), inc=True)
        w.vdrain()
        w.wt("V", "WL", wd_lt)
        dv_lmt = w.v(lambda E: E.tensor_tensor(lmt[:], ltri[:], trir[:],
                                               ALU.mult), inc=True)

        st_last = {"mm": 0, "dv": 0, "apply": {}, "sq": {}, "mix": 0}

        def ln_stats(gg):
            stat_mm = {}
            w.wt("T", "DVE", dv_const)
            for k in range(KC):
                sl = (k % 2) * 512
                w.wt("V", f"XG{gg}", xg_w[gg])
                if k >= 2:
                    w.wt("V", "PE", stat_mm[k - 2])
                else:
                    w.wt("V", "PE", st_last["sq"].get(k, 0))
                dv_sq = w.v(lambda E, k=k, sl=sl, gg=gg: E.tensor_tensor(
                    x2[:, sl:sl + 512], xn_d[:, k, gg * GT:(gg + 1) * GT],
                    xn_d[:, k, gg * GT:(gg + 1) * GT], ALU.mult), inc=True)
                w.wt("T", "DVE", dv_sq)
                w.wt("T", f"XG{gg}", xg_w[gg])
                if k == 0:
                    w.fill(2)
                    w.fill(3)
                w.t(lambda E, k=k, gg=gg: MM(
                    E, pb[2][0:1, :], onek[:], xn_d[:, k, gg * GT:(gg + 1) * GT],
                    start=(k == 0), stop=(k == KC - 1)))
                stat_mm[k] = w.t(lambda E, k=k, sl=sl: MM(
                    E, pb[3][0:1, :], onek[:], x2[:, sl:sl + 512],
                    start=(k == 0), stop=(k == KC - 1)), inc=True)
            st_last["mm"] = stat_mm[KC - 1]
            st_last["sq"] = {k - (KC - 2): v for k, v in stat_mm.items()
                             if k >= KC - 2}
            w.wt("V", "PE", st_last["mm"])
            w.v(lambda E: E.tensor_scalar(st0[:], pb[2][0:1, :], 1.0 / D, None,
                                          ALU.mult))
            dv_a = w.v(lambda E: E.tensor_scalar(st1[:], pb[3][0:1, :], 1.0 / D,
                                                 None, ALU.mult), inc=True)
            w.rd(2, "DVE", dv_a)
            w.rd(3, "DVE", dv_a)
            w.vdrain()
            w.v(lambda E: E.tensor_mul(st2[:], st0[:], st0[:]))
            w.vdrain()
            dv_var = w.v(lambda E: E.tensor_sub(st1[:], st1[:], st2[:]), inc=True)
            w.wt("C", "DVE", dv_var)
            w.wt("C", "DVE", dv_eps)
            ac_std = w.sc(lambda E: E.activation(st1[:], st1[:], AF.Sqrt,
                                                 bias=eps_t[:]), inc=True)
            w.wt("V", "ACT", ac_std)
            w.v(lambda E: E.reciprocal(st2[:], st1[:]))
            w.vdrain()
            st_last["dv"] = w.v(lambda E: E.tensor_mul(st0[:], st0[:], st2[:]),
                                inc=True)

        def ln_apply(gg):
            w.wt("T", "DVE", st_last["dv"])
            w.fill(2)
            w.fill(3)
            w.t(lambda E: MM(E, pb[2][:], one1[:], st2[:], start=True, stop=True))
            pe_pl = w.t(lambda E: MM(E, pb[3][:], one1[:], st0[:],
                                     start=True, stop=True), inc=True)
            w.wt("V", "PE", pe_pl)
            w.wt("V", "PE", st_last["mix"])
            dv_apply = 0
            for k in range(KC):
                tsl = (k % 2) * 512
                w.v(lambda E, k=k, tsl=tsl, gg=gg: E.tensor_tensor(
                    tmp[:, tsl:tsl + 512], xn_d[:, k, gg * GT:(gg + 1) * GT],
                    pb[2][:], ALU.mult))
                w.vdrain()
                dv_apply = w.v(lambda E, k=k, gg=gg, tsl=tsl: E.tensor_tensor(
                    xn[:, k * L + gg * GT: k * L + gg * GT + GT],
                    tmp[:, tsl:tsl + 512], pb[3][:], ALU.subtract), inc=True)
                w.vdrain()
            st_last["apply"][gg] = dv_apply
            w.rd(2, "DVE", dv_apply)
            w.rd(3, "DVE", dv_apply)

        w.wt("T", "DVE", dv_lmt)
        w.wt("T", "WA", wd_wct)
        w.fill(0)
        w.fill(1)
        pe_bm = {}
        for h in range(H):
            b = (h % 2) * 64
            j = h // 2
            bmb = h % 2
            pe_bm[bmb] = w.t(lambda E, b=b, j=j, bmb=bmb: MM(
                E, pb[bmb][b:b + 64, j * 64:j * 64 + 64],
                lmt[b:b + 64, j * 64:j * 64 + 64],
                lmt[b:b + 64, j * 64:j * 64 + 64],
                start=True, stop=True, tile_position=(b, b),
                skip_group_check=True), inc=(h >= H - 2))
        ac_bm = {}
        for bmb in range(2):
            b = bmb * 64
            w.wt("C", "PE", pe_bm[bmb])
            ac_bm[bmb] = w.sc(lambda E, b=b, bmb=bmb: E.activation(
                bm[b:b + 64, :], pb[bmb][b:b + 64, :], AF.Copy), inc=True)
            w.rd(bmb, "ACT", ac_bm[bmb])
        w.wt("T", "ACT", ac_bm[0])
        w.wt("T", "ACT", ac_bm[1])
        pe_fold_done = 0
        fo_seq = {0: 0, 1: 0}
        for h in range(H):
            b = (h % 2) * 64
            j = h // 2
            par = h % 2
            for o2 in range(2):
                fob = 2 + par + 2 * (fo_seq[par] % 2)
                fo_seq[par] += 1
                w.fill(fob)
                pe_fo = w.t(lambda E, b=b, j=j, o2=o2, fob=fob: MM(
                    E, pb[fob][b:b + 64, 0:512],
                    bm[b:b + 64, j * 64:j * 64 + 64],
                    wct[b:b + 64, j * 1024 + o2 * 512: j * 1024 + o2 * 512 + 512],
                    start=True, stop=True, tile_position=(b, b),
                    skip_group_check=True), inc=True)
                w.wt("C", "PE", pe_fo)
                af = w.sc(lambda E, b=b, j=j, o2=o2, fob=fob: E.activation(
                    wce[b:b + 64, j * 1024 + o2 * 512: j * 1024 + o2 * 512 + 512],
                    pb[fob][b:b + 64, 0:512], AF.Copy), inc=True)
                w.rd(fob, "ACT", af)
                pe_fold_done = pe_fo

        ln_stats(0)
        ln_apply(0)


        ac_scopy = 0
        ac_evQK = 0
        pe_mix_last = 0
        pe_sproj_done = 0
        pe_q_last = 0
        pe_ktv_last = 0
        pe_tr_last = 0
        ac_cmbT_last = 0
        ac_sp = {}
        so_w = {}
        so_i = 0
        for g in range(NG):
            gcol = g * GT
            w.wt("T", "DVE", st_last["apply"][g])
            w.wt("T", "WG", wd_gw)
            w.fill(6)
            for tc in range(4):
                for k in range(KC):
                    pe_gate = w.t(lambda E, k=k, tc=tc, gcol=gcol: MM(
                        E, pb[6][:, tc * 8:tc * 8 + 8],
                        xn_d[:, k, gcol + tc * 128: gcol + tc * 128 + 128],
                        gw[:, k * 8:(k + 1) * 8],
                        start=(k == 0), stop=(k == KC - 1)),
                        inc=(tc == 3 and k == KC - 1))
            w.wt("V", "PE", pe_gate)
            dv_m = 0
            for tc in range(4):
                dv_m = w.v(lambda E, tc=tc: E.reduce_max(
                    gs[:, tc * 16 + 8:tc * 16 + 9], pb[6][:, tc * 8:tc * 8 + 8],
                    axis=AX.X, negate=True), inc=True)
            w.rd(6, "DVE", dv_m)
            ac_exp = 0
            for tc in range(4):
                w.wt("C", "DVE", dv_m)
                ac_exp = w.sc(lambda E, tc=tc: E.activation(
                    gs[:, tc * 16:tc * 16 + 8], pb[6][:, tc * 8:tc * 8 + 8],
                    AF.Exp, bias=gs[:, tc * 16 + 8:tc * 16 + 9]), inc=True)
            w.rd(6, "ACT", ac_exp)
            w.wt("V", "ACT", ac_exp)
            for tc in range(4):
                w.v(lambda E, tc=tc: E.reduce_sum(
                    gs[:, tc * 16 + 9:tc * 16 + 10], gs[:, tc * 16:tc * 16 + 8],
                    axis=AX.X))
            w.vdrain()
            for tc in range(4):
                w.v(lambda E, tc=tc: E.reciprocal(
                    gs[:, tc * 16 + 10:tc * 16 + 11],
                    gs[:, tc * 16 + 9:tc * 16 + 10]))
            w.vdrain()
            for tc in range(4):
                w.v(lambda E, tc=tc: E.tensor_scalar(
                    gs[:, tc * 16:tc * 16 + 8], gs[:, tc * 16:tc * 16 + 8],
                    gs[:, tc * 16 + 10:tc * 16 + 11], None, ALU.mult))
            w.vdrain()
            for tc in range(4):
                w.v(lambda E, tc=tc: E.reduce_max(
                    gs[:, tc * 16 + 9:tc * 16 + 10], gs[:, tc * 16:tc * 16 + 8],
                    axis=AX.X))
            w.vdrain()
            for tc in range(4):
                w.v(lambda E, tc=tc: E.tensor_scalar(
                    gk[:, tc * 16:tc * 16 + 8], gs[:, tc * 16:tc * 16 + 8],
                    gs[:, tc * 16 + 9:tc * 16 + 10], None, ALU.is_ge))
            w.vdrain()
            for tc in range(4):
                w.v(lambda E, tc=tc: E.tensor_tensor(
                    gk[:, tc * 16 + 8:tc * 16 + 16], gs[:, tc * 16:tc * 16 + 8],
                    gk[:, tc * 16:tc * 16 + 8], ALU.mult))
            w.vdrain()
            for tc in range(4):
                w.v(lambda E, tc=tc: E.tensor_sub(
                    gk[:, tc * 16 + 8:tc * 16 + 16], gs[:, tc * 16:tc * 16 + 8],
                    gk[:, tc * 16 + 8:tc * 16 + 16]))
            w.vdrain()
            for tc in range(4):
                w.v(lambda E, tc=tc: E.reduce_max(
                    gs[:, tc * 16 + 9:tc * 16 + 10],
                    gk[:, tc * 16 + 8:tc * 16 + 16], axis=AX.X))
            w.vdrain()
            for tc in range(4):
                w.v(lambda E, tc=tc: E.tensor_scalar(
                    gk[:, tc * 16 + 8:tc * 16 + 16],
                    gk[:, tc * 16 + 8:tc * 16 + 16],
                    gs[:, tc * 16 + 9:tc * 16 + 10], None, ALU.is_ge))
            w.vdrain()
            for tc in range(4):
                w.v(lambda E, tc=tc: E.tensor_add(
                    gk[:, tc * 16:tc * 16 + 8], gk[:, tc * 16:tc * 16 + 8],
                    gk[:, tc * 16 + 8:tc * 16 + 16]))
            w.vdrain()
            w.wt("V", "PE", pe_tr_last)
            dv_gate_done = 0
            for tc in range(4):
                dv_gate_done = w.v(lambda E, tc=tc: E.tensor_tensor(
                    cmb4[:, tc * 9:tc * 9 + 8], gs[:, tc * 16:tc * 16 + 8],
                    gk[:, tc * 16:tc * 16 + 8], ALU.mult), inc=True)

            w.wt("T", "WQ", wd_wq)
            w.wt("T", "DVE", st_last["apply"][g])
            for m in range(KC):
                bk = m % 2
                w.fill(bk)
                for k in range(KC):
                    pe_fill = w.t(lambda E, k=k, m=m, bk=bk, gcol=gcol: MM(
                        E, pb[bk][:],
                        wq[:, k * 1024 + m * 128: k * 1024 + m * 128 + 128],
                        xn_d[:, k, gcol:gcol + GT],
                        start=(k == 0), stop=(k == KC - 1)),
                        inc=(k == KC - 1))
                w.wt("C", "PE", pe_mix_last)
                w.wt("C", "PE", pe_fill)
                ac_evQK = w.sc(lambda E, m=m, bk=bk, gcol=gcol: E.activation(
                    Qb[:, m * GT:(m + 1) * GT], pb[bk][:], AF.Copy), inc=True)
                w.rd(bk, "ACT", ac_evQK)
            pe_q_last = pe_fill
            if g < NG - 1:
                ln_stats(g + 1)
            w.wt("T", "WK", wd_wk)
            for m in range(KC):
                bk = m % 2
                w.fill(bk)
                for k in range(KC):
                    pe_fill = w.t(lambda E, k=k, m=m, bk=bk, gcol=gcol: MM(
                        E, pb[bk][:],
                        wk[:, k * 1024 + m * 128: k * 1024 + m * 128 + 128],
                        xn_d[:, k, gcol:gcol + GT],
                        start=(k == 0), stop=(k == KC - 1)),
                        inc=(k == KC - 1))
                w.wt("C", "PE", pe_mix_last)
                w.wt("C", "PE", pe_fill)
                ac_evQK = w.sc(lambda E, m=m, bk=bk, gcol=gcol: E.activation(
                    Kf[:, m * GT:(m + 1) * GT], pb[bk][:], AF.Copy), inc=True)
                w.rd(bk, "ACT", ac_evQK)
            pe_q_last = max(pe_q_last, pe_fill)

            ac_yev = {}
            ac_ktv = 0
            for tc in range(4):
                Ch = g * 4 + tc
                sl = tc % 2
                w.wt("T", "ACT", ac_evQK)
                pe_at = {}
                for q in range(2):
                    Ab = 2 + q
                    Bb = 4 + q
                    w.fill(Ab)
                    w.fill(Bb)
                    for jj in range(4):
                        j = q * 4 + jj
                        w.t(lambda E, j=j, jj=jj, Ab=Ab, tc=tc: MM(
                            E, pb[Ab][:, jj * 128:jj * 128 + 128],
                            Kf[0:64, j * GT + tc * 128: j * GT + tc * 128 + 128],
                            Qb[0:64, j * GT + tc * 128: j * GT + tc * 128 + 128],
                            start=True, stop=True, skip_group_check=True))
                        pe_at[q] = w.t(lambda E, j=j, jj=jj, Bb=Bb, tc=tc: MM(
                            E, pb[Bb][:, jj * 128:jj * 128 + 128],
                            Kf[64:128, j * GT + tc * 128: j * GT + tc * 128 + 128],
                            Qb[64:128, j * GT + tc * 128: j * GT + tc * 128 + 128],
                            start=True, stop=True, tile_position=(64, 0),
                            skip_group_check=True), inc=(jj == 3))
                fi = 0
                w.wt("T", "WV", wd_wv)
                for half in range(2):
                    bk = fi % 2
                    w.fill(bk)
                    for k in range(KC):
                        pe_fill = w.t(lambda E, k=k, half=half, bk=bk,
                                      tc=tc, gcol=gcol: MM(
                            E, pb[bk][:],
                            xn_d[:, k, gcol + tc * 128: gcol + tc * 128 + 128],
                            wv[:, k * 1024 + half * 512: k * 1024 + half * 512 + 512],
                            start=(k == 0), stop=(k == KC - 1)),
                            inc=(k == KC - 1))
                    w.wt("C", "PE", pe_fill)
                    ac_ktv = w.sc(lambda E, half=half, bk=bk, sl=sl:
                                  E.activation(
                        Vt[:, sl * 1024 + half * 512: sl * 1024 + half * 512 + 512],
                        pb[bk][:], AF.Copy), inc=True)
                    w.rd(bk, "ACT", ac_ktv)
                    fi += 1
                for half in range(2):
                    bk = fi % 2
                    w.fill(bk)
                    for m4 in range(4):
                        m = half * 4 + m4
                        pe_fill = w.t(lambda E, m=m, m4=m4, bk=bk, tc=tc,
                                      gcol=gcol: MM(
                            E, pb[bk][:, m4 * 128:m4 * 128 + 128],
                            Kf[:, m * GT + tc * 128: m * GT + tc * 128 + 128],
                            idnb[:],
                            start=True, stop=True, skip_group_check=True),
                            inc=(m4 == 3))
                    w.wt("C", "PE", pe_fill)
                    ac_ktv = w.sc(lambda E, half=half, bk=bk, sl=sl:
                                  E.activation(
                        Kt[:, sl * 1024 + half * 512: sl * 1024 + half * 512 + 512],
                        pb[bk][:], AF.Copy), inc=True)
                    w.rd(bk, "ACT", ac_ktv)
                    fi += 1
                pe_ktv_last = pe_fill
                if tc == 1:
                    w.wt("T", "DVE", dv_gate_done)
                    w.fill(6)
                    for tc2 in range(4):
                        pe_tr_last = w.t(lambda E, tc2=tc2: E.transpose(
                            pb[6][0:9, tc2 * 128:tc2 * 128 + 128],
                            cmb4[:, tc2 * 9:tc2 * 9 + 9], idn[:]), inc=True)
                        w.wt("C", "PE", pe_tr_last)
                        Cg = g * 4 + tc2
                        ac_cmbT_last = w.sc(lambda E, Cg=Cg, tc2=tc2:
                                            E.activation(
                            cmbT[0:9, Cg * 128:(Cg + 1) * 128],
                            pb[6][0:9, tc2 * 128:tc2 * 128 + 128],
                            AF.Copy), inc=True)
                        w.rd(6, "ACT", ac_cmbT_last)
                dv_mask = {}
                for q in range(2):
                    Ab = 2 + q
                    Bb = 4 + q
                    w.wt("V", "PE", pe_at[q])
                    w.v(lambda E, q=q, Ab=Ab: E.tensor_tensor(
                        atm[:, q * 1024: q * 1024 + 512], pb[Ab][:],
                        maskr[:], ALU.mult))
                    dv_mask[q] = w.v(lambda E, q=q, Bb=Bb: E.tensor_tensor(
                        atm[:, q * 1024 + 512: q * 1024 + 1024], pb[Bb][:],
                        maskr[:], ALU.mult), inc=True)
                    w.rd(Ab, "DVE", dv_mask[q])
                    w.rd(Bb, "DVE", dv_mask[q])
                w.wt("T", "ACT", ac_ktv)
                w.wt("T", "ACT", ac_scopy)
                for q in range(2):
                    w.wt("T", "DVE", dv_mask[q])
                    for jj in range(4):
                        j = q * 4 + jj
                        yb = (j // 2) % 2
                        st = (j % 2) * 128
                        if j % 2 == 0:
                            w.fill(yb)
                        w.t(lambda E, j=j, yb=yb, st=st, tc=tc: MM(
                            E, pb[yb][0:64, st:st + 128],
                            Ssb[:, j * 128: j * 128 + 64],
                            Qb[:, j * GT + tc * 128: j * GT + tc * 128 + 128],
                            start=True, stop=False, tile_position=(0, 0),
                            skip_group_check=True))
                        w.t(lambda E, j=j, yb=yb, st=st, tc=tc: MM(
                            E, pb[yb][64:128, st:st + 128],
                            Ssb[:, j * 128 + 64:(j + 1) * 128],
                            Qb[:, j * GT + tc * 128: j * GT + tc * 128 + 128],
                            start=True, stop=False, tile_position=(0, 64),
                            skip_group_check=True))
                        w.t(lambda E, j=j, jj=jj, yb=yb, st=st, sl=sl, q=q: MM(
                            E, pb[yb][0:64, st:st + 128],
                            Vt[:, sl * 1024 + (2 * j) * 64: sl * 1024 + (2 * j) * 64 + 64],
                            atm[:, q * 1024 + jj * 128: q * 1024 + jj * 128 + 128],
                            start=False, stop=False, tile_position=(0, 0),
                            skip_group_check=True))
                        pe_y = w.t(lambda E, j=j, jj=jj, yb=yb, st=st, sl=sl,
                                   q=q: MM(
                            E, pb[yb][64:128, st:st + 128],
                            Vt[:, sl * 1024 + (2 * j + 1) * 64: sl * 1024 + (2 * j + 1) * 64 + 64],
                            atm[:, q * 1024 + 512 + jj * 128: q * 1024 + 512 + jj * 128 + 128],
                            start=False, stop=True, tile_position=(0, 64),
                            skip_group_check=True), inc=True)
                        w.t(lambda E, j=j, sl=sl: MM(
                            E, pb[7][0:64, j * 64:(j + 1) * 64],
                            Kt[:, sl * 1024 + (2 * j) * 64: sl * 1024 + (2 * j) * 64 + 64],
                            Vt[:, sl * 1024 + (2 * j) * 64: sl * 1024 + (2 * j) * 64 + 64],
                            start=False, stop=False, tile_position=(0, 0),
                            skip_group_check=True))
                        pe_state = w.t(lambda E, j=j, sl=sl: MM(
                            E, pb[7][64:128, j * 64:(j + 1) * 64],
                            Kt[:, sl * 1024 + (2 * j + 1) * 64: sl * 1024 + (2 * j + 1) * 64 + 64],
                            Vt[:, sl * 1024 + (2 * j + 1) * 64: sl * 1024 + (2 * j + 1) * 64 + 64],
                            start=False, stop=False, tile_position=(0, 64),
                            skip_group_check=True), inc=True)
                        if j % 2 == 1:
                            w.wt("C", "PE", pe_y)
                            aye = w.sc(lambda E, j=j, yb=yb, tc=tc: E.activation(
                                Yb[:].rearrange("p (m t) -> p m t", m=KC)
                                  [:, j - 1:j + 1, tc * 128:tc * 128 + 128],
                                pb[yb][:].rearrange("p (m t) -> p m t", m=4)
                                  [:, 0:2, :], AF.Copy), inc=True)
                            ac_yev[j] = aye
                            w.rd(yb, "ACT", aye)
                    pe_mix_last = pe_state
                w.wt("C", "PE", pe_mix_last)
                w.sc(lambda E: E.activation(
                    Ssb[0:64, :].rearrange("p (j e) -> p j e", j=8)[:, :, 0:64],
                    pb[7][0:64, :].rearrange("p (j e) -> p j e", j=8), AF.Copy))
                ac_scopy = w.sc(lambda E: E.activation(
                    Ssb[64:128, :].rearrange("p (j e) -> p j e", j=8)[:, :, 64:128],
                    pb[7][64:128, :].rearrange("p (j e) -> p j e", j=8),
                    AF.Copy), inc=True)
                w.rd(7, "ACT", ac_scopy)
            if g < NG - 1:
                st_last["mix"] = pe_mix_last
                ln_apply(g + 1)
            w.wt("T", "ACT", ac_yev[7])
            for m in range(KC):
                bk = m % 2
                w.fill(bk)
                for j in range(KC):
                    pe_sp = w.t(lambda E, m=m, j=j, bk=bk: MM(
                        E, pb[bk][:],
                        wce[:, j * 1024 + m * 128: j * 1024 + m * 128 + 128],
                        Yb[:, j * GT:(j + 1) * GT],
                        start=(j == 0), stop=(j == KC - 1)), inc=(j == KC - 1))
                w.wt("C", "PE", pe_sp)
                slot = so_i % 2
                if so_i >= 2:
                    w.wt("C", f"SO{slot}", so_w[so_i - 2])
                acs = w.sc(lambda E, bk=bk, slot=slot: E.activation(
                    sst[:, slot * 512: slot * 512 + 512], pb[bk][:],
                    AF.Copy), inc=True)
                ac_sp[so_i] = acs
                w.rd(bk, "ACT", acs)
                w.wt("S", "ACT", acs)
                so_w[so_i] = w.sdma(
                    dsd[:, m * L + g * GT: m * L + g * GT + GT],
                    sst[:, slot * 512: slot * 512 + 512], key=f"SO{slot}")
                so_i += 1
            pe_sproj_done = pe_sp

        dv_yadd = 0
        pe_u1_done = 0
        pe_u3_done = 0
        pe_y_done = 0
        ac_t1_last = 0
        ac_b2_th = {}
        dvh_ic = {}
        dvh2 = {}
        dv_og = {}
        dv_oadd = {}
        ot_w = {}
        xc_w = {}
        sr_w = {}
        ac_g_prev = 0
        gi_count = 2 * 2 * KC

        def tile_col(gi):
            th_, r = divmod(gi, 16)
            g2_, m_ = divmod(r, KC)
            return m_ * L + th_ * 1024 + g2_ * 512

        def prefetch_xc(gi):
            if gi >= gi_count:
                return
            slot = gi % 4
            if gi < 4:
                w.wt("S", "PE", pe_sproj_done)
            else:
                w.wt("S", "DVE", dv_oadd[gi - 4])
            col = tile_col(gi)
            xc_w[gi] = w.sdma(xcr[:, slot * 512: slot * 512 + 512],
                              dx[:, col:col + 512], key=f"XC{slot}")

        def prefetch_sr(gi):
            if gi >= gi_count:
                return
            slot = gi % 2
            if gi < 2:
                w.wt("G", "PE", st_last["mm"])
                w.wt("G", "SO0", so_w[30])
                w.wt("G", "SO1", so_w[31])
            else:
                w.wt("G", "DVE", dv_og[gi - 2])
            col = tile_col(gi)
            sr_w[gi] = w.gdma(srg[:, slot * 512: slot * 512 + 512],
                              dsd[:, col:col + 512], key=f"SR{slot}")

        def final_tile(gi, dv_ya):
            col = tile_col(gi)
            xcs = gi % 4
            srs = gi % 2
            ogs = gi % 2
            w.wt("C", "DVE", dv_ya)
            w.wt("C", "DVE", dv_og.get(gi - 1, 0))
            w.wt("C", "PE", pe_mix_last)
            ysb_t = ysb0 if gi < 16 else ysb1
            th_, r = divmod(gi, 16)
            g2_, m_ = divmod(r, KC)
            yc = m_ * 1024 + g2_ * 512
            ac_g = w.sc(lambda E, ysb_t=ysb_t, yc=yc: E.activation(
                gts[:], ysb_t[:, yc:yc + 512], AF.Silu), inc=True)
            w.wt("V", "ACT", ac_g)
            w.wt("V", f"SR{srs}", sr_w[gi])
            w.wt("V", f"OT{ogs}", ot_w.get(gi - 2, 0))
            dv_og[gi] = w.v(lambda E, ogs=ogs, srs=srs: E.tensor_tensor(
                og[:, ogs * 512: ogs * 512 + 512], gts[:],
                srg[:, srs * 512: srs * 512 + 512], ALU.mult), inc=True)
            w.vdrain()
            w.wt("V", f"XC{xcs}", xc_w[gi])
            dv_oadd[gi] = w.v(lambda E, ogs=ogs, xcs=xcs: E.tensor_tensor(
                og[:, ogs * 512: ogs * 512 + 512],
                og[:, ogs * 512: ogs * 512 + 512],
                xcr[:, xcs * 512: xcs * 512 + 512], ALU.add), inc=True)
            w.wt("S", "DVE", dv_oadd[gi])
            ot_w[gi] = w.sdma(dout[:, col:col + 512],
                              og[:, ogs * 512: ogs * 512 + 512],
                              key=f"OT{ogs}")
            prefetch_xc(gi + 4)
            prefetch_sr(gi + 2)

        for e in range(NE):
            if e == 0:
                w.wt("G", "PE", pe_fold_done)
            else:
                w.wt("G", "PE", pe_u1_done)
            wd_w1 = w.gdma(w1b[:], dw1[e], key="W1")
            if e == 0:
                w.wt("G", "PE", pe_q_last)
            else:
                w.wt("G", "PE", pe_u3_done)
            wd_w3 = w.gdma(w3b[:], dw3[e], key="W3")
            if e == 0:
                w.wt("G", "PE", pe_ktv_last)
            else:
                w.wt("G", "PE", pe_y_done)
            wd_w2 = w.gdma(w2b[:], dw2[e], key="W2")
            if e == NE - 1:
                for gi0 in range(4):
                    prefetch_xc(gi0)
                for gi0 in range(2):
                    prefetch_sr(gi0)
            for th in range(2):
                base = th * 1024
                pe_cp = 0
                for g2 in range(2):
                    gc2 = base + g2 * 512
                    cpb = 6 + (th * 2 + g2) % 2
                    if e < 8:
                        w.wt("T", "DVE", ei)
                        w.wt("T", "ACT", ac_cmbT_last)
                        w.fill(cpb)
                        pe_cp = w.t(lambda E, e=e, gc2=gc2, cpb=cpb: MM(
                            E, pb[cpb][:], eind[:, e * 128:(e + 1) * 128],
                            cmbT[:, gc2:gc2 + 512], start=True, stop=True),
                            inc=True)
                    w.wt("T", "W1", wd_w1)
                    for ic in range(KC):
                        bk = ic % 2
                        w.fill(bk)
                        for k in range(KC):
                            pe_u1 = w.t(lambda E, k=k, ic=ic, bk=bk, gc2=gc2:
                                        MM(
                                E, pb[bk][:],
                                w1b[:, k * 1024 + ic * 128: k * 1024 + ic * 128 + 128],
                                xn_d[:, k, gc2:gc2 + 512],
                                start=(k == 0), stop=(k == KC - 1)),
                                inc=(k == KC - 1))
                        w.wt("C", "PE", pe_u1)
                        w.wt("C", "X", xd_b3)
                        w.wt("C", "DVE", dvh_ic.get(ic, 0))
                        ac_t1_last = w.sc(lambda E, ic=ic, e=e, bk=bk:
                                          E.activation(
                            t1[:, ic * 512: ic * 512 + 512],
                            pb[bk][:], AF.Silu,
                            bias=b1s[:, e * 8 + ic: e * 8 + ic + 1]), inc=True)
                        w.rd(bk, "ACT", ac_t1_last)
                    pe_u1_done = pe_u1
                    w.wt("T", "W3", wd_w3)
                    for ic in range(KC):
                        bk = 2 + ic % 2
                        w.fill(bk)
                        for k in range(KC):
                            pe_u3 = w.t(lambda E, k=k, ic=ic, bk=bk, gc2=gc2:
                                        MM(
                                E, pb[bk][:],
                                w3b[:, k * 1024 + ic * 128: k * 1024 + ic * 128 + 128],
                                xn_d[:, k, gc2:gc2 + 512],
                                start=(k == 0), stop=(k == KC - 1)),
                                inc=(k == KC - 1))
                        w.wt("V", "PE", pe_u3)
                        w.wt("V", "ACT", ac_t1_last)
                        w.wt("V", "X", xd_b3)
                        hs = ic * 1024 + g2 * 512
                        if e < 8:
                            w.wt("V", "PE", pe_cp)
                            w.v(lambda E, ic=ic, e=e, bk=bk, hs=hs:
                                E.scalar_tensor_tensor(
                                hb[:, hs:hs + 512], pb[bk][:],
                                b3s[:, e * 8 + ic: e * 8 + ic + 1],
                                t1[:, ic * 512: ic * 512 + 512],
                                ALU.add, ALU.mult))
                            w.vdrain()
                            dv_h = w.v(lambda E, hs=hs, cpb=cpb: E.tensor_tensor(
                                hb[:, hs:hs + 512], hb[:, hs:hs + 512],
                                pb[cpb][:], ALU.mult), inc=True)
                            w.rd(cpb, "DVE", dv_h)
                        else:
                            dv_h = w.v(lambda E, ic=ic, e=e, bk=bk, hs=hs:
                                       E.scalar_tensor_tensor(
                                hb[:, hs:hs + 512], pb[bk][:],
                                b3s[:, e * 8 + ic: e * 8 + ic + 1],
                                t1[:, ic * 512: ic * 512 + 512],
                                ALU.add, ALU.mult),
                                inc=True)
                        w.rd(bk, "DVE", dv_h)
                        dvh_ic[ic] = dv_h
                        dvh2[(g2, ic)] = dv_h
                    pe_u3_done = pe_u3
                if e == 0:
                    w.wt("T", "ACT", ac_cmbT_last)
                    w.wt("T", "WB", wd_b2)
                    ac_b2 = 0
                    fi = 0
                    for g2 in range(2):
                        for m in range(KC):
                            bk = fi % 2
                            w.fill(bk)
                            pe_b2 = w.t(lambda E, m=m, g2=g2, bk=bk, base=base:
                                        MM(
                                E, pb[bk][:], b2s[:, m * 128:(m + 1) * 128],
                                cmbT[:, base + g2 * 512: base + g2 * 512 + 512],
                                start=True, stop=True), inc=True)
                            w.wt("C", "PE", pe_b2)
                            if th == 0:
                                w.wt("C", "PE", pe_sproj_done)
                                w.wt("C", "PE", pe_mix_last)
                            else:
                                w.wt("C", "SO0", so_w[30])
                                w.wt("C", "SO1", so_w[31])
                            ysb_t = ysb0 if th == 0 else ysb1
                            ac_b2 = w.sc(lambda E, m=m, g2=g2, bk=bk,
                                         ysb_t=ysb_t: E.activation(
                                ysb_t[:, m * 1024 + g2 * 512: m * 1024 + g2 * 512 + 512],
                                pb[bk][:], AF.Copy), inc=True)
                            w.rd(bk, "ACT", ac_b2)
                            fi += 1
                    ac_b2_th[th] = ac_b2
                w.wt("T", "W2", wd_w2)
                fi = 0
                for g2 in range(2):
                    w.wt("T", "DVE", dvh2[(g2, KC - 1)])
                    for m in range(KC):
                        bk = fi % 2
                        w.fill(bk)
                        for ic in range(KC):
                            pe_yf = w.t(lambda E, ic=ic, m=m, g2=g2, bk=bk,
                                        base=base: MM(
                                E, pb[bk][:],
                                w2b[:, ic * 1024 + m * 128: ic * 1024 + m * 128 + 128],
                                hb[:, ic * 1024 + g2 * 512: ic * 1024 + g2 * 512 + 512],
                                start=(ic == 0), stop=(ic == KC - 1)),
                                inc=(ic == KC - 1))
                        w.wt("V", "PE", pe_yf)
                        w.wt("V", "ACT", ac_b2_th[th])
                        ysb_t = ysb0 if th == 0 else ysb1
                        dv_yadd = w.v(lambda E, m=m, g2=g2, bk=bk, ysb_t=ysb_t:
                                      E.tensor_tensor(
                            ysb_t[:, m * 1024 + g2 * 512: m * 1024 + g2 * 512 + 512],
                            ysb_t[:, m * 1024 + g2 * 512: m * 1024 + g2 * 512 + 512],
                            pb[bk][:], ALU.add), inc=True)
                        w.rd(bk, "DVE", dv_yadd)
                        if e == NE - 1:
                            gi = th * 16 + g2 * KC + m
                            final_tile(gi, dv_yadd)
                        fi += 1
                pe_y_done = pe_yf

        for slot in range(2):
            w.wt("S", f"OT{slot}", w.c[f"OT{slot}"])

    roles = {}
    with nc.Block() as block:
        def run_role(role, engine):
            w = Walk(role, {role: engine}, sems)
            emit(w)
            roles[role] = w.c

        block.tensor(lambda E: run_role("T", E))
        block.vector(lambda E: run_role("V", E))
        block.scalar(lambda E: run_role("C", E))
        block.gpsimd(lambda E: run_role("G", E))
        block.sync(lambda E: run_role("S", E))
    base = roles["T"]
    for r, cc in roles.items():
        assert cc == base, f"walk divergence {r}: {cc} vs {base}"
    est.close()
    return nc


def _img_km(wkm):
    K, M = wkm.shape
    assert K == 1024
    return np.ascontiguousarray(
        wkm.reshape(8, 128, M).transpose(1, 0, 2).reshape(128, 8 * M))


def prep_inputs(L0_attn_X, L0_attn_Wq, L0_attn_Wk, L0_attn_Wv, L0_attn_Wc,
                L0_attn_Ltri, L0_ffn_gate_w, L0_ffn_w1, L0_ffn_b1, L0_ffn_w2,
                L0_ffn_b2, L0_ffn_w3, L0_ffn_b3, L0_ffn_sw1, L0_ffn_sb1,
                L0_ffn_sw2, L0_ffn_sb2, L0_ffn_sw3, L0_ffn_sb3):
    f = np.float32
    shared = {
        "wq": _img_km(np.asarray(L0_attn_Wq, f).T),
        "wk": _img_km(np.asarray(L0_attn_Wk, f).T),
        "wv": _img_km(np.asarray(L0_attn_Wv, f).T),
        "wct": _img_km(np.asarray(L0_attn_Wc, f).T),
        "gw": np.ascontiguousarray(
            np.asarray(L0_ffn_gate_w, f).T.reshape(8, 128, 8)
            .transpose(1, 0, 2).reshape(128, 64)),
    }
    lt = np.zeros((128, 512), f)
    ltT = np.asarray(L0_attn_Ltri, f).transpose(0, 2, 1)
    for h in range(H):
        lt[(h % 2) * 64:(h % 2) * 64 + 64,
           (h // 2) * 64:(h // 2) * 64 + 64] = ltT[h]
    shared["lt"] = lt
    w1 = np.concatenate([np.asarray(L0_ffn_w1, f),
                         np.asarray(L0_ffn_sw1, f)[None]], 0)
    w3 = np.concatenate([np.asarray(L0_ffn_w3, f),
                         np.asarray(L0_ffn_sw3, f)[None]], 0)
    w2 = np.concatenate([np.asarray(L0_ffn_w2, f),
                         np.asarray(L0_ffn_sw2, f)[None]], 0)
    shared["w1"] = np.stack([_img_km(w1[e]) for e in range(NE)])
    shared["w3"] = np.stack([_img_km(w3[e]) for e in range(NE)])
    shared["w2"] = np.stack([_img_km(w2[e]) for e in range(NE)])
    b1e = np.concatenate([np.asarray(L0_ffn_b1, f),
                          np.asarray(L0_ffn_sb1, f)[None]], 0)
    b3e = np.concatenate([np.asarray(L0_ffn_b3, f),
                          np.asarray(L0_ffn_sb3, f)[None]], 0)
    shared["b1"] = np.ascontiguousarray(
        b1e.reshape(9, 8, 128).transpose(2, 0, 1).reshape(128, 72))
    shared["b3"] = np.ascontiguousarray(
        b3e.reshape(9, 8, 128).transpose(2, 0, 1).reshape(128, 72))
    shared["b2s"] = np.ascontiguousarray(np.concatenate(
        [np.asarray(L0_ffn_b2, f), np.asarray(L0_ffn_sb2, f)[None]], 0))
    X = np.asarray(L0_attn_X, f)
    in_maps = []
    for b in range(8):
        d = dict(shared)
        d["x"] = _img_km(np.ascontiguousarray(X[b].T))
        in_maps.append(d)
    return in_maps


def kernel(**inputs):
    in_maps = prep_inputs(**inputs)
    nc = build_program()
    res = run_bass_kernel_spmd(nc, in_maps, core_ids=list(range(8)))
    outs = []
    for b in range(8):
        o = np.asarray(res.results[b]["out"])
        outs.append(o.reshape(128, 8, 2048).transpose(1, 0, 2)
                    .reshape(1024, 2048).T)
    return np.ascontiguousarray(np.stack(outs)).astype(np.float32)


# version 15
# speedup vs baseline: 1.1478x; 1.0050x over previous
import numpy as np
import concourse.bass as bass
import concourse.mybir as mybir
from concourse.bass_utils import run_bass_kernel_spmd

F32 = mybir.dt.float32
BF16 = mybir.dt.bfloat16
U8 = mybir.dt.uint8
AF = mybir.ActivationFunctionType
ALU = mybir.AluOpType
AX = mybir.AxisListType

D = 1024
L = 2048
H = 16
KC = 8
NG = 4
GT = 512
NE = 9
EPS = 1e-5

XN_O   = 0
Y1_O   = 32768
SST_O  = 32768
C_O    = 65536
D_O    = 98304
E_O    = 180224
F_O    = 188416
SB_TOTAL = 211968

Q_O    = C_O
KF_O   = C_O + 8192
KT_O   = C_O + 16384
V_O    = C_O + 20480
ATM_O  = C_O + 24576
TMP_O  = C_O + 24576
SSB_O  = C_O + 28672
X2_O   = C_O + 30720
IOT_O  = C_O + 20480
IOTC_O = C_O + 20992
IOT2_O = C_O + 21024
TRIR_O = C_O + 21056
EITMP_O = C_O + 23104
H_O    = C_O
T1_O   = C_O + 16384
OG_O   = C_O + 24576
GTS_O  = C_O + 28672
SR_O   = C_O + 30720

WCT_O  = D_O
WQ_O   = D_O + 16384
WK_O   = D_O + 32768
WV_O   = D_O + 49152
WCE_O  = D_O + 65536
W1_O   = D_O
W3_O   = D_O + 16384
W2_O   = D_O + 32768
Y0_O   = D_O + 49152

Y_O    = E_O
XC_O   = E_O

MASK_O = F_O
IDN_O  = F_O + 2048
ONE1_O = F_O + 2560
ONEK_O = F_O + 3072
B1_O   = F_O + 3104
B3_O   = F_O + 3424
GW_O   = F_O + 3744
CMB_O  = F_O + 3872
GS_O   = F_O + 4032
GK_O   = F_O + 4288
EPS_O  = F_O + 4544
CMBT_O = F_O + 4576
EIND_O = F_O + 8672
B2S_O  = F_O + 10720
LTRI_O = F_O + 12768
LMT_O  = F_O + 13792
BM_O   = F_O + 14816
STAT_O = F_O + 15840
IDNB_O = F_O + 21984


class Arena:
    def __init__(self, nc, total_bytes):
        self.nc = nc
        self.handle = nc.sbuf_tensor("arena", [128, total_bytes], U8).__enter__()
        self.base = nc.lookup_mloc(self.handle).addr

    def t(self, name, shape, dtype, off):
        nc = self.nc
        hnd = nc.virtual_sbuf_tensor(name, list(shape), dtype)
        nc._tensor(hnd.name, hnd.shape, hnd.dtype, type="SB")
        mloc = nc.lookup_mloc(hnd)
        mloc.allocated = True
        mloc.addr = self.base + off
        return hnd


class Walk:
    OWN = {"T": "PE", "C": "ACT", "V": "DVE"}

    KEYS = ("PE", "ACT", "DVE", "GP", "X",
            "WL", "WA", "WQ", "WG", "WK", "WV", "WB", "W1", "W3", "W2",
            "XG0", "XG1", "XG2", "XG3",
            "SO0", "SO1", "XC0", "XC1", "XC2", "XC3",
            "SR0", "SR1", "OT0", "OT1")

    def __init__(self, role, engines, sems):
        self.role = role
        self.eng = engines
        self.sem = sems
        self.c = {k: 0 for k in self.KEYS}
        self.waited = {k: 0 for k in self.KEYS}
        self.bank_rd = [dict() for _ in range(8)]

    def _do(self, key, erole, fn, inc):
        if inc:
            self.c[key] += 1
        if self.role == erole:
            inst = fn(self.eng[erole])
            if inc:
                inst.then_inc(self.sem[key], 1)
        return self.c[key]

    def t(self, fn, inc=False):
        return self._do("PE", "T", fn, inc)

    def v(self, fn, inc=False):
        return self._do("DVE", "V", fn, inc)

    def sc(self, fn, inc=False):
        return self._do("ACT", "C", fn, inc)

    def g(self, fn, inc=False):
        return self._do("GP", "G", fn, inc)

    def vdrain(self):
        if self.role == "V":
            self.eng["V"].drain()

    def scdrain(self):
        if self.role == "C":
            self.eng["C"].drain()

    def gdma(self, dst, src, key):
        self.c[key] += 16
        if self.role == "G":
            self.eng["G"].dma_start(out=dst, in_=src).then_inc(self.sem[key], 16)
        return self.c[key]

    def sdma(self, dst, src, key):
        self.c[key] += 16
        if self.role == "S":
            self.eng["S"].dma_start(out=dst, in_=src).then_inc(self.sem[key], 16)
        return self.c[key]

    def wt(self, r, key, val):
        assert val <= self.c[key], f"wait on future {key}: {val} > {self.c[key]}"
        if val <= 0 or self.OWN.get(r) == key:
            return
        if self.role == r:
            if val <= self.waited[key]:
                return
            self.waited[key] = val
            self.eng[r].wait_ge(self.sem[key], val)

    def fill(self, b):
        for k, v in self.bank_rd[b].items():
            self.wt("T", k, v)
        self.bank_rd[b] = {}

    def rd(self, b, key, val):
        self.bank_rd[b][key] = max(self.bank_rd[b].get(key, 0), val)


def build_program():
    nc = bass.Bass("TRN2", target_bir_lowering=False)
    dx = nc.dram_tensor("x", [128, KC * L], F32, kind="ExternalInput").ap()
    dwq = nc.dram_tensor("wq", [128, 8192], F32, kind="ExternalInput").ap()
    dwk = nc.dram_tensor("wk", [128, 8192], F32, kind="ExternalInput").ap()
    dwv = nc.dram_tensor("wv", [128, 8192], F32, kind="ExternalInput").ap()
    dwct = nc.dram_tensor("wct", [128, 8192], F32, kind="ExternalInput").ap()
    dlt = nc.dram_tensor("lt", [128, 512], F32, kind="ExternalInput").ap()
    dgw = nc.dram_tensor("gw", [128, 64], F32, kind="ExternalInput").ap()
    dw1 = nc.dram_tensor("w1", [NE, 128, 8192], F32, kind="ExternalInput").ap()
    dw3 = nc.dram_tensor("w3", [NE, 128, 8192], F32, kind="ExternalInput").ap()
    dw2 = nc.dram_tensor("w2", [NE, 128, 8192], F32, kind="ExternalInput").ap()
    db1 = nc.dram_tensor("b1", [128, 72], F32, kind="ExternalInput").ap()
    db3 = nc.dram_tensor("b3", [128, 72], F32, kind="ExternalInput").ap()
    db2s = nc.dram_tensor("b2s", [9, 1024], F32, kind="ExternalInput").ap()
    dout = nc.dram_tensor("out", [128, KC * L], F32, kind="ExternalOutput").ap()
    dsd = nc.dram_tensor("sdram", [128, KC * L], BF16, kind="Internal").ap()

    ar = Arena(nc, SB_TOTAL)
    T = ar.t
    xn = T("xn", [128, KC * L], BF16, XN_O)
    ysb0 = T("ysb0", [128, 8192], F32, Y0_O)
    ysb1 = T("ysb1", [128, 8192], F32, Y1_O)
    sst = T("sst", [128, 2 * 512], BF16, SST_O)
    Qb = T("Qb", [128, KC * GT], BF16, Q_O)
    Kf = T("Kf", [128, KC * GT], BF16, KF_O)
    Kt = T("Kt", [128, 2 * 1024], BF16, KT_O)
    Vt = T("Vt", [128, 2 * 1024], BF16, V_O)
    atm = T("atm", [128, 2048], BF16, ATM_O)
    tmp = T("tmp", [128, 2 * 512], F32, TMP_O)
    Ssb = T("Ssb", [128, 1024], BF16, SSB_O)
    x2 = T("x2", [128, 2 * 512], BF16, X2_O)
    iot = T("iot", [128, 128], F32, IOT_O)
    iotc = T("iotc", [128, 1], F32, IOTC_O)
    iot2 = T("iot2", [128, 1], F32, IOT2_O)
    trir = T("trir", [128, 512], F32, TRIR_O)
    eitmp = T("eitmp", [9, 1024], F32, EITMP_O)
    hb = T("hb", [128, 8 * 1024], BF16, H_O)
    t1 = T("t1", [128, 8 * 512], BF16, T1_O)
    og = T("og", [128, 2 * 512], F32, OG_O)
    gts = T("gts", [128, 512], F32, GTS_O)
    srg = T("srg", [128, 2 * 512], BF16, SR_O)
    wct = T("wct", [128, 8192], BF16, WCT_O)
    wq = T("wq", [128, 8192], BF16, WQ_O)
    wk = T("wk", [128, 8192], BF16, WK_O)
    wv = T("wv", [128, 8192], BF16, WV_O)
    wce = T("wce", [128, 8192], BF16, WCE_O)
    w1b = T("w1b", [128, 8192], BF16, W1_O)
    w3b = T("w3b", [128, 8192], BF16, W3_O)
    w2b = T("w2b", [128, 8192], BF16, W2_O)
    Yb = T("Yb", [128, KC * GT], BF16, Y_O)
    xcr = T("xcr", [128, 4 * 512], F32, XC_O)
    maskr = T("maskr", [128, 512], F32, MASK_O)
    idn = T("idn", [128, 128], F32, IDN_O)
    idnb = T("idnb", [128, 128], BF16, IDNB_O)
    one1 = T("one1", [1, 128], F32, ONE1_O)
    onek = T("onek", [128, 1], BF16, ONEK_O)
    b1s = T("b1s", [128, 72], F32, B1_O)
    b3s = T("b3s", [128, 72], F32, B3_O)
    gw = T("gw", [128, 64], BF16, GW_O)
    cmb4 = T("cmb4", [128, 36], F32, CMB_O)
    gs = T("gs", [128, 64], F32, GS_O)
    gk = T("gk", [128, 64], F32, GK_O)
    eps_t = T("eps_t", [1, 1], F32, EPS_O)
    cmbT = T("cmbT", [9, 2048], BF16, CMBT_O)
    eind = T("eind", [9, 1024], BF16, EIND_O)
    b2s = T("b2s", [9, 1024], BF16, B2S_O)
    ltri = T("ltri", [128, 512], BF16, LTRI_O)
    lmt = T("lmt", [128, 512], BF16, LMT_O)
    bm = T("bm", [128, 512], BF16, BM_O)
    st0 = T("st0", [1, 512], F32, STAT_O)
    st1 = T("st1", [1, 512], F32, STAT_O + 2048)
    st2 = T("st2", [1, 512], F32, STAT_O + 4096)

    from contextlib import ExitStack
    est = ExitStack()
    pb = [est.enter_context(nc.psum_tensor(f"pb{i}", [128, 512], F32))
          for i in range(8)]
    sems = {k: est.enter_context(nc.semaphore(f"s{k}"))
            for k in Walk.KEYS}

    def emit(w: Walk):
        MM = lambda E, *a, **k: E.matmul(*a, **k)
        x_d = dx.rearrange("p (k t) -> p k t", k=KC)
        xn_d = xn[:].rearrange("p (k t) -> p k t", k=KC)

        w.g(lambda E: E.iota(iot[:], [[1, 128]], channel_multiplier=0,
                             allow_small_or_imprecise_dtypes=True))
        w.g(lambda E: E.iota(iotc[:], [[0, 1]], channel_multiplier=1,
                             allow_small_or_imprecise_dtypes=True))
        gp_iota = w.g(lambda E: E.iota(
            eitmp[:].rearrange("p (e m) -> p e m", e=8), [[1, 8], [0, 128]],
            channel_multiplier=0, allow_small_or_imprecise_dtypes=True),
            inc=True)
        wd_lt = w.gdma(ltri[:], dlt, key="WL")
        wd_wct = w.gdma(wct[:], dwct, key="WA")
        xg_w = {}
        xg_w[0] = w.gdma(xn_d[:, :, 0:GT], x_d[:, :, 0:GT], key="XG0")
        wd_gw = w.gdma(gw[:], dgw, key="WG")
        wd_wq = w.gdma(wq[:], dwq, key="WQ")
        wd_wk = w.gdma(wk[:], dwk, key="WK")
        wd_wv = w.gdma(wv[:], dwv, key="WV")
        for gg in range(1, NG):
            xg_w[gg] = w.gdma(xn_d[:, :, gg * GT:(gg + 1) * GT],
                              x_d[:, :, gg * GT:(gg + 1) * GT], key=f"XG{gg}")
        wd_b2 = w.gdma(b2s[:], db2s, key="WB")
        xd_b1 = w.sdma(b1s[:], db1, key="X")
        xd_b3 = w.sdma(b3s[:], db3, key="X")
        w.wt("V", "GP", gp_iota)
        w.v(lambda E: E.memset(one1[:], 1.0))
        w.v(lambda E: E.memset(onek[:], 1.0))
        w.v(lambda E: E.memset(
            cmb4[:].rearrange("p (c e) -> p c e", c=4)[:, :, 8:9], 1.0))
        w.v(lambda E: E.memset(Ssb[:], 0.0))
        w.v(lambda E: E.memset(pb[7][:], 0.0))
        dv_eps = w.v(lambda E: E.memset(eps_t[:], EPS), inc=True)
        w.v(lambda E: E.tensor_single_scalar(iot2[:], iotc[:], 64.0, ALU.subtract))
        w.v(lambda E: E.tensor_scalar(idn[:], iot[:], iotc[:], None, ALU.is_equal))
        w.v(lambda E: E.tensor_scalar(maskr[:, 0:128], iot[:], iotc[:], None,
                                      ALU.is_ge))
        w.v(lambda E: E.tensor_scalar(eind[:], eitmp[:], iotc[0:9, :], None,
                                      ALU.is_equal))
        w.vdrain()
        w.v(lambda E: E.tensor_copy(idnb[:], idn[:]))
        for q in range(1, 4):
            w.v(lambda E, q=q: E.tensor_copy(maskr[:, q * 128:(q + 1) * 128],
                                             maskr[:, 0:128]))
        w.v(lambda E: E.tensor_scalar(trir[0:64, 0:64], iot[0:64, 0:64],
                                      iotc[0:64], None, ALU.is_ge))
        w.v(lambda E: E.tensor_scalar(trir[64:128, 0:64], iot[64:128, 0:64],
                                      iot2[64:128], None, ALU.is_ge))
        ei = w.v(lambda E: E.engine_nop(), inc=True)
        w.vdrain()
        for q in range(1, 8):
            w.v(lambda E, q=q: E.tensor_copy(trir[:, q * 64:(q + 1) * 64],
                                             trir[:, 0:64]))
        dv_const = w.v(lambda E: E.engine_nop(), inc=True)
        w.vdrain()
        w.wt("V", "WL", wd_lt)
        dv_lmt = w.v(lambda E: E.tensor_tensor(lmt[:], ltri[:], trir[:],
                                               ALU.mult), inc=True)

        st_last = {"mm": 0, "dv": 0, "apply": {}, "sq": {}, "mix": 0}

        def ln_stats(gg):
            stat_mm = {}
            w.wt("T", "DVE", dv_const)
            for k in range(KC):
                sl = (k % 2) * 512
                w.wt("V", f"XG{gg}", xg_w[gg])
                if k >= 2:
                    w.wt("V", "PE", stat_mm[k - 2])
                else:
                    w.wt("V", "PE", st_last["sq"].get(k, 0))
                dv_sq = w.v(lambda E, k=k, sl=sl, gg=gg: E.tensor_tensor(
                    x2[:, sl:sl + 512], xn_d[:, k, gg * GT:(gg + 1) * GT],
                    xn_d[:, k, gg * GT:(gg + 1) * GT], ALU.mult), inc=True)
                w.wt("T", "DVE", dv_sq)
                w.wt("T", f"XG{gg}", xg_w[gg])
                if k == 0:
                    w.fill(2)
                    w.fill(3)
                w.t(lambda E, k=k, gg=gg: MM(
                    E, pb[2][0:1, :], onek[:], xn_d[:, k, gg * GT:(gg + 1) * GT],
                    start=(k == 0), stop=(k == KC - 1)))
                stat_mm[k] = w.t(lambda E, k=k, sl=sl: MM(
                    E, pb[3][0:1, :], onek[:], x2[:, sl:sl + 512],
                    start=(k == 0), stop=(k == KC - 1)), inc=True)
            st_last["mm"] = stat_mm[KC - 1]
            st_last["sq"] = {k - (KC - 2): v for k, v in stat_mm.items()
                             if k >= KC - 2}
            w.wt("V", "PE", st_last["mm"])
            w.v(lambda E: E.tensor_scalar(st0[:], pb[2][0:1, :], 1.0 / D, None,
                                          ALU.mult))
            dv_a = w.v(lambda E: E.tensor_scalar(st1[:], pb[3][0:1, :], 1.0 / D,
                                                 None, ALU.mult), inc=True)
            w.rd(2, "DVE", dv_a)
            w.rd(3, "DVE", dv_a)
            w.vdrain()
            w.v(lambda E: E.tensor_mul(st2[:], st0[:], st0[:]))
            w.vdrain()
            dv_var = w.v(lambda E: E.tensor_sub(st1[:], st1[:], st2[:]), inc=True)
            w.wt("C", "DVE", dv_var)
            w.wt("C", "DVE", dv_eps)
            ac_std = w.sc(lambda E: E.activation(st1[:], st1[:], AF.Sqrt,
                                                 bias=eps_t[:]), inc=True)
            w.wt("V", "ACT", ac_std)
            w.v(lambda E: E.reciprocal(st2[:], st1[:]))
            w.vdrain()
            st_last["dv"] = w.v(lambda E: E.tensor_mul(st0[:], st0[:], st2[:]),
                                inc=True)

        def ln_apply(gg):
            w.wt("T", "DVE", st_last["dv"])
            w.fill(2)
            w.fill(3)
            w.t(lambda E: MM(E, pb[2][:], one1[:], st2[:], start=True, stop=True))
            pe_pl = w.t(lambda E: MM(E, pb[3][:], one1[:], st0[:],
                                     start=True, stop=True), inc=True)
            w.wt("V", "PE", pe_pl)
            w.wt("V", "PE", st_last["mix"])
            dv_apply = 0
            for k in range(KC):
                tsl = (k % 2) * 512
                w.v(lambda E, k=k, tsl=tsl, gg=gg: E.tensor_tensor(
                    tmp[:, tsl:tsl + 512], xn_d[:, k, gg * GT:(gg + 1) * GT],
                    pb[2][:], ALU.mult))
                w.vdrain()
                dv_apply = w.v(lambda E, k=k, gg=gg, tsl=tsl: E.tensor_tensor(
                    xn[:, k * L + gg * GT: k * L + gg * GT + GT],
                    tmp[:, tsl:tsl + 512], pb[3][:], ALU.subtract), inc=True)
                w.vdrain()
            st_last["apply"][gg] = dv_apply
            w.rd(2, "DVE", dv_apply)
            w.rd(3, "DVE", dv_apply)

        w.wt("T", "DVE", dv_lmt)
        w.wt("T", "WA", wd_wct)
        w.fill(0)
        w.fill(1)
        pe_bm = {}
        for h in range(H):
            b = (h % 2) * 64
            j = h // 2
            bmb = h % 2
            pe_bm[bmb] = w.t(lambda E, b=b, j=j, bmb=bmb: MM(
                E, pb[bmb][b:b + 64, j * 64:j * 64 + 64],
                lmt[b:b + 64, j * 64:j * 64 + 64],
                lmt[b:b + 64, j * 64:j * 64 + 64],
                start=True, stop=True, tile_position=(b, b),
                skip_group_check=True), inc=(h >= H - 2))
        ac_bm = {}
        for bmb in range(2):
            b = bmb * 64
            w.wt("C", "PE", pe_bm[bmb])
            ac_bm[bmb] = w.sc(lambda E, b=b, bmb=bmb: E.activation(
                bm[b:b + 64, :], pb[bmb][b:b + 64, :], AF.Copy), inc=True)
            w.rd(bmb, "ACT", ac_bm[bmb])
        w.wt("T", "ACT", ac_bm[0])
        w.wt("T", "ACT", ac_bm[1])
        pe_fold_done = 0
        fo_seq = {0: 0, 1: 0}
        for h in range(H):
            b = (h % 2) * 64
            j = h // 2
            par = h % 2
            for o2 in range(2):
                fob = 2 + par + 2 * (fo_seq[par] % 2)
                fo_seq[par] += 1
                w.fill(fob)
                pe_fo = w.t(lambda E, b=b, j=j, o2=o2, fob=fob: MM(
                    E, pb[fob][b:b + 64, 0:512],
                    bm[b:b + 64, j * 64:j * 64 + 64],
                    wct[b:b + 64, j * 1024 + o2 * 512: j * 1024 + o2 * 512 + 512],
                    start=True, stop=True, tile_position=(b, b),
                    skip_group_check=True), inc=True)
                w.wt("C", "PE", pe_fo)
                af = w.sc(lambda E, b=b, j=j, o2=o2, fob=fob: E.activation(
                    wce[b:b + 64, j * 1024 + o2 * 512: j * 1024 + o2 * 512 + 512],
                    pb[fob][b:b + 64, 0:512], AF.Copy), inc=True)
                w.rd(fob, "ACT", af)
                pe_fold_done = pe_fo

        ln_stats(0)
        ln_apply(0)


        ac_scopy = 0
        ac_evQK = 0
        pe_mix_last = 0
        pe_sproj_done = 0
        pe_q_last = 0
        pe_ktv_last = 0
        pe_tr_last = 0
        ac_cmbT_last = 0
        ac_sp = {}
        so_w = {}
        so_i = 0
        for g in range(NG):
            gcol = g * GT
            w.wt("T", "DVE", st_last["apply"][g])
            w.wt("T", "WG", wd_gw)
            w.fill(6)
            for tc in range(4):
                for k in range(KC):
                    pe_gate = w.t(lambda E, k=k, tc=tc, gcol=gcol: MM(
                        E, pb[6][:, tc * 8:tc * 8 + 8],
                        xn_d[:, k, gcol + tc * 128: gcol + tc * 128 + 128],
                        gw[:, k * 8:(k + 1) * 8],
                        start=(k == 0), stop=(k == KC - 1)),
                        inc=(tc == 3 and k == KC - 1))
            w.wt("V", "PE", pe_gate)
            dv_m = 0
            for tc in range(4):
                dv_m = w.v(lambda E, tc=tc: E.reduce_max(
                    gs[:, tc * 16 + 8:tc * 16 + 9], pb[6][:, tc * 8:tc * 8 + 8],
                    axis=AX.X, negate=True), inc=True)
            w.rd(6, "DVE", dv_m)
            ac_exp = 0
            for tc in range(4):
                w.wt("C", "DVE", dv_m)
                ac_exp = w.sc(lambda E, tc=tc: E.activation(
                    gs[:, tc * 16:tc * 16 + 8], pb[6][:, tc * 8:tc * 8 + 8],
                    AF.Exp, bias=gs[:, tc * 16 + 8:tc * 16 + 9]), inc=True)
            w.rd(6, "ACT", ac_exp)
            w.wt("V", "ACT", ac_exp)
            for tc in range(4):
                w.v(lambda E, tc=tc: E.reduce_sum(
                    gs[:, tc * 16 + 9:tc * 16 + 10], gs[:, tc * 16:tc * 16 + 8],
                    axis=AX.X))
            w.vdrain()
            for tc in range(4):
                w.v(lambda E, tc=tc: E.reciprocal(
                    gs[:, tc * 16 + 10:tc * 16 + 11],
                    gs[:, tc * 16 + 9:tc * 16 + 10]))
            w.vdrain()
            for tc in range(4):
                w.v(lambda E, tc=tc: E.tensor_scalar(
                    gs[:, tc * 16:tc * 16 + 8], gs[:, tc * 16:tc * 16 + 8],
                    gs[:, tc * 16 + 10:tc * 16 + 11], None, ALU.mult))
            w.vdrain()
            for tc in range(4):
                w.v(lambda E, tc=tc: E.reduce_max(
                    gs[:, tc * 16 + 9:tc * 16 + 10], gs[:, tc * 16:tc * 16 + 8],
                    axis=AX.X))
            w.vdrain()
            for tc in range(4):
                w.v(lambda E, tc=tc: E.tensor_scalar(
                    gk[:, tc * 16:tc * 16 + 8], gs[:, tc * 16:tc * 16 + 8],
                    gs[:, tc * 16 + 9:tc * 16 + 10], None, ALU.is_ge))
            w.vdrain()
            for tc in range(4):
                w.v(lambda E, tc=tc: E.tensor_tensor(
                    gk[:, tc * 16 + 8:tc * 16 + 16], gs[:, tc * 16:tc * 16 + 8],
                    gk[:, tc * 16:tc * 16 + 8], ALU.mult))
            w.vdrain()
            for tc in range(4):
                w.v(lambda E, tc=tc: E.tensor_sub(
                    gk[:, tc * 16 + 8:tc * 16 + 16], gs[:, tc * 16:tc * 16 + 8],
                    gk[:, tc * 16 + 8:tc * 16 + 16]))
            w.vdrain()
            for tc in range(4):
                w.v(lambda E, tc=tc: E.reduce_max(
                    gs[:, tc * 16 + 9:tc * 16 + 10],
                    gk[:, tc * 16 + 8:tc * 16 + 16], axis=AX.X))
            w.vdrain()
            for tc in range(4):
                w.v(lambda E, tc=tc: E.tensor_scalar(
                    gk[:, tc * 16 + 8:tc * 16 + 16],
                    gk[:, tc * 16 + 8:tc * 16 + 16],
                    gs[:, tc * 16 + 9:tc * 16 + 10], None, ALU.is_ge))
            w.vdrain()
            for tc in range(4):
                w.v(lambda E, tc=tc: E.tensor_add(
                    gk[:, tc * 16:tc * 16 + 8], gk[:, tc * 16:tc * 16 + 8],
                    gk[:, tc * 16 + 8:tc * 16 + 16]))
            w.vdrain()
            w.wt("V", "PE", pe_tr_last)
            dv_gate_done = 0
            for tc in range(4):
                dv_gate_done = w.v(lambda E, tc=tc: E.tensor_tensor(
                    cmb4[:, tc * 9:tc * 9 + 8], gs[:, tc * 16:tc * 16 + 8],
                    gk[:, tc * 16:tc * 16 + 8], ALU.mult), inc=True)

            w.wt("T", "WQ", wd_wq)
            w.wt("T", "DVE", st_last["apply"][g])
            for m in range(KC):
                bk = m % 2
                w.fill(bk)
                for k in range(KC):
                    pe_fill = w.t(lambda E, k=k, m=m, bk=bk, gcol=gcol: MM(
                        E, pb[bk][:],
                        wq[:, k * 1024 + m * 128: k * 1024 + m * 128 + 128],
                        xn_d[:, k, gcol:gcol + GT],
                        start=(k == 0), stop=(k == KC - 1)),
                        inc=(k == KC - 1))
                w.wt("C", "PE", pe_mix_last)
                w.wt("C", "PE", pe_fill)
                ac_evQK = w.sc(lambda E, m=m, bk=bk, gcol=gcol: E.activation(
                    Qb[:, m * GT:(m + 1) * GT], pb[bk][:], AF.Copy), inc=True)
                w.rd(bk, "ACT", ac_evQK)
            pe_q_last = pe_fill
            if g < NG - 1:
                ln_stats(g + 1)
            w.wt("T", "WK", wd_wk)
            for m in range(KC):
                bk = m % 2
                w.fill(bk)
                for k in range(KC):
                    pe_fill = w.t(lambda E, k=k, m=m, bk=bk, gcol=gcol: MM(
                        E, pb[bk][:],
                        wk[:, k * 1024 + m * 128: k * 1024 + m * 128 + 128],
                        xn_d[:, k, gcol:gcol + GT],
                        start=(k == 0), stop=(k == KC - 1)),
                        inc=(k == KC - 1))
                w.wt("C", "PE", pe_mix_last)
                w.wt("C", "PE", pe_fill)
                ac_evQK = w.sc(lambda E, m=m, bk=bk, gcol=gcol: E.activation(
                    Kf[:, m * GT:(m + 1) * GT], pb[bk][:], AF.Copy), inc=True)
                w.rd(bk, "ACT", ac_evQK)
            pe_q_last = max(pe_q_last, pe_fill)

            ac_yev = {}
            ac_ktv = 0
            for tc in range(4):
                Ch = g * 4 + tc
                sl = tc % 2
                w.wt("T", "ACT", ac_evQK)
                pe_at = {}
                for q in range(2):
                    Ab = 2 + q
                    Bb = 4 + q
                    w.fill(Ab)
                    w.fill(Bb)
                    for jj in range(4):
                        j = q * 4 + jj
                        w.t(lambda E, j=j, jj=jj, Ab=Ab, tc=tc: MM(
                            E, pb[Ab][:, jj * 128:jj * 128 + 128],
                            Kf[0:64, j * GT + tc * 128: j * GT + tc * 128 + 128],
                            Qb[0:64, j * GT + tc * 128: j * GT + tc * 128 + 128],
                            start=True, stop=True, skip_group_check=True))
                        pe_at[q] = w.t(lambda E, j=j, jj=jj, Bb=Bb, tc=tc: MM(
                            E, pb[Bb][:, jj * 128:jj * 128 + 128],
                            Kf[64:128, j * GT + tc * 128: j * GT + tc * 128 + 128],
                            Qb[64:128, j * GT + tc * 128: j * GT + tc * 128 + 128],
                            start=True, stop=True, tile_position=(64, 0),
                            skip_group_check=True), inc=(jj == 3))
                fi = 0
                w.wt("T", "WV", wd_wv)
                for half in range(2):
                    bk = fi % 2
                    w.fill(bk)
                    for k in range(KC):
                        pe_fill = w.t(lambda E, k=k, half=half, bk=bk,
                                      tc=tc, gcol=gcol: MM(
                            E, pb[bk][:],
                            xn_d[:, k, gcol + tc * 128: gcol + tc * 128 + 128],
                            wv[:, k * 1024 + half * 512: k * 1024 + half * 512 + 512],
                            start=(k == 0), stop=(k == KC - 1)),
                            inc=(k == KC - 1))
                    w.wt("C", "PE", pe_fill)
                    ac_ktv = w.sc(lambda E, half=half, bk=bk, sl=sl:
                                  E.activation(
                        Vt[:, sl * 1024 + half * 512: sl * 1024 + half * 512 + 512],
                        pb[bk][:], AF.Copy), inc=True)
                    w.rd(bk, "ACT", ac_ktv)
                    fi += 1
                for half in range(2):
                    bk = fi % 2
                    w.fill(bk)
                    for m4 in range(4):
                        m = half * 4 + m4
                        pe_fill = w.t(lambda E, m=m, m4=m4, bk=bk, tc=tc,
                                      gcol=gcol: MM(
                            E, pb[bk][:, m4 * 128:m4 * 128 + 128],
                            Kf[:, m * GT + tc * 128: m * GT + tc * 128 + 128],
                            idnb[:],
                            start=True, stop=True, skip_group_check=True),
                            inc=(m4 == 3))
                    w.wt("C", "PE", pe_fill)
                    ac_ktv = w.sc(lambda E, half=half, bk=bk, sl=sl:
                                  E.activation(
                        Kt[:, sl * 1024 + half * 512: sl * 1024 + half * 512 + 512],
                        pb[bk][:], AF.Copy), inc=True)
                    w.rd(bk, "ACT", ac_ktv)
                    fi += 1
                pe_ktv_last = pe_fill
                if tc == 1:
                    w.wt("T", "DVE", dv_gate_done)
                    w.fill(6)
                    for tc2 in range(4):
                        pe_tr_last = w.t(lambda E, tc2=tc2: E.transpose(
                            pb[6][0:9, tc2 * 128:tc2 * 128 + 128],
                            cmb4[:, tc2 * 9:tc2 * 9 + 9], idn[:]), inc=True)
                        w.wt("C", "PE", pe_tr_last)
                        Cg = g * 4 + tc2
                        ac_cmbT_last = w.sc(lambda E, Cg=Cg, tc2=tc2:
                                            E.activation(
                            cmbT[0:9, Cg * 128:(Cg + 1) * 128],
                            pb[6][0:9, tc2 * 128:tc2 * 128 + 128],
                            AF.Copy), inc=True)
                        w.rd(6, "ACT", ac_cmbT_last)
                dv_mask = {}
                for q in range(2):
                    Ab = 2 + q
                    Bb = 4 + q
                    w.wt("V", "PE", pe_at[q])
                    w.v(lambda E, q=q, Ab=Ab: E.tensor_tensor(
                        atm[:, q * 1024: q * 1024 + 512], pb[Ab][:],
                        maskr[:], ALU.mult))
                    dv_mask[q] = w.v(lambda E, q=q, Bb=Bb: E.tensor_tensor(
                        atm[:, q * 1024 + 512: q * 1024 + 1024], pb[Bb][:],
                        maskr[:], ALU.mult), inc=True)
                    w.rd(Ab, "DVE", dv_mask[q])
                    w.rd(Bb, "DVE", dv_mask[q])
                w.wt("T", "ACT", ac_ktv)
                w.wt("T", "ACT", ac_scopy)
                for q in range(2):
                    w.wt("T", "DVE", dv_mask[q])
                    for jj in range(4):
                        j = q * 4 + jj
                        yb = (j // 2) % 2
                        st = (j % 2) * 128
                        if j % 2 == 0:
                            w.fill(yb)
                        w.t(lambda E, j=j, yb=yb, st=st, tc=tc: MM(
                            E, pb[yb][0:64, st:st + 128],
                            Ssb[:, j * 128: j * 128 + 64],
                            Qb[:, j * GT + tc * 128: j * GT + tc * 128 + 128],
                            start=True, stop=False, tile_position=(0, 0),
                            skip_group_check=True))
                        w.t(lambda E, j=j, yb=yb, st=st, tc=tc: MM(
                            E, pb[yb][64:128, st:st + 128],
                            Ssb[:, j * 128 + 64:(j + 1) * 128],
                            Qb[:, j * GT + tc * 128: j * GT + tc * 128 + 128],
                            start=True, stop=False, tile_position=(0, 64),
                            skip_group_check=True))
                        w.t(lambda E, j=j, jj=jj, yb=yb, st=st, sl=sl, q=q: MM(
                            E, pb[yb][0:64, st:st + 128],
                            Vt[:, sl * 1024 + (2 * j) * 64: sl * 1024 + (2 * j) * 64 + 64],
                            atm[:, q * 1024 + jj * 128: q * 1024 + jj * 128 + 128],
                            start=False, stop=False, tile_position=(0, 0),
                            skip_group_check=True))
                        pe_y = w.t(lambda E, j=j, jj=jj, yb=yb, st=st, sl=sl,
                                   q=q: MM(
                            E, pb[yb][64:128, st:st + 128],
                            Vt[:, sl * 1024 + (2 * j + 1) * 64: sl * 1024 + (2 * j + 1) * 64 + 64],
                            atm[:, q * 1024 + 512 + jj * 128: q * 1024 + 512 + jj * 128 + 128],
                            start=False, stop=True, tile_position=(0, 64),
                            skip_group_check=True), inc=True)
                        w.t(lambda E, j=j, sl=sl: MM(
                            E, pb[7][0:64, j * 64:(j + 1) * 64],
                            Kt[:, sl * 1024 + (2 * j) * 64: sl * 1024 + (2 * j) * 64 + 64],
                            Vt[:, sl * 1024 + (2 * j) * 64: sl * 1024 + (2 * j) * 64 + 64],
                            start=False, stop=False, tile_position=(0, 0),
                            skip_group_check=True))
                        pe_state = w.t(lambda E, j=j, sl=sl: MM(
                            E, pb[7][64:128, j * 64:(j + 1) * 64],
                            Kt[:, sl * 1024 + (2 * j + 1) * 64: sl * 1024 + (2 * j + 1) * 64 + 64],
                            Vt[:, sl * 1024 + (2 * j + 1) * 64: sl * 1024 + (2 * j + 1) * 64 + 64],
                            start=False, stop=False, tile_position=(0, 64),
                            skip_group_check=True), inc=True)
                        if j % 2 == 1:
                            w.wt("C", "PE", pe_y)
                            aye = w.sc(lambda E, j=j, yb=yb, tc=tc: E.activation(
                                Yb[:].rearrange("p (m t) -> p m t", m=KC)
                                  [:, j - 1:j + 1, tc * 128:tc * 128 + 128],
                                pb[yb][:].rearrange("p (m t) -> p m t", m=4)
                                  [:, 0:2, :], AF.Copy), inc=True)
                            ac_yev[j] = aye
                            w.rd(yb, "ACT", aye)
                    pe_mix_last = pe_state
                w.wt("C", "PE", pe_mix_last)
                w.sc(lambda E: E.activation(
                    Ssb[0:64, :].rearrange("p (j e) -> p j e", j=8)[:, :, 0:64],
                    pb[7][0:64, :].rearrange("p (j e) -> p j e", j=8), AF.Copy))
                ac_scopy = w.sc(lambda E: E.activation(
                    Ssb[64:128, :].rearrange("p (j e) -> p j e", j=8)[:, :, 64:128],
                    pb[7][64:128, :].rearrange("p (j e) -> p j e", j=8),
                    AF.Copy), inc=True)
                w.rd(7, "ACT", ac_scopy)
            if g < NG - 1:
                st_last["mix"] = pe_mix_last
                ln_apply(g + 1)
            w.wt("T", "ACT", ac_yev[7])
            for m in range(KC):
                bk = m % 2
                w.fill(bk)
                for j in range(KC):
                    pe_sp = w.t(lambda E, m=m, j=j, bk=bk: MM(
                        E, pb[bk][:],
                        wce[:, j * 1024 + m * 128: j * 1024 + m * 128 + 128],
                        Yb[:, j * GT:(j + 1) * GT],
                        start=(j == 0), stop=(j == KC - 1)), inc=(j == KC - 1))
                w.wt("C", "PE", pe_sp)
                slot = so_i % 2
                if so_i >= 2:
                    w.wt("C", f"SO{slot}", so_w[so_i - 2])
                acs = w.sc(lambda E, bk=bk, slot=slot: E.activation(
                    sst[:, slot * 512: slot * 512 + 512], pb[bk][:],
                    AF.Copy), inc=True)
                ac_sp[so_i] = acs
                w.rd(bk, "ACT", acs)
                w.wt("S", "ACT", acs)
                so_w[so_i] = w.sdma(
                    dsd[:, m * L + g * GT: m * L + g * GT + GT],
                    sst[:, slot * 512: slot * 512 + 512], key=f"SO{slot}")
                so_i += 1
            pe_sproj_done = pe_sp

        dv_yadd = 0
        pe_u1_done = 0
        pe_u3_done = 0
        pe_y_done = 0
        ac_t1_last = 0
        ac_b2_th = {}
        dvh_ic = {}
        dvh2 = {}
        dv_og = {}
        dv_oadd = {}
        ot_w = {}
        xc_w = {}
        sr_w = {}
        ac_g_prev = 0
        gi_count = 2 * 2 * KC

        def tile_col(gi):
            th_, r = divmod(gi, 16)
            g2_, m_ = divmod(r, KC)
            return m_ * L + th_ * 1024 + g2_ * 512

        def prefetch_xc(gi):
            if gi >= gi_count:
                return
            slot = gi % 4
            if gi < 4:
                w.wt("S", "PE", pe_sproj_done)
            else:
                w.wt("S", "DVE", dv_oadd[gi - 4])
            col = tile_col(gi)
            xc_w[gi] = w.sdma(xcr[:, slot * 512: slot * 512 + 512],
                              dx[:, col:col + 512], key=f"XC{slot}")

        def prefetch_sr(gi):
            if gi >= gi_count:
                return
            slot = gi % 2
            if gi < 2:
                w.wt("G", "PE", st_last["mm"])
                w.wt("G", "SO0", so_w[30])
                w.wt("G", "SO1", so_w[31])
            else:
                w.wt("G", "DVE", dv_og[gi - 2])
            col = tile_col(gi)
            sr_w[gi] = w.gdma(srg[:, slot * 512: slot * 512 + 512],
                              dsd[:, col:col + 512], key=f"SR{slot}")

        def final_tile(gi, dv_ya):
            col = tile_col(gi)
            xcs = gi % 4
            srs = gi % 2
            ogs = gi % 2
            w.wt("C", "DVE", dv_ya)
            w.wt("C", "DVE", dv_og.get(gi - 1, 0))
            w.wt("C", "PE", pe_mix_last)
            ysb_t = ysb0 if gi < 16 else ysb1
            th_, r = divmod(gi, 16)
            g2_, m_ = divmod(r, KC)
            yc = m_ * 1024 + g2_ * 512
            ac_g = w.sc(lambda E, ysb_t=ysb_t, yc=yc: E.activation(
                gts[:], ysb_t[:, yc:yc + 512], AF.Silu), inc=True)
            w.wt("V", "ACT", ac_g)
            w.wt("V", f"SR{srs}", sr_w[gi])
            w.wt("V", f"OT{ogs}", ot_w.get(gi - 2, 0))
            dv_og[gi] = w.v(lambda E, ogs=ogs, srs=srs: E.tensor_tensor(
                og[:, ogs * 512: ogs * 512 + 512], gts[:],
                srg[:, srs * 512: srs * 512 + 512], ALU.mult), inc=True)
            w.vdrain()
            w.wt("V", f"XC{xcs}", xc_w[gi])
            dv_oadd[gi] = w.v(lambda E, ogs=ogs, xcs=xcs: E.tensor_tensor(
                og[:, ogs * 512: ogs * 512 + 512],
                og[:, ogs * 512: ogs * 512 + 512],
                xcr[:, xcs * 512: xcs * 512 + 512], ALU.add), inc=True)
            w.wt("S", "DVE", dv_oadd[gi])
            ot_w[gi] = w.sdma(dout[:, col:col + 512],
                              og[:, ogs * 512: ogs * 512 + 512],
                              key=f"OT{ogs}")
            prefetch_xc(gi + 4)
            prefetch_sr(gi + 2)

        for e in range(NE):
            if e == 0:
                w.wt("G", "PE", pe_fold_done)
            else:
                w.wt("G", "PE", pe_u1_done)
            wd_w1 = w.gdma(w1b[:], dw1[e], key="W1")
            if e == 0:
                w.wt("G", "PE", pe_q_last)
            else:
                w.wt("G", "PE", pe_u3_done)
            wd_w3 = w.gdma(w3b[:], dw3[e], key="W3")
            if e == 0:
                w.wt("G", "PE", pe_ktv_last)
            else:
                w.wt("G", "PE", pe_y_done)
            wd_w2 = w.gdma(w2b[:], dw2[e], key="W2")
            if e == NE - 1:
                for gi0 in range(4):
                    prefetch_xc(gi0)
                for gi0 in range(2):
                    prefetch_sr(gi0)
            for th in range(2):
                base = th * 1024
                pe_cp = 0
                for g2 in range(2):
                    gc2 = base + g2 * 512
                    cpb = 6 + (th * 2 + g2) % 2
                    if e < 8:
                        w.wt("T", "DVE", ei)
                        w.wt("T", "ACT", ac_cmbT_last)
                        w.fill(cpb)
                        pe_cp = w.t(lambda E, e=e, gc2=gc2, cpb=cpb: MM(
                            E, pb[cpb][:], eind[:, e * 128:(e + 1) * 128],
                            cmbT[:, gc2:gc2 + 512], start=True, stop=True),
                            inc=True)
                    w.wt("T", "W1", wd_w1)
                    for ic in range(KC):
                        bk = ic % 2
                        w.fill(bk)
                        for k in range(KC):
                            pe_u1 = w.t(lambda E, k=k, ic=ic, bk=bk, gc2=gc2:
                                        MM(
                                E, pb[bk][:],
                                w1b[:, k * 1024 + ic * 128: k * 1024 + ic * 128 + 128],
                                xn_d[:, k, gc2:gc2 + 512],
                                start=(k == 0), stop=(k == KC - 1)),
                                inc=(k == KC - 1))
                        w.wt("C", "PE", pe_u1)
                        w.wt("C", "X", xd_b3)
                        w.wt("C", "DVE", dvh_ic.get(ic, 0))
                        ac_t1_last = w.sc(lambda E, ic=ic, e=e, bk=bk:
                                          E.activation(
                            t1[:, ic * 512: ic * 512 + 512],
                            pb[bk][:], AF.Silu,
                            bias=b1s[:, e * 8 + ic: e * 8 + ic + 1]), inc=True)
                        w.rd(bk, "ACT", ac_t1_last)
                    pe_u1_done = pe_u1
                    w.wt("T", "W3", wd_w3)
                    for ic in range(KC):
                        bk = 2 + ic % 2
                        w.fill(bk)
                        for k in range(KC):
                            pe_u3 = w.t(lambda E, k=k, ic=ic, bk=bk, gc2=gc2:
                                        MM(
                                E, pb[bk][:],
                                w3b[:, k * 1024 + ic * 128: k * 1024 + ic * 128 + 128],
                                xn_d[:, k, gc2:gc2 + 512],
                                start=(k == 0), stop=(k == KC - 1)),
                                inc=(k == KC - 1))
                        w.wt("V", "PE", pe_u3)
                        w.wt("V", "ACT", ac_t1_last)
                        w.wt("V", "X", xd_b3)
                        hs = ic * 1024 + g2 * 512
                        if e < 8:
                            w.wt("V", "PE", pe_cp)
                            w.v(lambda E, ic=ic, e=e, bk=bk, hs=hs:
                                E.scalar_tensor_tensor(
                                hb[:, hs:hs + 512], pb[bk][:],
                                b3s[:, e * 8 + ic: e * 8 + ic + 1],
                                t1[:, ic * 512: ic * 512 + 512],
                                ALU.add, ALU.mult))
                            w.vdrain()
                            dv_h = w.v(lambda E, hs=hs, cpb=cpb: E.tensor_tensor(
                                hb[:, hs:hs + 512], hb[:, hs:hs + 512],
                                pb[cpb][:], ALU.mult), inc=True)
                            w.rd(cpb, "DVE", dv_h)
                        else:
                            dv_h = w.v(lambda E, ic=ic, e=e, bk=bk, hs=hs:
                                       E.scalar_tensor_tensor(
                                hb[:, hs:hs + 512], pb[bk][:],
                                b3s[:, e * 8 + ic: e * 8 + ic + 1],
                                t1[:, ic * 512: ic * 512 + 512],
                                ALU.add, ALU.mult),
                                inc=True)
                        w.rd(bk, "DVE", dv_h)
                        dvh_ic[ic] = dv_h
                        dvh2[(g2, ic)] = dv_h
                    pe_u3_done = pe_u3
                if e == 0:
                    w.wt("T", "ACT", ac_cmbT_last)
                    w.wt("T", "WB", wd_b2)
                    ac_b2 = 0
                    fi = 0
                    for g2 in range(2):
                        for m in range(KC):
                            bk = fi % 2
                            w.fill(bk)
                            pe_b2 = w.t(lambda E, m=m, g2=g2, bk=bk, base=base:
                                        MM(
                                E, pb[bk][:], b2s[:, m * 128:(m + 1) * 128],
                                cmbT[:, base + g2 * 512: base + g2 * 512 + 512],
                                start=True, stop=True), inc=True)
                            w.wt("C", "PE", pe_b2)
                            if th == 0:
                                w.wt("C", "PE", pe_sproj_done)
                                w.wt("C", "PE", pe_mix_last)
                            else:
                                w.wt("C", "SO0", so_w[30])
                                w.wt("C", "SO1", so_w[31])
                            ysb_t = ysb0 if th == 0 else ysb1
                            ac_b2 = w.sc(lambda E, m=m, g2=g2, bk=bk,
                                         ysb_t=ysb_t: E.activation(
                                ysb_t[:, m * 1024 + g2 * 512: m * 1024 + g2 * 512 + 512],
                                pb[bk][:], AF.Copy), inc=True)
                            w.rd(bk, "ACT", ac_b2)
                            fi += 1
                    ac_b2_th[th] = ac_b2
                w.wt("T", "W2", wd_w2)
                fi = 0
                for g2 in range(2):
                    w.wt("T", "DVE", dvh2[(g2, KC - 1)])
                    for m in range(KC):
                        bk = fi % 2
                        w.fill(bk)
                        for ic in range(KC):
                            pe_yf = w.t(lambda E, ic=ic, m=m, g2=g2, bk=bk,
                                        base=base: MM(
                                E, pb[bk][:],
                                w2b[:, ic * 1024 + m * 128: ic * 1024 + m * 128 + 128],
                                hb[:, ic * 1024 + g2 * 512: ic * 1024 + g2 * 512 + 512],
                                start=(ic == 0), stop=(ic == KC - 1)),
                                inc=(ic == KC - 1))
                        w.wt("V", "PE", pe_yf)
                        w.wt("V", "ACT", ac_b2_th[th])
                        ysb_t = ysb0 if th == 0 else ysb1
                        dv_yadd = w.v(lambda E, m=m, g2=g2, bk=bk, ysb_t=ysb_t:
                                      E.tensor_tensor(
                            ysb_t[:, m * 1024 + g2 * 512: m * 1024 + g2 * 512 + 512],
                            ysb_t[:, m * 1024 + g2 * 512: m * 1024 + g2 * 512 + 512],
                            pb[bk][:], ALU.add), inc=True)
                        w.rd(bk, "DVE", dv_yadd)
                        if e == NE - 1:
                            gi = th * 16 + g2 * KC + m
                            final_tile(gi, dv_yadd)
                        fi += 1
                pe_y_done = pe_yf

        for slot in range(2):
            w.wt("S", f"OT{slot}", w.c[f"OT{slot}"])

    roles = {}
    with nc.Block() as block:
        def run_role(role, engine):
            w = Walk(role, {role: engine}, sems)
            emit(w)
            roles[role] = w.c

        block.tensor(lambda E: run_role("T", E))
        block.vector(lambda E: run_role("V", E))
        block.scalar(lambda E: run_role("C", E))
        block.gpsimd(lambda E: run_role("G", E))
        block.sync(lambda E: run_role("S", E))
    base = roles["T"]
    for r, cc in roles.items():
        assert cc == base, f"walk divergence {r}: {cc} vs {base}"
    est.close()
    return nc


def _img_km(wkm):
    K, M = wkm.shape
    assert K == 1024
    return np.ascontiguousarray(
        wkm.reshape(8, 128, M).transpose(1, 0, 2).reshape(128, 8 * M))


def prep_inputs(L0_attn_X, L0_attn_Wq, L0_attn_Wk, L0_attn_Wv, L0_attn_Wc,
                L0_attn_Ltri, L0_ffn_gate_w, L0_ffn_w1, L0_ffn_b1, L0_ffn_w2,
                L0_ffn_b2, L0_ffn_w3, L0_ffn_b3, L0_ffn_sw1, L0_ffn_sb1,
                L0_ffn_sw2, L0_ffn_sb2, L0_ffn_sw3, L0_ffn_sb3):
    f = np.float32
    shared = {
        "wq": _img_km(np.asarray(L0_attn_Wq, f).T),
        "wk": _img_km(np.asarray(L0_attn_Wk, f).T),
        "wv": _img_km(np.asarray(L0_attn_Wv, f).T),
        "wct": _img_km(np.asarray(L0_attn_Wc, f).T),
        "gw": np.ascontiguousarray(
            np.asarray(L0_ffn_gate_w, f).T.reshape(8, 128, 8)
            .transpose(1, 0, 2).reshape(128, 64)),
    }
    lt = np.zeros((128, 512), f)
    ltT = np.asarray(L0_attn_Ltri, f).transpose(0, 2, 1)
    for h in range(H):
        lt[(h % 2) * 64:(h % 2) * 64 + 64,
           (h // 2) * 64:(h // 2) * 64 + 64] = ltT[h]
    shared["lt"] = lt
    w1 = np.concatenate([np.asarray(L0_ffn_w1, f),
                         np.asarray(L0_ffn_sw1, f)[None]], 0)
    w3 = np.concatenate([np.asarray(L0_ffn_w3, f),
                         np.asarray(L0_ffn_sw3, f)[None]], 0)
    w2 = np.concatenate([np.asarray(L0_ffn_w2, f),
                         np.asarray(L0_ffn_sw2, f)[None]], 0)
    shared["w1"] = np.stack([_img_km(w1[e]) for e in range(NE)])
    shared["w3"] = np.stack([_img_km(w3[e]) for e in range(NE)])
    shared["w2"] = np.stack([_img_km(w2[e]) for e in range(NE)])
    b1e = np.concatenate([np.asarray(L0_ffn_b1, f),
                          np.asarray(L0_ffn_sb1, f)[None]], 0)
    b3e = np.concatenate([np.asarray(L0_ffn_b3, f),
                          np.asarray(L0_ffn_sb3, f)[None]], 0)
    shared["b1"] = np.ascontiguousarray(
        b1e.reshape(9, 8, 128).transpose(2, 0, 1).reshape(128, 72))
    shared["b3"] = np.ascontiguousarray(
        b3e.reshape(9, 8, 128).transpose(2, 0, 1).reshape(128, 72))
    shared["b2s"] = np.ascontiguousarray(np.concatenate(
        [np.asarray(L0_ffn_b2, f), np.asarray(L0_ffn_sb2, f)[None]], 0))
    X = np.asarray(L0_attn_X, f)
    in_maps = []
    for b in range(8):
        d = dict(shared)
        d["x"] = _img_km(np.ascontiguousarray(X[b].T))
        in_maps.append(d)
    return in_maps


def kernel(**inputs):
    in_maps = prep_inputs(**inputs)
    nc = build_program()
    res = run_bass_kernel_spmd(nc, in_maps, core_ids=list(range(8)))
    outs = []
    for b in range(8):
        o = np.asarray(res.results[b]["out"])
        outs.append(o.reshape(128, 8, 2048).transpose(1, 0, 2)
                    .reshape(1024, 2048).T)
    return np.ascontiguousarray(np.stack(outs)).astype(np.float32)
